# version 13
# baseline (speedup 1.0000x reference)
import numpy as np
import ml_dtypes
import concourse.bass as bass
import concourse.mybir as mybir
from concourse.bass_utils import run_bass_kernel_spmd

F32 = mybir.dt.float32
BF16 = mybir.dt.bfloat16
AF = mybir.ActivationFunctionType
ALU = mybir.AluOpType
BF = ml_dtypes.bfloat16

DM, DI, NH = 1024, 2048, 32
CONVD = DI + 2 * 128
PROJ = 2 * DI + 2 * 128 + NH
NCH, NG, L = 4, 8, 4096
NJ = 35


def _hilo(x):
    h = x.astype(BF).astype(np.float32)
    return h.astype(BF), (x - h).astype(BF)


def host_consts():
    j = np.arange(DM)
    ang = 2 * np.pi * np.outer(j, j) / DM
    Fr = np.cos(ang).astype(np.float32)
    Fi = (-np.sin(ang)).astype(np.float32)
    c = {}
    c['cFr_h'], c['cFr_l'] = _hilo(Fr)
    c['cGi_h'], c['cGi_l'] = _hilo(-Fi)
    c['cFi_h'], c['cFi_l'] = _hilo(Fi)
    c['cFrd_h'], c['cFrd_l'] = _hilo(Fr / DM)
    c['cFid_h'], c['cFid_l'] = _hilo(Fi / DM)
    c['cIdent'] = np.eye(128).astype(BF)
    tri = np.tril(np.ones((128, 128), np.float32))
    c['cTriF'] = tri.T.astype(BF)
    c['cTriB'] = tri.astype(BF)
    indF = np.zeros((16, 512), np.float32)
    indB = np.zeros((16, 512), np.float32)
    for i in range(4):
        blk = slice(i * 128, (i + 1) * 128)
        indF[i * 4 + 0, blk] = -1.0; indF[i * 4 + 1, blk] = -1.0
        indF[i * 4 + 2, blk] = 1.0; indF[i * 4 + 3, blk] = 1.0
        indB[i * 4 + 0, blk] = 1.0; indB[i * 4 + 1, blk] = 1.0
        indB[i * 4 + 2, blk] = 1.0; indB[i * 4 + 3, blk] = 1.0
    c['cIndF'] = indF.astype(BF)
    c['cIndB'] = indB.astype(BF)
    c['cOnes2'] = np.ones((2, 128), np.float32).astype(BF)
    c['cOnes2N'] = (-np.ones((2, 128), np.float32)).astype(BF)
    ev = np.ones((4, 128), np.float32); ev[0] = -1.0; ev[1] = -1.0
    c['cEvB'] = ev.astype(BF)
    c['cOnesF'] = np.ones((32, 128), np.float32)
    return c


class Builder:
    def __init__(self, debug=False):
        import contextlib
        self.nc = bass.Bass("TRN2", target_bir_lowering=False, debug=False)
        self.ctx = contextlib.ExitStack()
        self.debug = debug
        self.q = {e: [] for e in ("te", "ve", "se", "gp", "sy")}
        self.cnt = {}
        self._declare()

    def emit(self, eng, fn, waits=(), inc=(), is_dma=False):
        if fn is not None and not is_dma and eng in ('te', 've', 'se', 'gp'):
            waits = list(waits) + [(eng, self.cnt[eng])]
            if not any(s == eng for s, _ in inc):
                inc = list(inc) + [(eng, 1)]
        incs = []
        for s, a in inc:
            self.cnt[s] += a
            incs.append((s, a))
        wts = tuple((s, v) for (s, v) in waits if v > 0)
        self.q[eng].append((fn, wts, tuple(incs)))
        return {s: self.cnt[s] for s, _ in incs}

    def now(self, s):
        return self.cnt[s]

    def dma(self, qeng, dst, src, waits=(), sem=None, slow=False):
        sem = sem or ('dsy' if qeng == 'sy' else 'dgp')
        return self.emit(qeng, lambda nc, e, dst=dst, src=src, slow=slow:
                         e.dma_start(out=dst, in_=src, allow_slow_non_contiguous=slow),
                         waits=waits, inc=[(sem, 16)], is_dma=True)

    def mm(self, psd, lhsT, rhs, start, stop, waits=(), inc=(), tp=None):
        def f(nc, e, a=psd, b=lhsT, c=rhs, st=start, sp=stop, tp=tp):
            return nc.tensor.matmul(a, b, c, start=st, stop=sp, tile_position=tp)
        return self.emit('te', f, waits=waits, inc=inc)

    def trans(self, psd, src, waits=(), inc=()):
        ident = self.sb['ident']
        k = src.shape[0]
        return self.emit('te', lambda nc, e, a=psd, b=src, i=ident, k=k:
                         nc.tensor.transpose(a, b, i[0:k, 0:k]), waits=waits, inc=inc)

    def ve(self, fn, waits=(), inc=()):
        return self.emit('ve', fn, waits=waits, inc=inc)

    def se(self, fn, waits=(), inc=()):
        return self.emit('se', fn, waits=waits, inc=inc)

    def gp(self, fn, waits=(), inc=()):
        return self.emit('gp', fn, waits=waits, inc=inc)

    def _declare(self):
        nc = self.nc
        self.d = {}
        def din(name, shape, dt=F32):
            self.d[name] = nc.dram_tensor(name, shape, dt, kind="ExternalInput").ap()
        din('x', [L, DM])
        for s in ('f', 'b'):
            din(f'wi_{s}', [DM, PROJ]); din(f'wo_{s}', [DI, DM])
            din(f'convw_{s}', [4, CONVD]); din(f'convb_{s}', [CONVD])
            din(f'alog_{s}', [NH]); din(f'dtb_{s}', [NH])
            din(f'ddg_{s}', [128, 4096])
        din('wm', [DM, DM]); din('bm', [DM])
        din('wfr', [DM, DM]); din('wfi', [DM, DM]); din('fbias', [DM])
        for k, v in host_consts().items():
            din(k, list(v.shape), BF16 if v.dtype == BF else F32)
        self.out = nc.dram_tensor('out', [L, DM], F32, kind="ExternalOutput").ap()
        dbg = "ExternalOutput" if self.debug else "Internal"
        self.s_wi = {s: nc.dram_tensor(f's_wi_{s}', [NJ, 128, 8, 128], BF16, kind="Internal").ap()
                     for s in ('f', 'b')}
        self.s_yf = nc.dram_tensor('s_yf', [L, DM], BF16, kind=dbg).ap()
        self.s_w = nc.dram_tensor('s_w', [4, 8, 8, 128, 128], BF16, kind=dbg).ap()
        self.s_a = nc.dram_tensor('s_a', [4, 8, 8, 128, 128], BF16, kind=dbg).ap()
        self.s_m = nc.dram_tensor('s_m', [2, 8, 2, 128, 512], BF16, kind=dbg).ap()
        self.s_cr = nc.dram_tensor('s_cr', [5, 32, 512], BF16, kind="Internal").ap()
        if self.debug:
            self.dbgt = {}
            for nm, shape, dt in [('dxnT', [128, 4096], BF16), ('dzsil', [128, 8192], BF16),
                                  ('dxbcs', [128, 9216], BF16), ('ddt', [32, 512], F32),
                                  ('dcum', [32, 512], F32), ('dlexp', [128, 2048], BF16),
                                  ('dlv', [128, 2048], BF16), ('dmt', [128, 2048], BF16),
                                  ('dgm', [128, 128], BF16), ('dyg', [128, 2048], BF16),
                                  ('demega', [128, 512], BF16), ('dcrc', [4, 4096], BF16),
                                  ('dwv', [32, 512], BF16), ('dxt', [128, 2048], BF16),
                                  ('dhst', [128, 2048], BF16), ('decl', [128, 512], F32),
                                  ('dhprev', [128, 2048], BF16)]:
                self.dbgt[nm] = nc.dram_tensor(nm, shape, dt, kind='ExternalOutput').ap()

        self.sb = {}
        def sbuf(name, shape, dt):
            self.sb[name] = self.ctx.enter_context(nc.sbuf_tensor(name, shape, dt))
        for name, shape, dt in [
            ('b_wi', [128, 4096], BF16), ('b_wo', [128, 16384], BF16),
            ('b_m', [128, 4096], BF16),
            ('zsil', [128, 8192], BF16), ('xbcs', [128, 9216], BF16),
            ('tmpc', [128, 1040], BF16), ('cstash', [128, 54], BF16),
            ('cw', [128, 72], F32), ('cb', [128, 18], F32),
            ('yg', [128, 4096], BF16), ('xm', [128, 2048], BF16), ('yfc', [128, 2048], BF16),
            ('xnT', [128, 4096], BF16), ('xnTl', [128, 4096], BF16),
            ('xq', [128, 2048], F32),
            ('xnch', [128, 1024], BF16), ('xncl', [128, 1024], BF16),
            ('lv', [128, 2048], BF16),
            ('mt', [128, 2048], BF16), ('mt2', [128, 2048], BF16), ('lv2', [128, 2048], BF16),
            ('gmask', [128, 128], BF16), ('xt', [128, 2048], BF16),
            ('hst', [128, 2048], BF16),
            ('Bt', [128, 128], BF16), ('wct', [128, 32], F32), ('pqbc', [128, 32], F32),
            ('ddiag', [128, 4096], BF16),
            ('dt_', [32, 512], F32), ('lgda', [32, 512], F32), ('cum', [32, 512], F32),
            ('lgdt', [32, 512], F32),
            ('cumhi', [32, 512], BF16), ('cumlo', [32, 512], BF16),
            ('ldthi', [32, 512], BF16), ('ldtlo', [32, 512], BF16),
            ('wv', [32, 512], BF16), ('pqb', [32, 4], BF16),
            ('clh', [32, 512], BF16), ('cll', [32, 512], BF16),
            ('clsc', [32, 2], F32),
            ('emc', [16, 1024], BF16), ('crc', [4, 4096], BF16), ('pqrow', [1, 128], BF16),
            ('pk32', [128, 512], BF16),
            ('onesf', [32, 128], F32),
            ('t1', [128, 2048], F32),

            ('mean', [128, 1], F32), ('sqs', [128, 1], F32), ('rstd', [128, 1], F32),
            ('nmrs', [128, 1], F32), ('epsc', [128, 1], F32),
            ('negA', [32, 1], F32), ('aexp', [32, 1], F32), ('dtbc', [32, 1], F32),
            ('bmr', [2, 1024], BF16), ('fbr', [2, 1024], BF16),
            ('ident', [128, 128], BF16), ('triF', [128, 128], BF16), ('triB', [128, 128], BF16),
            ('indF', [16, 512], BF16), ('indB', [16, 512], BF16),
            ('ones2', [2, 128], BF16), ('ones2n', [2, 128], BF16), ('evB', [4, 128], BF16),
            ('wprep', [128, 1024], F32),
        ]:
            sbuf(name, shape, dt)
        self.sb['pstr'] = self.sb['zsil']
        self.sb['lstr'] = self.sb['xbcs']
        self.sb['wpo'] = self.sb['yg']
        self.sb['convt'] = self.sb['wprep']
        self.sb['cump'] = self.sb['lgda']
        self.sb['wvin'] = self.sb['dt_']
        self.sb['brow'] = None
        self.sb['xmb'] = self.sb['xnch']
        self.sb['xmT'] = self.sb['xncl']

        self.ps = {}
        def psumt(name, shape, dt):
            self.ps[name] = self.ctx.enter_context(nc.psum_tensor(name, shape, dt))
        psumt('pS0', [128, 512], F32); psumt('pS1', [128, 512], F32)
        psumt('pQ0', [128, 512], F32); psumt('pQ1', [128, 512], F32); psumt('pQ2', [128, 512], F32)
        psumt('pT', [128, 1024], BF16)
        psumt('pY', [128, 1024], F32)

        self.semnames = ['dsy', 'dgp', 'te', 've', 'se', 'gp',
                         'do0', 'do1',
                         'sst0', 'sst1', 'sld0', 'sld1', 'wld', 'wst', 'bld',
                         'ph0', 'ph1', 'ph2', 'ph3', 'pev', 'dp', 'dd', 'wo',
                         'xq0', 'xq1', 'wi0', 'wi1', 'wi2', 'wi3',
                         'pk', 'em', 'sc', 'co', 'cr', 'pc',
                         'yf0', 'yf1', 'yl0', 'yl1',
                         'wm0', 'wm1', 'wm2', 'wm3', 'fm0', 'fm1', 'fm2', 'fm3',
                         'fl0', 'fl1', 'fl2', 'fl3']
        self.sems = {n: self.ctx.enter_context(nc.semaphore(n)) for n in self.semnames}
        for n in self.semnames:
            self.cnt[n] = 0

        self.slots = [self.ps['pS0'][:, :], self.ps['pS1'][:, :], self.ps['pQ0'][:, :],
                      self.ps['pQ1'][:, :], self.ps['pQ2'][:, :]]
        self.slot_free = [{'ve': 0, 'se': 0} for _ in range(5)]
        self.pool_i = 0
        self.tk = dict(
            pT_free={'ve': 0}, pY_free={'ve': 0},
            yg_free=[{'te': 0}, {'te': 0}],
            yfc_free=[{'st': 0, 've': 0}, {'st': 0, 've': 0}],
            xm_free=[{'ve': 0}, {'ve': 0}], xm_full=[{'ve': 0}, {'ve': 0}],
            xq_free=[{'ve': 0}, {'ve': 0}], xq2_free=[{'ve': 0}, {'ve': 0}],
            outb_free=[{'do': 0}, {'do': 0}],
            wi_free=[{'te': 0}, {'te': 0}, {'te': 0}, {'te': 0}],
            bm_free=[{'te': 0} for _ in range(8)],
            group_done={'te': 0, 've': 0, 'se': 0, 'gp': 0},
            ymm_free={'te': 0}, hbf_users={'te': 0},
            t1_free=[{'ve': 0}, {'ve': 0}],
        )

    def next_slot(self):
        i = self.pool_i % 5
        self.pool_i += 1
        return i

    def dbg_dump(self, pairs):
        wd = [(n, self.now(n)) for n in self.semnames]
        t = None
        for nm, src in pairs:
            t = self.dma('sy', self.dbgt[nm], src, waits=wd)
            wd = ()
        for q in ('te', 've', 'se', 'gp'):
            self.emit(q, None, waits=[('dsy', t['dsy'])])

    def build(self):
        self._prep()
        self._phase0()
        self._dir(False)
        self._dir(True)
        self.emit('sy', None, waits=[('do0', self.now('do0')), ('do1', self.now('do1'))])
        self._emit_block()
        return self.nc

    def _prep(self):
        d, sb = self.d, self.sb
        for nm, tgt in [('cIdent', 'ident'), ('cTriF', 'triF'), ('cTriB', 'triB'),
                        ('cIndF', 'indF'), ('cIndB', 'indB'), ('cOnes2', 'ones2'),
                        ('cOnes2N', 'ones2n'), ('cEvB', 'evB'), ('cOnesF', 'onesf')]:
            self.dma('sy', sb[tgt][:, :], d[nm])


        for q in ('te', 've', 'se', 'gp'):
            self.emit(q, None, waits=[('dsy', self.now('dsy'))])
        self.ve(lambda nc, e: nc.vector.memset(sb['epsc'][:, :], 1e-5), inc=[('ve', 1)])
        prev = {}
        for s in ('f', 'b'):
            wi_r = d[f'wi_{s}'].rearrange('(kt p) c -> p kt c', p=128)
            for j in range(NJ):
                jw = PROJ - (NJ - 1) * 128 if j == NJ - 1 else 128
                par = j % 2
                key = (s, j - 2) if j >= 2 else (('f', NJ - 1 - par) if s == 'b' else None)
                wts = [(f'sst{par}', prev[key][f'sst{par}'])] if key in prev else ()
                tld = self.dma('gp', sb['wpo'][:, par * 1024: par * 1024 + 8 * jw]
                               .rearrange('p (kt jj) -> p kt jj', jj=jw),
                               wi_r[:, :, j * 128: j * 128 + jw], waits=wts, sem=f'sld{par}')
                prev[(s, j)] = self.dma(
                    'sy', self.s_wi[s][j, :, :, 0:jw],
                    sb['wpo'][:, par * 1024: par * 1024 + 8 * jw]
                    .rearrange('p (kt jj) -> p kt jj', jj=jw),
                    waits=[(f'sld{par}', tld[f'sld{par}'])], sem=f'sst{par}')
        self._sst = [('sst0', self.now('sst0')), ('sst1', self.now('sst1'))]
        tlast_ve = {'ve': self.now('ve')}
        tlast_st = {'wst': 0}
        first_hl = [('sst0', self.now('sst0')), ('sst1', self.now('sst1'))]
        for wnm, base in [('wfr', 0), ('wfi', 2)]:
            for kt in range(8):
                tld = self.dma('gp', sb['wprep'][:, :], d[wnm][kt * 128:(kt + 1) * 128, :],
                               waits=[('ve', tlast_ve['ve'])], sem='wld')
                self.ve(lambda nc, e: nc.vector.tensor_copy(sb['wpo'][:, 0:1024], sb['wprep'][:, :]),
                        waits=[('wld', tld['wld']), ('wst', tlast_st['wst'])] + first_hl)
                first_hl = []
                tlast_ve = self.ve(lambda nc, e: nc.vector.tensor_sub(
                    sb['wpo'][:, 1024:2048], sb['wprep'][:, :], sb['wpo'][:, 0:1024]),
                    inc=[('ve', 1)])
                self.dma('sy', self.s_w[base, kt].transpose([1, 0, 2]),
                         sb['wpo'][:, 0:1024].rearrange('p (ot c) -> p ot c', c=128),
                         waits=[('ve', tlast_ve['ve'])], sem='wst')
                tlast_st = self.dma('sy', self.s_w[base + 1, kt].transpose([1, 0, 2]),
                                    sb['wpo'][:, 1024:2048].rearrange('p (ot c) -> p ot c', c=128),
                                    waits=[('ve', tlast_ve['ve'])], sem='wst')
        for src, tgt in [('bm', 'bmr'), ('fbias', 'fbr')]:
            tld = self.dma('gp', sb['wprep'][0:1, :], d[src].rearrange('(a n) -> a n', a=1),
                           waits=[('ve', self.now('ve')), ('wld', self.now('wld')),
                                  ('bld', self.now('bld'))], sem='bld')
            self.ve(lambda nc, e, tgt=tgt: nc.vector.tensor_copy(
                sb[tgt][0:1, :], sb['wprep'][0:1, :]), waits=[('bld', tld['bld'])])
            tlo = self.ve(lambda nc, e, tgt=tgt: nc.vector.tensor_sub(
                sb['t1'][0:1, 0:1024], sb['wprep'][0:1, :], sb[tgt][0:1, :]), inc=[('ve', 1)])
            self.dma('gp', sb[tgt][1:2, :], sb['t1'][0:1, 0:1024],
                     waits=[('ve', tlo['ve'])], sem='bld')

    def _ph0_evac(self, pd, dst_h, dst_l, first=(), tiled=False):
        sb = self.sb
        self.ve(lambda nc, e, pd=pd: nc.vector.tensor_copy(sb['wpo'][:, 0:512], pd),
                waits=[('te', self.now('te')), ('pev', self.now('pev'))] + list(first))
        tev = self.ve(lambda nc, e, pd=pd: nc.vector.tensor_sub(
            sb['wpo'][:, 512:1024], pd, sb['wpo'][:, 0:512]), inc=[('ve', 1)])
        src_h, src_l = sb['wpo'][:, 0:512], sb['wpo'][:, 512:1024]
        if tiled:
            src_h = src_h.rearrange('p (mt c) -> p mt c', c=128)
            src_l = src_l.rearrange('p (mt c) -> p mt c', c=128)
            dst_h = dst_h.transpose([1, 0, 2])
            dst_l = dst_l.transpose([1, 0, 2])
        self.dma('sy', dst_h, src_h, waits=[('ve', tev['ve'])], sem='pev')
        self.dma('sy', dst_l, src_l, waits=[('ve', tev['ve'])], sem='pev')
        return tev

    def _phase0(self):
        self.emit('sy', None, waits=[('dsy', self.now('dsy')), ('sst0', self.now('sst0')),
                                     ('sst1', self.now('sst1')), ('wst', self.now('wst'))])
        d, sb = self.d, self.sb
        CONST1 = ['cFr_h', 'cFr_l', 'cGi_h', 'cGi_l', 'cFi_h', 'cFi_l']
        terms_r = [(0, 0), (1, 0), (0, 1), (2, 2), (3, 2), (2, 3)]
        terms_i = [(2, 0), (3, 0), (2, 1), (0, 4), (1, 4), (0, 5)]
        lhs_tok = [{'te': 0}, {'te': 0}, {'te': 0}, {'te': 0}]
        wst_tot = self.now('wst')
        first_ev = [('sst0', self.now('sst0')), ('sst1', self.now('sst1')),
                    ('wst', self.now('wst'))]

        def rhs_cache(q):
            col = q * 512
            if col < 16384:
                return sb['b_wo'][:, col:col + 512]
            return sb['zsil'][:, col - 16384:col - 16384 + 512]

        def load_cache(tiles):
            gen = [('te', self.now('te'))]
            for q, (cn, r0, c0) in enumerate(tiles):
                self.dma('sy', rhs_cache(q), d[cn][r0:r0 + 128, c0:c0 + 512],
                         waits=gen, sem='pc')
            return self.now('pc')

        for mh in range(2):
            pc_tok = load_cache([(cn, kt * 128, mh * 512)
                                 for kt in range(8) for cn in CONST1])
            for ot in range(8):
                i_r = self.next_slot()
                i_i = self.next_slot()
                pdr, pdi = self.slots[i_r], self.slots[i_i]
                for kt in range(8):
                    kp = kt % 4
                    t_lhs = None
                    for wi_i in range(4):
                        t_lhs = self.dma('sy', sb['lstr'][:, (kp * 4 + wi_i) * 128:(kp * 4 + wi_i + 1) * 128],
                                         self.s_w[wi_i, kt, ot],
                                         waits=[('te', lhs_tok[kp]['te']), ('wst', wst_tot)],
                                         sem=f'ph{kp}')
                    for ri, (pd, idx, terms) in enumerate(((pdr, i_r, terms_r), (pdi, i_i, terms_i))):
                        for ti, (wl, cc) in enumerate(terms):
                            w0 = []
                            if ri == 0 and ti == 0:
                                w0 = [(f'ph{kp}', t_lhs[f'ph{kp}']), ('pc', pc_tok)]
                                if kt == 0:
                                    w0 += [('ve', self.slot_free[i_r]['ve']), ('se', self.slot_free[i_r]['se']),
                                           ('ve', self.slot_free[i_i]['ve']), ('se', self.slot_free[i_i]['se'])]
                            last = (ri == 1 and ti == 5)
                            self.mm(pd, sb['lstr'][:, (kp * 4 + wl) * 128:(kp * 4 + wl + 1) * 128],
                                    rhs_cache(kt * 6 + cc),
                                    start=(kt == 0 and ti == 0), stop=(kt == 7 and ti == 5),
                                    waits=w0, inc=[('te', 1)] if last else ())
                    lhs_tok[kp] = {'te': self.now('te')}
                for ri, (pd, idx) in enumerate(((pdr, i_r), (pdi, i_i))):
                    tev = self._ph0_evac(
                        pd,
                        self.s_a[ri * 2, ot, mh * 4:(mh + 1) * 4],
                        self.s_a[ri * 2 + 1, ot, mh * 4:(mh + 1) * 4],
                        first=first_ev, tiled=True)
                    first_ev = []
                    self.slot_free[idx] = {'ve': tev['ve'], 'se': 0}
        self.emit('sy', None, waits=[('pev', self.now('pev')), ('ph0', self.now('ph0')),
                                     ('ph1', self.now('ph1')), ('ph2', self.now('ph2')),
                                     ('ph3', self.now('ph3'))])
        terms2 = [(0, 0), (1, 0), (0, 1), (2, 2), (3, 2), (2, 3)]
        pev_s1 = self.now('pev')
        for oh in range(2):
            pc_tok = load_cache([(cn, ot * 128, oh * 512)
                                 for ot in range(8)
                                 for cn in ('cFrd_h', 'cFrd_l', 'cFid_h', 'cFid_l')])
            for mtl in range(8):
                idx = self.next_slot()
                pd = self.slots[idx]
                for ot in range(8):
                    kp = ot % 4
                    t_lhs = None
                    for ai in range(4):
                        t_lhs = self.dma('sy', sb['lstr'][:, (kp * 4 + ai) * 128:(kp * 4 + ai + 1) * 128],
                                         self.s_a[ai, ot, mtl],
                                         waits=[('te', lhs_tok[kp]['te']), ('pev', pev_s1)],
                                         sem=f'ph{kp}')
                    for ti, (al, cc) in enumerate(terms2):
                        w0 = []
                        if ti == 0:
                            w0 = [(f'ph{kp}', t_lhs[f'ph{kp}']), ('pc', pc_tok)]
                            if ot == 0:
                                w0 += [('ve', self.slot_free[idx]['ve']), ('se', self.slot_free[idx]['se'])]
                        self.mm(pd, sb['lstr'][:, (kp * 4 + al) * 128:(kp * 4 + al + 1) * 128],
                                rhs_cache(ot * 4 + cc),
                                start=(ot == 0 and ti == 0), stop=(ot == 7 and ti == 5),
                                waits=w0, inc=[('te', 1)] if ti == 5 else ())
                    lhs_tok[kp] = {'te': self.now('te')}
                tev = self._ph0_evac(
                    pd,
                    self.s_m[0, mtl, oh],
                    self.s_m[1, mtl, oh])
                self.slot_free[idx] = {'ve': tev['ve'], 'se': 0}

    def _dir(self, is_b):
        d, sb, tk = self.d, self.sb, self.tk
        self.emit('sy', None, waits=[('dsy', self.now('dsy')), ('pev', self.now('pev')),
                                     ('yf0', self.now('yf0')), ('yf1', self.now('yf1'))])
        dirw = 'b' if is_b else 'f'
        gd = tk['group_done']
        wo_gate = self.now('te')
        t_wo = None
        for kt in range(16):
            t_wo = self.dma('gp', sb['b_wo'][:, kt * 1024:(kt + 1) * 1024],
                            d[f'wo_{dirw}'][kt * 128:(kt + 1) * 128, :],
                            waits=[('te', wo_gate)], sem='wo')

        gdw = [('te', gd['te']), ('ve', gd['ve']), ('se', gd['se'])]
        self.dma('gp', sb['cw'][:, :].rearrange('p (tau ct) -> p tau ct', ct=18),
                 d[f'convw_{dirw}'].rearrange('tau (ct p) -> p tau ct', p=128), slow=True,
                 waits=gdw, sem='dp')
        self.dma('gp', sb['cb'][:, :], d[f'convb_{dirw}'].rearrange('(ct p) -> p ct', p=128),
                 slow=True, waits=gdw, sem='dp')
        t_al = self.dma('gp', sb['aexp'][:, :], d[f'alog_{dirw}'].rearrange('(a b) -> a b', b=1),
                        slow=True, waits=gdw, sem='dp')
        self.dma('gp', sb['dtbc'][:, :], d[f'dtb_{dirw}'].rearrange('(a b) -> a b', b=1),
                 slow=True, waits=gdw, sem='dp')
        t_dd = self.dma('gp', sb['ddiag'][:, :], d[f'ddg_{dirw}'],
                        waits=[('te', gd['te']), ('ve', gd['ve'])], sem='dd')
        tse = self.se(lambda nc, e: nc.scalar.activation(sb['negA'][:, :], sb['aexp'][:, :], AF.Exp),
                      waits=[('dp', self.now('dp'))], inc=[('se', 1)])
        t_na = self.ve(lambda nc, e: nc.vector.tensor_scalar_mul(sb['negA'][:, :], sb['negA'][:, :], -1.0),
                       waits=[('se', tse['se'])], inc=[('ve', 1)])
        t_h0 = self.ve(lambda nc, e: nc.vector.memset(sb['hst'][:, :], 0.0),
                       waits=[('te', tk['group_done']['te'])], inc=[('ve', 1)])
        self.ve(lambda nc, e: nc.vector.memset(sb['cstash'][:, :], 0.0), inc=[('ve', 1)])
        t_dirprep = {'dp': self.now('dp'), 've': self.now('ve')}

        tk_hup = t_h0
        tk_zsil_free = {'te': tk['group_done']['te'], 've': 0}
        tk_xbcs_free = {'te': tk['group_done']['te'], 'gp': tk['group_done']['gp']}
        tk_dt_free = {'ve': 0, 'co': 0}
        tk_xnT_free = {'te': tk['group_done']['te']}
        tk_cr_free = {'te': 0}
        tk_wv_users = {'te': 0}
        tk_mt_gp = {'gp': 0}
        tk_crc_users = {'te': 0}
        t_ddg = t_dd

        groups = list(range(NG - 1, -1, -1)) if is_b else list(range(NG))
        for g in groups:
            chunks = list(range(NCH - 1, -1, -1)) if is_b else list(range(NCH))
            tk_ln_sq = {'ve': getattr(self, '_conv_last', 0)}
            tk_ln_tr = {'te': 0}
            for c in chunks:
                gc = g * NCH + c
                par = gc % 2
                xqs = sb['xq'][:, par * 1024:(par + 1) * 1024]
                tld = self.dma('sy', xqs, d['x'][gc * 128:(gc + 1) * 128, :],
                               waits=[('ve', tk['xq_free'][par]['ve']),
                                      ('dp', t_dirprep['dp'])], sem=f'xq{par}')
                tsqr = self.se(lambda nc, e, xqs=xqs: nc.scalar.activation(
                    sb['wprep'][:, :], xqs, AF.Square),
                    waits=[(f'xq{par}', tld[f'xq{par}']), ('ve', tk_ln_sq['ve'])],
                    inc=[('se', 1)])
                def vc(fn, waits=()):
                    t = self.ve(fn, waits=list(waits) + [('ve', self.now('ve'))], inc=[('ve', 1)])
                    return t
                vc(lambda nc, e, xqs=xqs: nc.vector.reduce_sum(
                    sb['mean'][:, :], xqs, axis=mybir.AxisListType.X),
                    waits=[(f'xq{par}', tld[f'xq{par}'])])
                vc(lambda nc, e: nc.vector.tensor_scalar_mul(sb['mean'][:, :], sb['mean'][:, :], 1.0 / DM))
                tk_ln_sq = vc(lambda nc, e: nc.vector.reduce_sum(
                    sb['sqs'][:, :], sb['wprep'][:, :], axis=mybir.AxisListType.X),
                    waits=[('se', tsqr['se'])])
                vc(lambda nc, e: nc.vector.tensor_scalar_mul(sb['sqs'][:, :], sb['sqs'][:, :], 1.0 / DM))
                vc(lambda nc, e: nc.vector.tensor_mul(sb['rstd'][:, :], sb['mean'][:, :], sb['mean'][:, :]))
                tvar = vc(lambda nc, e: nc.vector.tensor_sub(sb['sqs'][:, :], sb['sqs'][:, :], sb['rstd'][:, :]))
                tsq = self.se(lambda nc, e: nc.scalar.activation(sb['rstd'][:, :], sb['sqs'][:, :], AF.Sqrt,
                                                                 bias=sb['epsc'][:, :]),
                              waits=[('ve', tvar['ve'])], inc=[('se', 1)])
                vc(lambda nc, e: nc.vector.reciprocal(sb['rstd'][:, :], sb['rstd'][:, :]),
                   waits=[('se', tsq['se'])])
                vc(lambda nc, e: nc.vector.tensor_mul(sb['nmrs'][:, :], sb['mean'][:, :], sb['rstd'][:, :]))
                vc(lambda nc, e: nc.vector.tensor_scalar_mul(sb['nmrs'][:, :], sb['nmrs'][:, :], -1.0))
                tnx = vc(lambda nc, e, xqs=xqs: nc.vector.tensor_scalar(
                    sb['xnch'][:, :], xqs, sb['rstd'][:, :], sb['nmrs'][:, :], ALU.mult, ALU.add),
                    waits=[('te', tk_ln_tr['te'])])
                if is_b:
                    vc(lambda nc, e, xqs=xqs: nc.vector.scalar_tensor_tensor(
                        sb['wprep'][:, :], xqs, sb['rstd'][:, :], sb['xnch'][:, :],
                        ALU.mult, ALU.subtract))
                    tnx = vc(lambda nc, e: nc.vector.tensor_scalar(
                        sb['xncl'][:, :], sb['wprep'][:, :], sb['nmrs'][:, :], None, ALU.add))
                self.tk['xq_free'][par] = tnx
                tk_ln_sq = {'ve': self.now('ve')}
                passes = [('xnch', 'xnT')] + ([('xncl', 'xnTl')] if is_b else [])
                for srcb, dstb in passes:
                    for kt in range(8):
                        self.trans(self.ps['pT'][:, kt * 128:(kt + 1) * 128],
                                   sb[srcb][:, kt * 128:(kt + 1) * 128],
                                   waits=([('ve', tnx['ve']), ('ve', tk['pT_free']['ve']),
                                           ('te', tk_xnT_free['te'])] if kt == 0 else ()),
                                   inc=[('te', 1)] if kt == 7 else ())
                    tkte = self.now('te')
                    tk_ln_tr = {'te': tkte}
                    dst = sb[dstb][:, :].rearrange('p (kt t) -> p kt t', t=512)[:, :, c * 128:(c + 1) * 128]
                    tk['pT_free'] = self.ve(lambda nc, e, dst=dst: nc.vector.tensor_copy(
                        dst, self.ps['pT'][:, :].rearrange('p (kt t) -> p kt t', t=128)),
                        waits=[('te', tkte)], inc=[('ve', 1)])
            tk_ln = {'ve': self.now('ve')}

            jorder = [34] + list(range(16, 34)) + list(range(16))
            t_wld = {}
            def load_wi(ii):
                j = jorder[ii]
                jw = PROJ - (NJ - 1) * 128 if j == NJ - 1 else 128
                bpar = ii % 4
                t_wld[j] = self.dma('sy', sb['b_wi'][:, bpar * 1024: bpar * 1024 + 8 * jw]
                                    .rearrange('p (kt jj) -> p kt jj', jj=jw),
                                    self.s_wi[dirw][j, :, :, 0:jw],
                                    waits=[('te', tk['wi_free'][bpar]['te'])] + self._sst,
                                    sem=f'wi{bpar}')
            load_wi(0); load_wi(1); load_wi(2); load_wi(3)
            par_evac = [dict(self.slot_free[0]), dict(self.slot_free[1])]
            tk_conv_done = {}
            for ii, j in enumerate(jorder):
                jw = PROJ - (NJ - 1) * 128 if j == NJ - 1 else 128
                ppar = ii % 2
                bsl = ii % 4
                pd = self.slots[ppar] if jw == 128 else self.ps['pS%d' % ppar][0:32, :]
                for kt in range(8):
                    w0 = []
                    if kt == 0:
                        w0 = [(f'wi{bsl}', t_wld[j][f'wi{bsl}']), ('ve', tk_ln['ve']),
                              ('se', par_evac[ppar]['se']), ('ve', par_evac[ppar]['ve'])]
                    self.mm(pd, sb['b_wi'][:, bsl * 1024 + kt * jw: bsl * 1024 + (kt + 1) * jw],
                            sb['xnT'][:, kt * 512:(kt + 1) * 512],
                            start=(kt == 0), stop=(kt == 7),
                            waits=w0, inc=[('te', 1)] if kt == 7 else ())
                tk['wi_free'][bsl] = {'te': self.now('te')}
                if ii + 4 < NJ:
                    load_wi(ii + 4)
                tkte = self.now('te')
                if j < 16:
                    tz = self.se(lambda nc, e, j=j, ppar=ppar: nc.scalar.activation(
                        sb['zsil'][:, j * 512:(j + 1) * 512], self.slots[ppar], AF.Silu),
                        waits=[('te', tkte), ('te', tk_zsil_free['te']),
                               ('ve', tk_zsil_free['ve'])], inc=[('se', 1)])
                    par_evac[ppar] = {'se': tz['se'], 've': 0}
                elif j < 34:
                    ct = j - 16
                    cpar = ct % 2
                    if not is_b:
                        data_off = cpar * 520 + 3; stash_off = cpar * 520
                        save_off = cpar * 520 + 512; tap = [0, 1, 2, 3]
                    else:
                        data_off = cpar * 520; stash_off = cpar * 520 + 512
                        save_off = cpar * 520; tap = [3, 2, 1, 0]
                    tx = self.se(lambda nc, e, ppar=ppar, o=data_off: nc.scalar.activation(
                        sb['tmpc'][:, o:o + 512], self.slots[ppar], AF.Copy),
                        waits=[('te', tkte),
                               ('ve', tk_conv_done.get(ct - 2, {'ve': 0})['ve'])],
                        inc=[('se', 1)])
                    par_evac[ppar] = {'se': tx['se'], 've': 0}
                    self.ve(lambda nc, e, ct=ct, so=stash_off: nc.vector.tensor_copy(
                        sb['tmpc'][:, so:so + 3], sb['cstash'][:, ct * 3:ct * 3 + 3]),
                        waits=[('se', tx['se'])])
                    base = cpar * 520
                    self.ve(lambda nc, e, b=base, ct=ct, t0=tap[0], cpar=cpar: nc.vector.tensor_scalar(
                        sb['convt'][:, cpar * 512:(cpar + 1) * 512], sb['tmpc'][:, b + t0: b + t0 + 512],
                        sb['cw'][:, ct:ct + 1], None, ALU.mult))
                    for tau in (1, 2, 3):
                        self.ve(lambda nc, e, b=base, ct=ct, tau=tau, to=tap[tau], cpar=cpar:
                                nc.vector.scalar_tensor_tensor(
                                    sb['convt'][:, cpar * 512:(cpar + 1) * 512],
                                    sb['tmpc'][:, b + to: b + to + 512],
                                    sb['cw'][:, tau * 18 + ct: tau * 18 + ct + 1],
                                    sb['convt'][:, cpar * 512:(cpar + 1) * 512], ALU.mult, ALU.add))
                    tcv = self.ve(lambda nc, e, ct=ct, so=save_off: nc.vector.tensor_copy(
                        sb['cstash'][:, ct * 3:ct * 3 + 3], sb['tmpc'][:, so:so + 3]),
                        inc=[('ve', 1)])
                    tk_conv_done[ct] = tcv
                    self.se(lambda nc, e, ct=ct, cpar=cpar: nc.scalar.activation(
                        sb['xbcs'][:, ct * 512:(ct + 1) * 512], sb['convt'][:, cpar * 512:(cpar + 1) * 512],
                        AF.Silu, bias=sb['cb'][:, ct:ct + 1]),
                        waits=[('ve', tcv['ve']), ('te', tk_xbcs_free['te']),
                               ('gp', tk_xbcs_free['gp'])], inc=[('se', 1)])
                else:
                    self.se(lambda nc, e, ppar=ppar: nc.scalar.activation(
                        sb['lgda'][:, :], self.ps['pS%d' % ppar][0:32, :], AF.Abs,
                        bias=sb['dtbc'][:, :]),
                        waits=[('te', tkte), ('ve', tk_dt_free['ve']),
                               ('co', tk_dt_free['co']), ('dp', t_dirprep['dp'])],
                        inc=[('se', 1)])
                    self.se(lambda nc, e: nc.scalar.activation(
                        sb['cum'][:, :], sb['lgda'][:, :], AF.Exp, scale=-1.0))
                    self.se(lambda nc, e: nc.scalar.activation(
                        sb['cum'][:, :], sb['cum'][:, :], AF.Ln, bias=sb['onesf'][0:32, 0:1]))
                    tdt = self.se(lambda nc, e, ppar=ppar: nc.scalar.activation(
                        sb['dt_'][:, :], self.ps['pS%d' % ppar][0:32, :], AF.Relu,
                        bias=sb['dtbc'][:, :]), inc=[('se', 1)])
                    par_evac[ppar] = {'se': tdt['se'], 've': 0}
                    self._t_dtadd = self.ve(lambda nc, e: nc.vector.tensor_add(
                        sb['dt_'][:, :], sb['dt_'][:, :], sb['cum'][:, :]),
                        waits=[('se', tdt['se'])], inc=[('ve', 1)])
                if j == 33:
                    conv_se = self.now('se')
            self._conv_last = tk_conv_done[17]['ve']
            tk_sweep = {'se': self.now('se'), 've': self.now('ve'), 'te': self.now('te')}
            self.slot_free[0] = {'ve': 0, 'se': tk_sweep['se']}
            self.slot_free[1] = {'ve': 0, 'se': tk_sweep['se']}

            tlgda = self.ve(lambda nc, e: nc.vector.tensor_scalar(
                sb['lgda'][:, :], sb['dt_'][:, :], sb['negA'][:, :], None, ALU.mult),
                waits=[('ve', self._t_dtadd['ve']), ('ve', t_na['ve']),
                       ('se', getattr(self, '_dtp_se', 0))], inc=[('ve', 1)])
            tscan = tlgda
            for c in range(NCH):
                tscan = self.ve(lambda nc, e, c=c: nc.vector.tensor_tensor_scan(
                    sb['cum'][:, c * 128:(c + 1) * 128], sb['lgda'][:, c * 128:(c + 1) * 128],
                    sb['lgda'][:, c * 128:(c + 1) * 128], 0.0, ALU.add, ALU.bypass),
                    waits=[('ve', tscan['ve'])], inc=[('ve', 1)])
            if is_b:
                self.ve(lambda nc, e: nc.vector.tensor_sub(sb['cump'][:, :], sb['cum'][:, :], sb['lgda'][:, :]),
                        waits=[('ve', tscan['ve'])])
            tlg = self.se(lambda nc, e: nc.scalar.activation(sb['lgdt'][:, :], sb['dt_'][:, :], AF.Ln),
                          waits=[('ve', self._t_dtadd['ve'])], inc=[('se', 1)])
            CC = sb['cump'] if is_b else sb['cum']
            self.ve(lambda nc, e, CC=CC: nc.vector.tensor_copy(sb['cumhi'][:, :], CC[:, :]),
                    waits=[('se', tlg['se']), ('pk', self.now('pk')), ('sc', self.now('sc')),
                           ('ve', tscan['ve'])])
            self.ve(lambda nc, e, CC=CC: nc.vector.tensor_tensor(
                sb['cumlo'][:, :], CC[:, :], sb['cumhi'][:, :], ALU.subtract))
            self.ve(lambda nc, e: nc.vector.tensor_copy(sb['ldthi'][:, :], sb['lgdt'][:, :]))
            self.ve(lambda nc, e: nc.vector.tensor_tensor(
                sb['ldtlo'][:, :], sb['lgdt'][:, :], sb['ldthi'][:, :], ALU.subtract))
            if is_b:
                self.ve(lambda nc, e: nc.vector.tensor_add(sb['wvin'][:, :], sb['lgdt'][:, :], sb['cump'][:, :]),
                        waits=[('se', tlg['se'])], inc=[('ve', 1)])
            else:
                self.ve(lambda nc, e: nc.vector.tensor_sub(sb['wvin'][:, :], sb['lgdt'][:, :], sb['cum'][:, :]),
                        waits=[('se', tlg['se'])], inc=[('ve', 1)])
            for c in range(NCH):
                kw = dict(bias=sb['cum'][:, c * 128 + 127:c * 128 + 128]) if not is_b else {}
                self.se(lambda nc, e, c=c, kw=kw: nc.scalar.activation(
                    sb['wv'][:, c * 128:(c + 1) * 128], sb['wvin'][:, c * 128:(c + 1) * 128],
                    AF.Exp, **kw),
                    waits=[('ve', self.now('ve')), ('te', tk_wv_users['te'])], inc=[('se', 1)])
            tpq = self.se(lambda nc, e: nc.scalar.activation(
                sb['pqb'][:, :], sb['cum'][:, 127:512:128], AF.Exp),
                waits=[('ve', tscan['ve']), ('sc', self.now('sc'))], inc=[('se', 1)])
            if is_b:
                for c in range(NCH):
                    def vc2(fn, waits=()):
                        return self.ve(fn, waits=list(waits) + [('ve', self.now('ve'))],
                                       inc=[('ve', 1)])
                    vc2(lambda nc, e, c=c: nc.vector.tensor_copy(
                        sb['clsc'][:, 0:1], sb['cum'][:, c * 128 + 127:c * 128 + 128]),
                        waits=[('se', tpq['se'])])
                    vc2(lambda nc, e, c=c: nc.vector.tensor_scalar(
                        sb['clh'][:, c * 128:(c + 1) * 128], sb['onesf'][:, 0:128],
                        sb['clsc'][:, 0:1], None, ALU.mult))
                    vc2(lambda nc, e, c=c: nc.vector.tensor_tensor(
                        sb['clsc'][:, 1:2], sb['clsc'][:, 0:1],
                        sb['clh'][:, c * 128:c * 128 + 1], ALU.subtract))
                    vc2(lambda nc, e, c=c: nc.vector.tensor_scalar(
                        sb['cll'][:, c * 128:(c + 1) * 128], sb['onesf'][:, 0:128],
                        sb['clsc'][:, 1:2], None, ALU.mult))
            tk_dtp = {'ve': self.now('ve'), 'se': self.now('se')}
            self._dtp_se = tk_dtp['se']
            wcr = [('ve', tk_dtp['ve']), ('se', tk_dtp['se']), ('te', tk_cr_free['te'])]
            wcr = wcr + [('em', self.now('em'))]
            for qq, srcb in enumerate(('cumhi', 'cumlo', 'ldthi', 'ldtlo')):
                self.dma('gp', sb['pk32'][qq:128:4, :], sb[srcb][:, :],
                         waits=wcr, sem='pk')
            tk_pk = {'pk': self.now('pk')}
            crstores = [(0, 'cumhi'), (1, 'cumlo')] + ([(2, 'clh'), (3, 'cll')] if is_b else [])
            wsc = [('ve', tk_dtp['ve']), ('se', tk_dtp['se']),
                   ('cr', self.now('cr')), ('co', self.now('co'))]
            for row, srcb in crstores:
                self.dma('gp', self.s_cr[row, :, :], sb[srcb][:, :], waits=wsc, sem='sc')
            self.dma('gp', self.s_cr[4, :, 0:4], sb['pqb'][:, :], waits=wsc, sem='sc')
            tk_crst = {'sc': self.now('sc')}
            t_coll = self.dma('gp', sb['pqrow'][0:1, :].rearrange('a (p c) -> a p c', c=4),
                              self.s_cr[4, :, 0:4].unsqueeze(0),
                              waits=[('sc', tk_crst['sc']), ('te', tk_cr_free['te'])], sem='co')
            tk_coll = {'co': self.now('co')}
            tk_dt_free = {'ve': tk_dtp['ve'], 'co': tk_coll['co']}

            for ci, c in enumerate(chunks):
                gc = g * NCH + c
                par = gc % 2
                crpairs = [(0, 'cumhi'), (1, 'cumlo')] + ([(2, 'clh'), (3, 'cll')] if is_b else [])
                for row, srcb in crpairs:
                    self.dma('gp', sb['crc'][row:row + 1, :].rearrange('a (h t) -> a h t', t=128),
                             self.s_cr[row, :, c * 128:(c + 1) * 128].unsqueeze(0),
                             waits=[('te', tk_crc_users['te']), ('sc', tk_crst['sc'])],
                             sem='cr')
                tk_coll_c = {'cr': self.now('cr')}
                for bb8 in range(8):
                    self.dma('gp', sb['emc'][:, bb8 * 128:(bb8 + 1) * 128],
                             sb['pk32'][bb8 * 16:(bb8 + 1) * 16, c * 128:(c + 1) * 128],
                             waits=[('te', tk_crc_users['te']), ('pk', tk_pk['pk'])],
                             sem='em')
                tk_em = {'em': self.now('em')}
                Btile = sb['xbcs'][:, 16 * 512 + c * 128: 16 * 512 + c * 128 + 128]
                Ctile = sb['xbcs'][:, 17 * 512 + c * 128: 17 * 512 + c * 128 + 128]
                for half in range(2):
                    for k8 in range(8):
                        cti = half * 8 + k8
                        self.trans(self.ps['pT'][:, k8 * 128:(k8 + 1) * 128],
                                   sb['xbcs'][:, cti * 512 + c * 128: cti * 512 + c * 128 + 128],
                                   waits=([('se', conv_se), ('ve', tk['pT_free']['ve']),
                                           ('gp', self.now('gp'))] if k8 == 0 else ()),
                                   inc=[('te', 1)] if k8 == 7 else ())
                    tkte = self.now('te')
                    tk['pT_free'] = self.ve(lambda nc, e, half=half: nc.vector.tensor_copy(
                        sb['xt'][:, half * 1024:(half + 1) * 1024], self.ps['pT'][:, :]),
                        waits=[('te', tkte)], inc=[('ve', 1)])
                self.trans(self.ps['pT'][:, 0:128], Btile,
                           waits=[('ve', tk['pT_free']['ve'])], inc=[('te', 1)])
                self.trans(self.ps['pT'][:, 128:160], sb['wv'][0:32, c * 128:(c + 1) * 128],
                           waits=[('se', tk_dtp['se'])], inc=[('te', 1)])
                tkte = self.now('te')
                tk_wv_users = {'te': tkte}
                self.ve(lambda nc, e: nc.vector.tensor_copy(sb['Bt'][:, :], self.ps['pT'][:, 0:128]),
                        waits=[('te', tkte)])
                tk['pT_free'] = self.ve(lambda nc, e: nc.vector.tensor_copy(
                    sb['wct'][:, :], self.ps['pT'][:, 128:160]), inc=[('ve', 1)])
                t_xt = tk['pT_free']
                isl = self.next_slot()
                gsl = self.slots[isl]
                self.mm(gsl[:, 0:128], Btile, Ctile, start=True, stop=True,
                        waits=[('ve', self.slot_free[isl]['ve']), ('se', self.slot_free[isl]['se']),
                               ('se', conv_se)], inc=[('te', 1)])
                tgm = self.ve(lambda nc, e, gsl=gsl, m=('triB' if is_b else 'triF'):
                              nc.vector.tensor_tensor(sb['gmask'][:, :], gsl[:, 0:128],
                                                      sb[m][:, :], ALU.mult),
                              waits=[('te', self.now('te'))], inc=[('ve', 1)])
                self.slot_free[isl] = {'ve': tgm['ve'], 'se': 0}
                grep = sb['gmask'][:, :].unsqueeze(1).broadcast_to([128, 16, 128])
                crep = Ctile.unsqueeze(1).broadcast_to([128, 16, 128])
                se_exp_tok = {'se': 0}
                tok_mt, tok_mv = [], []
                for half in range(2):
                    mtb = sb['mt' if half == 0 else 'mt2']
                    lvb = sb['lv' if half == 0 else 'lv2']
                    lv_tok = None
                    for bb in range(4):
                        b = half * 4 + bb
                        i_e = self.next_slot()
                        pe = self.slots[i_e]
                        crs = sb['crc'][0:2, b * 512:(b + 1) * 512]
                        self.mm(pe, sb['ones2n' if is_b else 'ones2'][:, :], crs,
                                start=True, stop=False,
                                waits=[('cr', tk_coll_c['cr']), ('em', tk_em['em']),
                                       ('ve', self.slot_free[i_e]['ve']),
                                       ('se', self.slot_free[i_e]['se'])])
                        self.mm(pe, sb['emc'][0:16, b * 128: b * 128 + 128],
                                sb['indB' if is_b else 'indF'][:, :], start=False, stop=True,
                                inc=[('te', 1)])
                        tke = self.now('te')
                        i_v = self.next_slot()
                        pv = self.slots[i_v]
                        if is_b:
                            self.mm(pv, sb['evB'][:, :], sb['crc'][0:4, b * 512:(b + 1) * 512],
                                    start=True, stop=True,
                                    waits=[('ve', self.slot_free[i_v]['ve']),
                                           ('se', self.slot_free[i_v]['se'])], inc=[('te', 1)])
                        else:
                            self.mm(pv, sb['ones2'][:, :], crs, start=True, stop=True,
                                    waits=[('ve', self.slot_free[i_v]['ve']),
                                           ('se', self.slot_free[i_v]['se'])], inc=[('te', 1)])
                        tkv = self.now('te')
                        e_tok = self.ve(lambda nc, e, pe=pe: nc.vector.tensor_scalar_min(
                            pe, pe, 30.0),
                            waits=[('te', tke)], inc=[('ve', 1)])
                        if self.debug and is_b and gc == 30 and half == 0 and bb == 0:
                            self.dbg_dump([('decl', pe)])
                        w_exp = [('ve', e_tok['ve'])]
                        if bb == 0:
                            w_exp.append(('te', tk['ymm_free']['te']))
                        se_exp_tok = self.se(lambda nc, e, bb=bb, mtb=mtb, pe=pe: nc.scalar.activation(
                            mtb[:, bb * 512:(bb + 1) * 512],
                            pe, AF.Exp),
                            waits=w_exp, inc=[('se', 1)])
                        self.slot_free[i_e] = {'ve': 0, 'se': se_exp_tok['se']}
                        lv_tok = self.se(lambda nc, e, pv=pv, bb=bb, lvb=lvb: nc.scalar.activation(
                            lvb[:, bb * 512:(bb + 1) * 512], pv, AF.Exp),
                            waits=[('te', tkv)], inc=[('se', 1)])
                        self.slot_free[i_v] = {'ve': 0, 'se': lv_tok['se']}
                    self.ve(lambda nc, e, grep=grep, mtb=mtb: nc.vector.tensor_tensor(
                        mtb[:, :].rearrange('p (h t) -> p h t', t=128),
                        mtb[:, :].rearrange('p (h t) -> p h t', t=128), grep, ALU.mult),
                        waits=[('se', se_exp_tok['se']), ('ve', tgm['ve'])])
                    tok_mt.append(self.ve(lambda nc, e, half=half, mtb=mtb: nc.vector.tensor_tensor(
                        mtb[:, :], mtb[:, :],
                        sb['ddiag'][:, half * 2048:(half + 1) * 2048], ALU.add),
                        waits=[('dd', t_ddg['dd'])], inc=[('ve', 1)]))
                    tok_mv.append(self.ve(lambda nc, e, crep=crep, lvb=lvb: nc.vector.tensor_tensor(
                        lvb[:, :].rearrange('p (h t) -> p h t', t=128),
                        lvb[:, :].rearrange('p (h t) -> p h t', t=128), crep, ALU.mult),
                        waits=[('se', lv_tok['se']), ('se', conv_se)], inc=[('ve', 1)]))
                tk_crc_users = {'te': self.now('te')}
                if self.debug and is_b and gc == 30:
                    self.dbg_dump([('dlexp', sb['mt2'][:, :]), ('dlv', sb['lv'][:, :]),
                                   ('dmt', sb['mt'][:, :]),
                                   ('dhprev', sb['hst'][:, :])])
                for half in range(2):
                    mtb = sb['mt' if half == 0 else 'mt2']
                    lvb = sb['lv' if half == 0 else 'lv2']
                    tmt, tmv = tok_mt[half], tok_mv[half]
                    for jj in range(8):
                        j = half * 8 + jj
                        h0, h1 = 2 * j, 2 * j + 1
                        l0, l1 = 2 * jj, 2 * jj + 1
                        slot = self.ps['pY'][:, jj * 128:(jj + 1) * 128]
                        w0 = [('ve', tmt['ve']), ('ve', tmv['ve']), ('ve', tk['pY_free']['ve']),
                              ('ve', tk_hup['ve']), ('ve', t_xt['ve'])] if jj == 0 else ()
                        self.mm(slot[0:64, :], sb['xt'][:, h0 * 64:(h0 + 1) * 64],
                                mtb[:, l0 * 128:(l0 + 1) * 128], True, False, waits=w0, tp=(0, 0))
                        self.mm(slot[0:64, :], sb['hst'][:, h0 * 64:(h0 + 1) * 64],
                                lvb[:, l0 * 128:(l0 + 1) * 128], False, True, tp=(0, 0))
                        self.mm(slot[64:128, :], sb['xt'][:, h1 * 64:(h1 + 1) * 64],
                                mtb[:, l1 * 128:(l1 + 1) * 128], True, False, tp=(0, 64))
                        self.mm(slot[64:128, :], sb['hst'][:, h1 * 64:(h1 + 1) * 64],
                                lvb[:, l1 * 128:(l1 + 1) * 128], False, True, tp=(0, 64),
                                inc=[('te', 1)] if jj == 7 else ())
                    tk_ymm = {'te': self.now('te')}
                    tk['ymm_free'] = tk_ymm
                    tk['hbf_users'] = tk_ymm
                    zslc = sb['zsil'][:, :].rearrange('p (j t) -> p j t', t=512)[
                        :, half * 8:(half + 1) * 8, c * 128:(c + 1) * 128]
                    tk['pY_free'] = self.ve(lambda nc, e, zslc=zslc, par=par, half=half:
                                            nc.vector.tensor_tensor(
                        sb['yg'][:, par * 2048 + half * 1024: par * 2048 + (half + 1) * 1024]
                        .rearrange('p (j t) -> p j t', t=128),
                        self.ps['pY'][:, :].rearrange('p (j t) -> p j t', t=128),
                        zslc, ALU.mult),
                        waits=[('te', tk_ymm['te']), ('te', tk['yg_free'][par]['te']),
                               ('se', tk_sweep['se'])], inc=[('ve', 1)])
                tk_zsil_users = tk['pY_free']
                i_p = self.next_slot()
                psl = self.slots[i_p]
                self.mm(psl[:, 0:32], sb['ones2'][0:1, :], sb['pqrow'][0:1, :].rearrange('a (p c) -> a p c', c=4)[:, :, c], True, True,
                        waits=[('co', tk_coll['co']), ('ve', self.slot_free[i_p]['ve']),
                               ('se', self.slot_free[i_p]['se'])], inc=[('te', 1)])
                tpqb = self.ve(lambda nc, e, psl=psl: nc.vector.tensor_copy(
                    sb['pqbc'][:, :], psl[:, 0:32]),
                    waits=[('te', self.now('te'))], inc=[('ve', 1)])
                self.slot_free[i_p] = {'ve': tpqb['ve'], 'se': 0}
                tk_xw = self.ve(lambda nc, e: nc.vector.tensor_tensor(
                    sb['mt'][:, :].rearrange('p (h q) -> p h q', q=64),
                    sb['xt'][:, :].rearrange('p (h q) -> p h q', q=64),
                    sb['wct'][:, :].unsqueeze(2).broadcast_to([128, NH, 64]), ALU.mult),
                    waits=[('ve', t_xt['ve']), ('te', tk_ymm['te'])], inc=[('ve', 1)])
                tk_mt_gp = {'gp': 0}
                for qq in range(4):
                    i_h = self.next_slot()
                    hs = self.slots[i_h]
                    w0 = [('ve', self.slot_free[i_h]['ve']), ('se', self.slot_free[i_h]['se'])]
                    if qq == 0:
                        w0.append(('ve', tk_xw['ve']))
                    self.mm(hs, sb['Bt'][:, :], sb['mt'][:, qq * 512:(qq + 1) * 512], True, True,
                            waits=w0, inc=[('te', 1)])
                    tkh = self.now('te')
                    hsl = sb['hst'][:, qq * 512:(qq + 1) * 512]
                    self.ve(lambda nc, e, hsl=hsl, qq=qq: nc.vector.tensor_tensor(
                        hsl.rearrange('p (h q) -> p h q', q=64),
                        hsl.rearrange('p (h q) -> p h q', q=64),
                        sb['pqbc'][:, qq * 8:(qq + 1) * 8].unsqueeze(2).broadcast_to([128, 8, 64]),
                        ALU.mult),
                        waits=[('te', tkh), ('ve', tpqb['ve'])])
                    tup = self.ve(lambda nc, e, hsl=hsl, hs=hs: nc.vector.tensor_tensor(
                        hsl, hsl, hs, ALU.add), inc=[('ve', 1)])
                    self.slot_free[i_h] = {'ve': tup['ve'], 'se': 0}
                tk['ymm_free'] = {'te': self.now('te')}
                tk_hup = {'ve': self.now('ve')}
                for dh in range(2):
                    for kt in range(16):
                        w0 = [('ve', tk['pY_free']['ve']), ('wo', t_wo['wo'])] \
                            if (dh == 0 and kt == 0) else ()
                        self.mm(self.ps['pY'][:, dh * 512:(dh + 1) * 512],
                                sb['yg'][:, par * 2048 + kt * 128: par * 2048 + (kt + 1) * 128],
                                sb['b_wo'][:, kt * 1024 + dh * 512: kt * 1024 + dh * 512 + 512],
                                start=(kt == 0), stop=(kt == 15),
                                waits=w0, inc=[('te', 1)] if (kt == 15 and dh == 1) else ())
                tk_wo = {'te': self.now('te')}
                tk['yg_free'][par] = tk_wo
                if self.debug and is_b and gc == 30:
                    self.dbg_dump([('dxnT', sb['xnT'][:, :]), ('dzsil', sb['zsil'][:, :]),
                                   ('dxbcs', sb['xbcs'][:, :]), ('ddt', sb['dt_'][:, :]),
                                   ('dcum', sb['cum'][:, :]),
                                   ('dgm', sb['gmask'][:, :]),
                                   ('dyg', sb['yg'][:, 0:2048]),
                                   ('demega', sb['pk32'][:, :]), ('dcrc', sb['crc'][:, :]),
                                   ('dwv', sb['wv'][:, :]), ('dxt', sb['xt'][:, :]),
                                   ('dhst', sb['hst'][:, :])])
                if not is_b:
                    tev = self.ve(lambda nc, e, par=par: nc.vector.tensor_copy(
                        sb['yfc'][:, par * 1024:(par + 1) * 1024], self.ps['pY'][:, :]),
                        waits=[('te', tk_wo['te']), (f'yf{par}', tk['yfc_free'][par]['st'])],
                        inc=[('ve', 1)])
                    tk['pY_free'] = tev
                    tst = self.dma('sy', self.s_yf[gc * 128:(gc + 1) * 128, :],
                                   sb['yfc'][:, par * 1024:(par + 1) * 1024],
                                   waits=[('ve', tev['ve'])], sem=f'yf{par}')
                    tk['yfc_free'][par] = {'st': tst[f'yf{par}'], 've': tev['ve']}
                else:
                    tldy = self.dma('sy', sb['yfc'][:, par * 1024:(par + 1) * 1024],
                                    self.s_yf[gc * 128:(gc + 1) * 128, :],
                                    waits=[('ve', tk['yfc_free'][par]['ve']),
                                           (f'yf{par}', self.now(f'yf{par}'))], sem=f'yl{par}')
                    tev = self.ve(lambda nc, e, par=par: nc.vector.tensor_tensor(
                        sb['xm'][:, par * 1024:(par + 1) * 1024], self.ps['pY'][:, :],
                        sb['yfc'][:, par * 1024:(par + 1) * 1024], ALU.add),
                        waits=[('te', tk_wo['te']), (f'yl{par}', tldy[f'yl{par}']),
                               ('ve', tk['xm_free'][par]['ve'])],
                        inc=[('ve', 1)])
                    tk['pY_free'] = tev
                    tk['yfc_free'][par] = {'st': tk['yfc_free'][par]['st'], 've': tev['ve']}
                    tk['xm_full'][par] = tev
                    if ci % 2 == 1:
                        self._final_pair(g, [chunks[ci - 1], c])
            tk_zsil_free = {'te': self.now('te'), 've': tk_zsil_users['ve']}
            tk_xbcs_free = {'te': self.now('te'), 'gp': self.now('gp')}
            tk_cr_free = {'te': self.now('te')}
            tk_xnT_free = {'te': self.now('te')}
        self.tk['group_done'] = {'te': self.now('te'), 've': self.now('ve'),
                                 'se': self.now('se'), 'gp': self.now('gp')}

    def _final_pair(self, g, ccs):
        sb, tk, d = self.sb, self.tk, self.d
        for idx, c in enumerate(ccs):
            gc = g * NCH + c
            par = gc % 2
            txb = self.ve(lambda nc, e, par=par: nc.vector.tensor_copy(
                sb['xmb'][:, :], sb['xm'][:, par * 1024:(par + 1) * 1024]),
                waits=[('ve', tk['xm_full'][par]['ve'])], inc=[('ve', 1)])
            for kt in range(8):
                self.trans(self.ps['pT'][:, kt * 128:(kt + 1) * 128],
                           sb['xmb'][:, kt * 128:(kt + 1) * 128],
                           waits=([('ve', txb['ve']), ('ve', tk['pT_free']['ve'])] if kt == 0 else ()),
                           inc=[('te', 1)] if kt == 7 else ())
            tk['pT_free'] = self.ve(lambda nc, e: nc.vector.tensor_copy(
                sb['xmT'][:, :], self.ps['pT'][:, :]),
                waits=[('te', self.now('te'))], inc=[('ve', 1)])
            txt = tk['pT_free']
            for dh in range(2):
                i_w = self.next_slot()
                pw = self.slots[i_w]
                for kt in range(8):
                    sl = kt % 4
                    twm = self.dma('gp', sb['b_m'][:, sl * 512: (sl + 1) * 512],
                                   d['wm'].rearrange('(kt p) n -> kt p n', p=128)[kt, :, dh * 512:(dh + 1) * 512],
                                   waits=[('te', tk['bm_free'][sl]['te'])], sem=f'wm{sl}')
                    w0 = [(f'wm{sl}', twm[f'wm{sl}'])]
                    if kt == 0:
                        w0 += [('ve', self.slot_free[i_w]['ve']), ('se', self.slot_free[i_w]['se']),
                               ('ve', txt['ve'])]
                    self.mm(pw, sb['xmT'][:, kt * 128:(kt + 1) * 128],
                            sb['b_m'][:, sl * 512:(sl + 1) * 512],
                            start=(kt == 0), stop=False, waits=w0,
                            inc=[('te', 1)])
                    tk['bm_free'][sl] = {'te': self.now('te')}
                self.mm(pw, sb['ones2'][:, :], sb['bmr'][:, dh * 512:(dh + 1) * 512],
                        start=False, stop=True, inc=[('te', 1)])
                tth = self.se(lambda nc, e, pw=pw, idx=idx, dh=dh: nc.scalar.activation(
                    sb['t1'][:, idx * 1024 + dh * 512: idx * 1024 + (dh + 1) * 512], pw, AF.Tanh),
                    waits=[('te', self.now('te')),
                           (f'do{idx}', tk['outb_free'][idx].get('do', 0))], inc=[('se', 1)])
                self.slot_free[i_w] = {'ve': 0, 'se': tth['se']}
        tk_t1 = {'se': self.now('se')}
        for dh in range(2):
            for kt in range(8):
                sl = kt % 4
                tmh = self.dma('sy', sb['b_m'][:, sl * 512: (sl + 1) * 512],
                               self.s_m[0, kt, dh],
                               waits=[('te', tk['bm_free'][sl]['te']), ('pev', self.now('pev'))],
                               sem=f'fm{sl}')
                tml = self.dma('sy', sb['b_m'][:, 2048 + sl * 512: 2048 + (sl + 1) * 512],
                               self.s_m[1, kt, dh],
                               waits=[('te', tk['bm_free'][4 + sl]['te']), ('pev', self.now('pev'))],
                               sem=f'fl{sl}')
                for idx, c in enumerate(ccs):
                    w0 = [(f'fm{sl}', tmh[f'fm{sl}']), (f'fl{sl}', tml[f'fl{sl}'])]
                    if kt == 0:
                        w0.append(('ve', tk['pY_free']['ve']))
                    self.mm(self.ps['pY'][:, idx * 512:(idx + 1) * 512],
                            sb['xnT'][:, kt * 512 + c * 128: kt * 512 + c * 128 + 128],
                            sb['b_m'][:, sl * 512:(sl + 1) * 512],
                            start=(kt == 0), stop=False, waits=w0)
                    self.mm(self.ps['pY'][:, idx * 512:(idx + 1) * 512],
                            sb['xnTl'][:, kt * 512 + c * 128: kt * 512 + c * 128 + 128],
                            sb['b_m'][:, sl * 512:(sl + 1) * 512], False, False,
                            inc=[('te', 1)] if idx == 1 else ())
                    if idx == 1:
                        tk['bm_free'][sl] = {'te': self.now('te')}
                    self.mm(self.ps['pY'][:, idx * 512:(idx + 1) * 512],
                            sb['xnT'][:, kt * 512 + c * 128: kt * 512 + c * 128 + 128],
                            sb['b_m'][:, 2048 + sl * 512: 2048 + (sl + 1) * 512], False, False,
                            inc=[('te', 1)] if idx == 1 else ())
                    if idx == 1:
                        tk['bm_free'][4 + sl] = {'te': self.now('te')}
            for idx, c in enumerate(ccs):
                self.mm(self.ps['pY'][:, idx * 512:(idx + 1) * 512],
                        sb['ones2'][:, :], sb['fbr'][:, dh * 512:(dh + 1) * 512],
                        False, True, inc=[('te', 1)])
            tkff = {'te': self.now('te')}
            tlast = None
            for idx, c in enumerate(ccs):
                tlast = self.ve(lambda nc, e, idx=idx, dh=dh: nc.vector.scalar_tensor_tensor(
                    sb['t1'][:, idx * 1024 + dh * 512: idx * 1024 + (dh + 1) * 512],
                    sb['t1'][:, idx * 1024 + dh * 512: idx * 1024 + (dh + 1) * 512],
                    0.1, self.ps['pY'][:, idx * 512:(idx + 1) * 512], ALU.mult, ALU.add),
                    waits=[('te', tkff['te']), ('se', tk_t1['se'])], inc=[('ve', 1)] if idx == 1 else ())
            tk['pY_free'] = {'ve': self.now('ve')}
        for idx, c in enumerate(ccs):
            gc = g * NCH + c
            par = gc % 2
            self.ve(lambda nc, e, idx=idx, par=par: nc.vector.tensor_tensor(
                sb['t1'][:, idx * 1024:(idx + 1) * 1024],
                sb['t1'][:, idx * 1024:(idx + 1) * 1024],
                sb['xm'][:, par * 1024:(par + 1) * 1024], ALU.add), inc=[('ve', 1)])
            tk['xm_free'][par] = {'ve': self.now('ve')}
            tld = self.dma('sy', sb['xq'][:, par * 1024:(par + 1) * 1024],
                           d['x'][gc * 128:(gc + 1) * 128, :],
                           waits=[('ve', tk['xq_free'][par]['ve'])], sem=f'xq{par}')
            tfin = self.ve(lambda nc, e, idx=idx, par=par: nc.vector.tensor_tensor(
                sb['t1'][:, idx * 1024:(idx + 1) * 1024],
                sb['t1'][:, idx * 1024:(idx + 1) * 1024],
                sb['xq'][:, par * 1024:(par + 1) * 1024], ALU.add),
                waits=[(f'xq{par}', tld[f'xq{par}'])], inc=[('ve', 1)])
            tk['xq_free'][par] = tfin
            tst = self.dma('sy', self.out[gc * 128:(gc + 1) * 128, :],
                           sb['t1'][:, idx * 1024:(idx + 1) * 1024],
                           waits=[('ve', tfin['ve'])], sem=f'do{idx}')
            tk['outb_free'][idx] = {'do': tst[f'do{idx}']}

    def _emit_block(self):
        nc = self.nc
        sems = self.sems

        def runner(name, eng):
            for fn, waits, incs in self.q[name]:
                for s, v in waits:
                    eng.wait_ge(sems[s], v)
                if fn is None:
                    continue
                inst = fn(nc, eng)
                for s, a in incs:
                    inst.then_inc(sems[s], a)

        with nc.Block() as blk:
            @blk.tensor
            def _(eng):
                runner('te', eng)

            @blk.vector
            def _(eng):
                runner('ve', eng)

            @blk.scalar
            def _(eng):
                runner('se', eng)

            @blk.gpsimd
            def _(eng):
                runner('gp', eng)

            @blk.sync
            def _(eng):
                runner('sy', eng)


def make_inputs(inp):
    c = host_consts()
    m = {}
    m['x'] = None
    for s in ('f', 'b'):
        m[f'wi_{s}'] = np.ascontiguousarray(inp[f'L0_attn_Wi_{s}'], np.float32)
        m[f'wo_{s}'] = np.ascontiguousarray(inp[f'L0_attn_Wo_{s}'], np.float32)
        m[f'convw_{s}'] = np.ascontiguousarray(inp[f'L0_attn_conv_{s}'], np.float32)
        m[f'convb_{s}'] = np.ascontiguousarray(inp[f'L0_attn_convb_{s}'], np.float32)
        m[f'alog_{s}'] = np.ascontiguousarray(inp[f'L0_attn_Alog_{s}'], np.float32)
        m[f'dtb_{s}'] = np.ascontiguousarray(inp[f'L0_attn_dtb_{s}'], np.float32)
        dd = np.zeros((128, 4096), np.float32)
        Dv = np.asarray(inp[f'L0_attn_D_{s}'], np.float32)
        for h in range(32):
            for ss in range(128):
                dd[ss, h * 128 + ss] = Dv[h]
        m[f'ddg_{s}'] = dd
    m['wm'] = np.ascontiguousarray(inp['L0_attn_Wm'], np.float32)
    m['bm'] = np.ascontiguousarray(inp['L0_attn_bm'], np.float32)
    m['wfr'] = np.ascontiguousarray(inp['L0_ffn_w_freq_real'], np.float32)
    m['wfi'] = np.ascontiguousarray(inp['L0_ffn_w_freq_imag'], np.float32)
    m['fbias'] = np.ascontiguousarray(inp['L0_ffn_bias'], np.float32)
    for k, v in c.items():
        m[k] = v
    return m


def kernel(**inp):
    x = np.asarray(inp['L0_attn_x'], np.float32)
    base = make_inputs(inp)
    b = Builder()
    nc = b.build()
    in_maps = []
    for core in range(8):
        mm_ = dict(base)
        mm_['x'] = np.ascontiguousarray(x[core])
        in_maps.append(mm_)
    res = run_bass_kernel_spmd(nc, in_maps, core_ids=list(range(8)))
    out = np.stack([r['out'] for r in res.results], axis=0)
    return out


# version 14
# speedup vs baseline: 1.7984x; 1.7984x over previous
import os
import numpy as np
import ml_dtypes
import concourse.bass as bass
import concourse.mybir as mybir
from concourse.bass_utils import run_bass_kernel_spmd

F32 = mybir.dt.float32
BF16 = mybir.dt.bfloat16
AF = mybir.ActivationFunctionType
ALU = mybir.AluOpType
BF = ml_dtypes.bfloat16

DM, DI, NH = 1024, 2048, 32
CONVD = DI + 2 * 128
PROJ = 2 * DI + 2 * 128 + NH
NCH, NG, L = 4, 8, 4096
NJ = 35


def _hilo(x):
    h = x.astype(BF).astype(np.float32)
    return h.astype(BF), (x - h).astype(BF)


def host_consts():
    j = np.arange(DM)
    ang = 2 * np.pi * np.outer(j, j) / DM
    Fr = np.cos(ang).astype(np.float32)
    Fi = (-np.sin(ang)).astype(np.float32)
    c = {}
    c['cFr_h'], c['cFr_l'] = _hilo(Fr)
    c['cGi_h'], c['cGi_l'] = _hilo(-Fi)
    c['cFi_h'], c['cFi_l'] = _hilo(Fi)
    c['cFrd_h'], c['cFrd_l'] = _hilo(Fr / DM)
    c['cFid_h'], c['cFid_l'] = _hilo(Fi / DM)
    c['cIdent'] = np.eye(128).astype(BF)
    tri = np.tril(np.ones((128, 128), np.float32))
    c['cTriF'] = tri.T.astype(BF)
    c['cTriB'] = tri.astype(BF)
    indF = np.zeros((16, 512), np.float32)
    indB = np.zeros((16, 512), np.float32)
    for i in range(4):
        blk = slice(i * 128, (i + 1) * 128)
        indF[i * 4 + 0, blk] = -1.0; indF[i * 4 + 1, blk] = -1.0
        indF[i * 4 + 2, blk] = 1.0; indF[i * 4 + 3, blk] = 1.0
        indB[i * 4 + 0, blk] = 1.0; indB[i * 4 + 1, blk] = 1.0
        indB[i * 4 + 2, blk] = 1.0; indB[i * 4 + 3, blk] = 1.0
    c['cIndF'] = indF.astype(BF)
    c['cIndB'] = indB.astype(BF)
    c['cOnes2'] = np.ones((2, 128), np.float32).astype(BF)
    c['cOnes2N'] = (-np.ones((2, 128), np.float32)).astype(BF)
    ev = np.ones((4, 128), np.float32); ev[0] = -1.0; ev[1] = -1.0
    c['cEvB'] = ev.astype(BF)
    c['cOnesF'] = np.ones((32, 128), np.float32)
    return c


class Builder:
    def __init__(self, debug=False):
        import contextlib
        self.fpo = os.environ.get('BASS_SIM_FP64') == '1'
        self.nc = bass.Bass("TRN2", target_bir_lowering=False, debug=False)
        self.ctx = contextlib.ExitStack()
        self.debug = debug
        self.q = {e: [] for e in ("te", "ve", "se", "gp", "sy")}
        self.cnt = {}
        self._declare()

    def emit(self, eng, fn, waits=(), inc=(), is_dma=False):
        if self.fpo and fn is not None and not is_dma and eng in ('te', 've', 'se', 'gp'):
            waits = list(waits) + [(eng, self.cnt[eng])]
            if not any(s == eng for s, _ in inc):
                inc = list(inc) + [(eng, 1)]
        incs = []
        for s, a in inc:
            self.cnt[s] += a
            incs.append((s, a))
        wts = tuple((s, v) for (s, v) in waits if v > 0)
        self.q[eng].append((fn, wts, tuple(incs)))
        return {s: self.cnt[s] for s, _ in incs}

    def now(self, s):
        return self.cnt[s]

    def dma(self, qeng, dst, src, waits=(), sem=None, slow=False):
        sem = sem or ('dsy' if qeng == 'sy' else 'dgp')
        return self.emit(qeng, lambda nc, e, dst=dst, src=src, slow=slow:
                         e.dma_start(out=dst, in_=src, allow_slow_non_contiguous=slow),
                         waits=waits, inc=[(sem, 16)], is_dma=True)

    def mm(self, psd, lhsT, rhs, start, stop, waits=(), inc=(), tp=None):
        def f(nc, e, a=psd, b=lhsT, c=rhs, st=start, sp=stop, tp=tp):
            return nc.tensor.matmul(a, b, c, start=st, stop=sp, tile_position=tp)
        return self.emit('te', f, waits=waits, inc=inc)

    def trans(self, psd, src, waits=(), inc=()):
        ident = self.sb['ident']
        k = src.shape[0]
        return self.emit('te', lambda nc, e, a=psd, b=src, i=ident, k=k:
                         nc.tensor.transpose(a, b, i[0:k, 0:k]), waits=waits, inc=inc)

    def ve(self, fn, waits=(), inc=()):
        return self.emit('ve', fn, waits=waits, inc=inc)

    def se(self, fn, waits=(), inc=()):
        return self.emit('se', fn, waits=waits, inc=inc)

    def gp(self, fn, waits=(), inc=()):
        return self.emit('gp', fn, waits=waits, inc=inc)

    def _declare(self):
        nc = self.nc
        self.d = {}
        def din(name, shape, dt=F32):
            self.d[name] = nc.dram_tensor(name, shape, dt, kind="ExternalInput").ap()
        din('x', [L, DM])
        for s in ('f', 'b'):
            din(f'wi_{s}', [DM, PROJ]); din(f'wo_{s}', [DI, DM])
            din(f'convw_{s}', [4, CONVD]); din(f'convb_{s}', [CONVD])
            din(f'alog_{s}', [NH]); din(f'dtb_{s}', [NH])
            din(f'ddg_{s}', [128, 4096])
        din('wm', [DM, DM]); din('bm', [DM])
        din('wfr', [DM, DM]); din('wfi', [DM, DM]); din('fbias', [DM])
        for k, v in host_consts().items():
            din(k, list(v.shape), BF16 if v.dtype == BF else F32)
        self.out = nc.dram_tensor('out', [L, DM], F32, kind="ExternalOutput").ap()
        dbg = "ExternalOutput" if self.debug else "Internal"
        self.s_wi = {s: nc.dram_tensor(f's_wi_{s}', [NJ, 128, 8, 128], BF16, kind="Internal").ap()
                     for s in ('f', 'b')}
        self.s_yf = nc.dram_tensor('s_yf', [L, DM], BF16, kind=dbg).ap()
        self.s_w = nc.dram_tensor('s_w', [4, 8, 8, 128, 128], BF16, kind=dbg).ap()
        self.s_a = nc.dram_tensor('s_a', [4, 8, 8, 128, 128], BF16, kind=dbg).ap()
        self.s_m = nc.dram_tensor('s_m', [2, 8, 2, 128, 512], BF16, kind=dbg).ap()
        self.s_cr = nc.dram_tensor('s_cr', [5, 32, 512], BF16, kind="Internal").ap()
        if self.debug:
            self.dbgt = {}
            for nm, shape, dt in [('dxnT', [128, 4096], BF16), ('dzsil', [128, 8192], BF16),
                                  ('dxbcs', [128, 9216], BF16), ('ddt', [32, 512], F32),
                                  ('dcum', [32, 512], F32), ('dlexp', [128, 2048], BF16),
                                  ('dlv', [128, 2048], BF16), ('dmt', [128, 2048], BF16),
                                  ('dgm', [128, 128], BF16), ('dyg', [128, 2048], BF16),
                                  ('demega', [128, 512], BF16), ('dcrc', [4, 4096], BF16),
                                  ('dwv', [32, 512], BF16), ('dxt', [128, 2048], BF16),
                                  ('dhst', [128, 2048], BF16), ('decl', [128, 512], F32),
                                  ('dhprev', [128, 2048], BF16)]:
                self.dbgt[nm] = nc.dram_tensor(nm, shape, dt, kind='ExternalOutput').ap()

        self.sb = {}
        def sbuf(name, shape, dt):
            self.sb[name] = self.ctx.enter_context(nc.sbuf_tensor(name, shape, dt))
        for name, shape, dt in [
            ('b_wi', [128, 4096], BF16), ('b_wo', [128, 16384], BF16),
            ('b_m', [128, 4096], BF16),
            ('zsil', [128, 8192], BF16), ('xbcs', [128, 9216], BF16),
            ('tmpc', [128, 1040], BF16), ('cstash', [128, 54], BF16),
            ('cw', [128, 72], F32), ('cb', [128, 18], F32),
            ('yg', [128, 4096], BF16), ('xm', [128, 2048], BF16), ('yfc', [128, 2048], BF16),
            ('xnT', [128, 4096], BF16), ('xnTl', [128, 4096], BF16),
            ('xq', [128, 2048], F32),
            ('xnch', [128, 1024], BF16), ('xncl', [128, 1024], BF16),
            ('lv', [128, 2048], BF16),
            ('mt', [128, 2048], BF16), ('mt2', [128, 2048], BF16), ('lv2', [128, 2048], BF16),
            ('gmask', [128, 128], BF16), ('xt', [128, 2048], BF16),
            ('hst', [128, 2048], BF16),
            ('Bt', [128, 128], BF16), ('wct', [128, 32], F32), ('pqbc', [128, 32], F32),
            ('ddiag', [128, 4096], BF16),
            ('dt_', [32, 512], F32), ('lgda', [32, 512], F32), ('cum', [32, 512], F32),
            ('lgdt', [32, 512], F32),
            ('cumhi', [32, 512], BF16), ('cumlo', [32, 512], BF16),
            ('ldthi', [32, 512], BF16), ('ldtlo', [32, 512], BF16),
            ('wv', [32, 512], BF16), ('pqb', [32, 4], BF16),
            ('clh', [32, 512], BF16), ('cll', [32, 512], BF16),
            ('clsc', [32, 2], F32),
            ('emc', [16, 1024], BF16), ('crc', [4, 4096], BF16), ('pqrow', [1, 128], BF16),
            ('pk32', [128, 512], BF16),
            ('onesf', [32, 128], F32),
            ('t1', [128, 2048], F32),

            ('mean', [128, 1], F32), ('sqs', [128, 1], F32), ('rstd', [128, 1], F32),
            ('nmrs', [128, 1], F32), ('epsc', [128, 1], F32),
            ('negA', [32, 1], F32), ('aexp', [32, 1], F32), ('dtbc', [32, 1], F32),
            ('bmr', [2, 1024], BF16), ('fbr', [2, 1024], BF16),
            ('ident', [128, 128], BF16), ('triF', [128, 128], BF16), ('triB', [128, 128], BF16),
            ('indF', [16, 512], BF16), ('indB', [16, 512], BF16),
            ('ones2', [2, 128], BF16), ('ones2n', [2, 128], BF16), ('evB', [4, 128], BF16),
            ('wprep', [128, 1024], F32),
        ]:
            sbuf(name, shape, dt)
        self.sb['pstr'] = self.sb['zsil']
        self.sb['lstr'] = self.sb['xbcs']
        self.sb['wpo'] = self.sb['yg']
        self.sb['convt'] = self.sb['wprep']
        self.sb['cump'] = self.sb['lgda']
        self.sb['wvin'] = self.sb['dt_']
        self.sb['brow'] = None
        self.sb['xmb'] = self.sb['xnch']
        self.sb['xmT'] = self.sb['xncl']

        self.ps = {}
        def psumt(name, shape, dt):
            self.ps[name] = self.ctx.enter_context(nc.psum_tensor(name, shape, dt))
        psumt('pS0', [128, 512], F32); psumt('pS1', [128, 512], F32)
        psumt('pQ0', [128, 512], F32); psumt('pQ1', [128, 512], F32); psumt('pQ2', [128, 512], F32)
        psumt('pT', [128, 1024], BF16)
        psumt('pY', [128, 1024], F32)

        self.semnames = ['dsy', 'dgp', 'te', 've', 'se', 'gp',
                         'do0', 'do1',
                         'sst0', 'sst1', 'sld0', 'sld1', 'wld', 'wst', 'bld',
                         'ph0', 'ph1', 'ph2', 'ph3', 'pev', 'dp', 'dd', 'wo',
                         'xq0', 'xq1', 'wi0', 'wi1', 'wi2', 'wi3',
                         'pk', 'em', 'sc', 'co', 'cr', 'pc',
                         'yf0', 'yf1', 'yl0', 'yl1',
                         'wm0', 'wm1', 'wm2', 'wm3', 'fm0', 'fm1', 'fm2', 'fm3',
                         'fl0', 'fl1', 'fl2', 'fl3']
        self.sems = {n: self.ctx.enter_context(nc.semaphore(n)) for n in self.semnames}
        for n in self.semnames:
            self.cnt[n] = 0

        self.slots = [self.ps['pS0'][:, :], self.ps['pS1'][:, :], self.ps['pQ0'][:, :],
                      self.ps['pQ1'][:, :], self.ps['pQ2'][:, :]]
        self.slot_free = [{'ve': 0, 'se': 0} for _ in range(5)]
        self.pool_i = 0
        self.tk = dict(
            pT_free={'ve': 0}, pY_free={'ve': 0},
            yg_free=[{'te': 0}, {'te': 0}],
            yfc_free=[{'st': 0, 've': 0}, {'st': 0, 've': 0}],
            xm_free=[{'ve': 0}, {'ve': 0}], xm_full=[{'ve': 0}, {'ve': 0}],
            xq_free=[{'ve': 0}, {'ve': 0}], xq2_free=[{'ve': 0}, {'ve': 0}],
            outb_free=[{'do': 0}, {'do': 0}],
            wi_free=[{'te': 0}, {'te': 0}, {'te': 0}, {'te': 0}],
            bm_free=[{'te': 0} for _ in range(8)],
            group_done={'te': 0, 've': 0, 'se': 0, 'gp': 0},
            ymm_free={'te': 0}, hbf_users={'te': 0},
            t1_free=[{'ve': 0}, {'ve': 0}],
        )

    def next_slot(self):
        i = self.pool_i % 5
        self.pool_i += 1
        return i

    def dbg_dump(self, pairs):
        wd = [(n, self.now(n)) for n in self.semnames]
        t = None
        for nm, src in pairs:
            t = self.dma('sy', self.dbgt[nm], src, waits=wd)
            wd = ()
        for q in ('te', 've', 'se', 'gp'):
            self.emit(q, None, waits=[('dsy', t['dsy'])])

    def build(self):
        self._prep()
        self._phase0()
        self._dir(False)
        self._dir(True)
        self.emit('sy', None, waits=[('do0', self.now('do0')), ('do1', self.now('do1'))])
        self._emit_block()
        return self.nc

    def _prep(self):
        d, sb = self.d, self.sb
        for nm, tgt in [('cIdent', 'ident'), ('cTriF', 'triF'), ('cTriB', 'triB'),
                        ('cIndF', 'indF'), ('cIndB', 'indB'), ('cOnes2', 'ones2'),
                        ('cOnes2N', 'ones2n'), ('cEvB', 'evB'), ('cOnesF', 'onesf')]:
            self.dma('sy', sb[tgt][:, :], d[nm])


        for q in ('te', 've', 'se', 'gp'):
            self.emit(q, None, waits=[('dsy', self.now('dsy'))])
        self.ve(lambda nc, e: nc.vector.memset(sb['epsc'][:, :], 1e-5), inc=[('ve', 1)])
        prev = {}
        for s in ('f', 'b'):
            wi_r = d[f'wi_{s}'].rearrange('(kt p) c -> p kt c', p=128)
            for j in range(NJ):
                jw = PROJ - (NJ - 1) * 128 if j == NJ - 1 else 128
                par = j % 2
                key = (s, j - 2) if j >= 2 else (('f', NJ - 1 - par) if s == 'b' else None)
                wts = [(f'sst{par}', prev[key][f'sst{par}'])] if key in prev else ()
                tld = self.dma('gp', sb['wpo'][:, par * 1024: par * 1024 + 8 * jw]
                               .rearrange('p (kt jj) -> p kt jj', jj=jw),
                               wi_r[:, :, j * 128: j * 128 + jw], waits=wts, sem=f'sld{par}')
                prev[(s, j)] = self.dma(
                    'sy', self.s_wi[s][j, :, :, 0:jw],
                    sb['wpo'][:, par * 1024: par * 1024 + 8 * jw]
                    .rearrange('p (kt jj) -> p kt jj', jj=jw),
                    waits=[(f'sld{par}', tld[f'sld{par}'])], sem=f'sst{par}')
        self._sst = [('sst0', self.now('sst0')), ('sst1', self.now('sst1'))]
        tlast_ve = {'ve': self.now('ve')}
        tlast_st = {'wst': 0}
        first_hl = [('sst0', self.now('sst0')), ('sst1', self.now('sst1'))]
        for wnm, base in [('wfr', 0), ('wfi', 2)]:
            for kt in range(8):
                tld = self.dma('gp', sb['wprep'][:, :], d[wnm][kt * 128:(kt + 1) * 128, :],
                               waits=[('ve', tlast_ve['ve'])], sem='wld')
                self.ve(lambda nc, e: nc.vector.tensor_copy(sb['wpo'][:, 0:1024], sb['wprep'][:, :]),
                        waits=[('wld', tld['wld']), ('wst', tlast_st['wst'])] + first_hl)
                first_hl = []
                tlast_ve = self.ve(lambda nc, e: nc.vector.tensor_sub(
                    sb['wpo'][:, 1024:2048], sb['wprep'][:, :], sb['wpo'][:, 0:1024]),
                    inc=[('ve', 1)])
                self.dma('sy', self.s_w[base, kt].transpose([1, 0, 2]),
                         sb['wpo'][:, 0:1024].rearrange('p (ot c) -> p ot c', c=128),
                         waits=[('ve', tlast_ve['ve'])], sem='wst')
                tlast_st = self.dma('sy', self.s_w[base + 1, kt].transpose([1, 0, 2]),
                                    sb['wpo'][:, 1024:2048].rearrange('p (ot c) -> p ot c', c=128),
                                    waits=[('ve', tlast_ve['ve'])], sem='wst')
        for src, tgt in [('bm', 'bmr'), ('fbias', 'fbr')]:
            tld = self.dma('gp', sb['wprep'][0:1, :], d[src].rearrange('(a n) -> a n', a=1),
                           waits=[('ve', self.now('ve')), ('wld', self.now('wld')),
                                  ('bld', self.now('bld'))], sem='bld')
            self.ve(lambda nc, e, tgt=tgt: nc.vector.tensor_copy(
                sb[tgt][0:1, :], sb['wprep'][0:1, :]), waits=[('bld', tld['bld'])])
            tlo = self.ve(lambda nc, e, tgt=tgt: nc.vector.tensor_sub(
                sb['t1'][0:1, 0:1024], sb['wprep'][0:1, :], sb[tgt][0:1, :]), inc=[('ve', 1)])
            self.dma('gp', sb[tgt][1:2, :], sb['t1'][0:1, 0:1024],
                     waits=[('ve', tlo['ve'])], sem='bld')

    def _ph0_evac(self, pd, dst_h, dst_l, first=(), tiled=False):
        sb = self.sb
        self.ve(lambda nc, e, pd=pd: nc.vector.tensor_copy(sb['wpo'][:, 0:512], pd),
                waits=[('te', self.now('te')), ('pev', self.now('pev'))] + list(first))
        tev = self.ve(lambda nc, e, pd=pd: nc.vector.tensor_sub(
            sb['wpo'][:, 512:1024], pd, sb['wpo'][:, 0:512]), inc=[('ve', 1)])
        src_h, src_l = sb['wpo'][:, 0:512], sb['wpo'][:, 512:1024]
        if tiled:
            src_h = src_h.rearrange('p (mt c) -> p mt c', c=128)
            src_l = src_l.rearrange('p (mt c) -> p mt c', c=128)
            dst_h = dst_h.transpose([1, 0, 2])
            dst_l = dst_l.transpose([1, 0, 2])
        self.dma('sy', dst_h, src_h, waits=[('ve', tev['ve'])], sem='pev')
        self.dma('sy', dst_l, src_l, waits=[('ve', tev['ve'])], sem='pev')
        return tev

    def _phase0(self):
        self.emit('sy', None, waits=[('dsy', self.now('dsy')), ('sst0', self.now('sst0')),
                                     ('sst1', self.now('sst1')), ('wst', self.now('wst'))])
        d, sb = self.d, self.sb
        CONST1 = ['cFr_h', 'cFr_l', 'cGi_h', 'cGi_l', 'cFi_h', 'cFi_l']
        terms_r = [(0, 0), (1, 0), (0, 1), (2, 2), (3, 2), (2, 3)]
        terms_i = [(2, 0), (3, 0), (2, 1), (0, 4), (1, 4), (0, 5)]
        lhs_tok = [{'te': 0}, {'te': 0}, {'te': 0}, {'te': 0}]
        wst_tot = self.now('wst')
        first_ev = [('sst0', self.now('sst0')), ('sst1', self.now('sst1')),
                    ('wst', self.now('wst'))]

        def rhs_cache(q):
            col = q * 512
            if col < 16384:
                return sb['b_wo'][:, col:col + 512]
            return sb['zsil'][:, col - 16384:col - 16384 + 512]

        def load_cache(tiles):
            gen = [('te', self.now('te'))]
            for q, (cn, r0, c0) in enumerate(tiles):
                self.dma('sy', rhs_cache(q), d[cn][r0:r0 + 128, c0:c0 + 512],
                         waits=gen if (q == 0 or self.fpo) else (), sem='pc')
            return self.now('pc')

        for mh in range(2):
            pc_tok = load_cache([(cn, kt * 128, mh * 512)
                                 for kt in range(8) for cn in CONST1])
            for ot in range(8):
                i_r = self.next_slot()
                i_i = self.next_slot()
                pdr, pdi = self.slots[i_r], self.slots[i_i]
                for kt in range(8):
                    kp = kt % 4
                    t_lhs = None
                    for wi_i in range(4):
                        t_lhs = self.dma('sy', sb['lstr'][:, (kp * 4 + wi_i) * 128:(kp * 4 + wi_i + 1) * 128],
                                         self.s_w[wi_i, kt, ot],
                                         waits=([('te', lhs_tok[kp]['te'])] +
                                                ([('wst', wst_tot)] if self.fpo else []))
                                         if (wi_i == 0 or self.fpo) else (),
                                         sem=f'ph{kp}')
                    for ri, (pd, idx, terms) in enumerate(((pdr, i_r, terms_r), (pdi, i_i, terms_i))):
                        for ti, (wl, cc) in enumerate(terms):
                            w0 = []
                            if ri == 0 and ti == 0:
                                w0 = [(f'ph{kp}', t_lhs[f'ph{kp}']), ('pc', pc_tok)]
                                if kt == 0:
                                    w0 += [('ve', self.slot_free[i_r]['ve']), ('se', self.slot_free[i_r]['se']),
                                           ('ve', self.slot_free[i_i]['ve']), ('se', self.slot_free[i_i]['se'])]
                            last = (ri == 1 and ti == 5)
                            self.mm(pd, sb['lstr'][:, (kp * 4 + wl) * 128:(kp * 4 + wl + 1) * 128],
                                    rhs_cache(kt * 6 + cc),
                                    start=(kt == 0 and ti == 0), stop=(kt == 7 and ti == 5),
                                    waits=w0, inc=[('te', 1)] if last else ())
                    lhs_tok[kp] = {'te': self.now('te')}
                for ri, (pd, idx) in enumerate(((pdr, i_r), (pdi, i_i))):
                    tev = self._ph0_evac(
                        pd,
                        self.s_a[ri * 2, ot, mh * 4:(mh + 1) * 4],
                        self.s_a[ri * 2 + 1, ot, mh * 4:(mh + 1) * 4],
                        first=first_ev, tiled=True)
                    first_ev = []
                    self.slot_free[idx] = {'ve': tev['ve'], 'se': 0}
        self.emit('sy', None, waits=[('pev', self.now('pev')), ('ph0', self.now('ph0')),
                                     ('ph1', self.now('ph1')), ('ph2', self.now('ph2')),
                                     ('ph3', self.now('ph3'))])
        terms2 = [(0, 0), (1, 0), (0, 1), (2, 2), (3, 2), (2, 3)]
        pev_s1 = self.now('pev')
        for oh in range(2):
            pc_tok = load_cache([(cn, ot * 128, oh * 512)
                                 for ot in range(8)
                                 for cn in ('cFrd_h', 'cFrd_l', 'cFid_h', 'cFid_l')])
            for mtl in range(8):
                idx = self.next_slot()
                pd = self.slots[idx]
                for ot in range(8):
                    kp = ot % 4
                    t_lhs = None
                    for ai in range(4):
                        t_lhs = self.dma('sy', sb['lstr'][:, (kp * 4 + ai) * 128:(kp * 4 + ai + 1) * 128],
                                         self.s_a[ai, ot, mtl],
                                         waits=([('te', lhs_tok[kp]['te'])] +
                                                ([('pev', pev_s1)] if self.fpo else []))
                                         if (ai == 0 or self.fpo) else (),
                                         sem=f'ph{kp}')
                    for ti, (al, cc) in enumerate(terms2):
                        w0 = []
                        if ti == 0:
                            w0 = [(f'ph{kp}', t_lhs[f'ph{kp}']), ('pc', pc_tok)]
                            if ot == 0:
                                w0 += [('ve', self.slot_free[idx]['ve']), ('se', self.slot_free[idx]['se'])]
                        self.mm(pd, sb['lstr'][:, (kp * 4 + al) * 128:(kp * 4 + al + 1) * 128],
                                rhs_cache(ot * 4 + cc),
                                start=(ot == 0 and ti == 0), stop=(ot == 7 and ti == 5),
                                waits=w0, inc=[('te', 1)] if ti == 5 else ())
                    lhs_tok[kp] = {'te': self.now('te')}
                tev = self._ph0_evac(
                    pd,
                    self.s_m[0, mtl, oh],
                    self.s_m[1, mtl, oh])
                self.slot_free[idx] = {'ve': tev['ve'], 'se': 0}

    def _dir(self, is_b):
        d, sb, tk = self.d, self.sb, self.tk
        self.emit('sy', None, waits=[('dsy', self.now('dsy')), ('pev', self.now('pev')),
                                     ('yf0', self.now('yf0')), ('yf1', self.now('yf1'))])
        dirw = 'b' if is_b else 'f'
        gd = tk['group_done']
        wo_gate = self.now('te')
        t_wo = None
        for kt in range(16):
            t_wo = self.dma('gp', sb['b_wo'][:, kt * 1024:(kt + 1) * 1024],
                            d[f'wo_{dirw}'][kt * 128:(kt + 1) * 128, :],
                            waits=[('te', wo_gate)] if (kt == 0 or self.fpo) else (), sem='wo')

        gdw = [('te', gd['te']), ('ve', gd['ve']), ('se', gd['se'])]
        self.dma('gp', sb['cw'][:, :].rearrange('p (tau ct) -> p tau ct', ct=18),
                 d[f'convw_{dirw}'].rearrange('tau (ct p) -> p tau ct', p=128), slow=True,
                 waits=gdw, sem='dp')
        gdw2 = gdw if self.fpo else []
        self.dma('gp', sb['cb'][:, :], d[f'convb_{dirw}'].rearrange('(ct p) -> p ct', p=128),
                 slow=True, waits=gdw2, sem='dp')
        t_al = self.dma('gp', sb['aexp'][:, :], d[f'alog_{dirw}'].rearrange('(a b) -> a b', b=1),
                        slow=True, waits=gdw2, sem='dp')
        self.dma('gp', sb['dtbc'][:, :], d[f'dtb_{dirw}'].rearrange('(a b) -> a b', b=1),
                 slow=True, waits=gdw2, sem='dp')
        t_dd = self.dma('gp', sb['ddiag'][:, :], d[f'ddg_{dirw}'],
                        waits=[('te', gd['te']), ('ve', gd['ve'])], sem='dd')
        tse = self.se(lambda nc, e: nc.scalar.activation(sb['negA'][:, :], sb['aexp'][:, :], AF.Exp),
                      waits=[('dp', self.now('dp'))], inc=[('se', 1)])
        t_na = self.ve(lambda nc, e: nc.vector.tensor_scalar_mul(sb['negA'][:, :], sb['negA'][:, :], -1.0),
                       waits=[('se', tse['se'])], inc=[('ve', 1)])
        t_h0 = self.ve(lambda nc, e: nc.vector.memset(sb['hst'][:, :], 0.0),
                       waits=[('te', tk['group_done']['te'])], inc=[('ve', 1)])
        self.ve(lambda nc, e: nc.vector.memset(sb['cstash'][:, :], 0.0), inc=[('ve', 1)])
        t_dirprep = {'dp': self.now('dp'), 've': self.now('ve')}

        tk_hup = t_h0
        tk_zsil_free = {'te': tk['group_done']['te'], 've': 0}
        tk_xbcs_free = {'te': tk['group_done']['te'], 'gp': tk['group_done']['gp']}
        tk_dt_free = {'ve': 0, 'co': 0}
        tk_xnT_free = {'te': tk['group_done']['te']}
        tk_cr_free = {'te': 0}
        tk_wv_users = {'te': 0}
        tk_mt_gp = {'gp': 0}
        tk_crc_users = {'te': 0}
        t_ddg = t_dd

        groups = list(range(NG - 1, -1, -1)) if is_b else list(range(NG))
        for g in groups:
            chunks = list(range(NCH - 1, -1, -1)) if is_b else list(range(NCH))
            tk_ln_sq = {'ve': getattr(self, '_conv_last', 0)}
            tk_ln_tr = {'te': 0}
            for c in chunks:
                gc = g * NCH + c
                par = gc % 2
                xqs = sb['xq'][:, par * 1024:(par + 1) * 1024]
                tld = self.dma('sy', xqs, d['x'][gc * 128:(gc + 1) * 128, :],
                               waits=[('ve', tk['xq_free'][par]['ve']),
                                      ('dp', t_dirprep['dp'])], sem=f'xq{par}')
                tsqr = self.se(lambda nc, e, xqs=xqs: nc.scalar.activation(
                    sb['wprep'][:, :], xqs, AF.Square),
                    waits=[(f'xq{par}', tld[f'xq{par}']), ('ve', tk_ln_sq['ve'])],
                    inc=[('se', 1)])
                def vc(fn, waits=()):
                    t = self.ve(fn, waits=list(waits) + [('ve', self.now('ve'))], inc=[('ve', 1)])
                    return t
                vc(lambda nc, e, xqs=xqs: nc.vector.reduce_sum(
                    sb['mean'][:, :], xqs, axis=mybir.AxisListType.X),
                    waits=[(f'xq{par}', tld[f'xq{par}'])])
                vc(lambda nc, e: nc.vector.tensor_scalar_mul(sb['mean'][:, :], sb['mean'][:, :], 1.0 / DM))
                tk_ln_sq = vc(lambda nc, e: nc.vector.reduce_sum(
                    sb['sqs'][:, :], sb['wprep'][:, :], axis=mybir.AxisListType.X),
                    waits=[('se', tsqr['se'])])
                vc(lambda nc, e: nc.vector.tensor_scalar_mul(sb['sqs'][:, :], sb['sqs'][:, :], 1.0 / DM))
                vc(lambda nc, e: nc.vector.tensor_mul(sb['rstd'][:, :], sb['mean'][:, :], sb['mean'][:, :]))
                tvar = vc(lambda nc, e: nc.vector.tensor_sub(sb['sqs'][:, :], sb['sqs'][:, :], sb['rstd'][:, :]))
                tsq = self.se(lambda nc, e: nc.scalar.activation(sb['rstd'][:, :], sb['sqs'][:, :], AF.Sqrt,
                                                                 bias=sb['epsc'][:, :]),
                              waits=[('ve', tvar['ve'])], inc=[('se', 1)])
                vc(lambda nc, e: nc.vector.reciprocal(sb['rstd'][:, :], sb['rstd'][:, :]),
                   waits=[('se', tsq['se'])])
                vc(lambda nc, e: nc.vector.tensor_mul(sb['nmrs'][:, :], sb['mean'][:, :], sb['rstd'][:, :]))
                vc(lambda nc, e: nc.vector.tensor_scalar_mul(sb['nmrs'][:, :], sb['nmrs'][:, :], -1.0))
                tnx = vc(lambda nc, e, xqs=xqs: nc.vector.tensor_scalar(
                    sb['xnch'][:, :], xqs, sb['rstd'][:, :], sb['nmrs'][:, :], ALU.mult, ALU.add),
                    waits=[('te', tk_ln_tr['te'])])
                if is_b:
                    vc(lambda nc, e, xqs=xqs: nc.vector.scalar_tensor_tensor(
                        sb['wprep'][:, :], xqs, sb['rstd'][:, :], sb['xnch'][:, :],
                        ALU.mult, ALU.subtract))
                    tnx = vc(lambda nc, e: nc.vector.tensor_scalar(
                        sb['xncl'][:, :], sb['wprep'][:, :], sb['nmrs'][:, :], None, ALU.add))
                self.tk['xq_free'][par] = tnx
                tk_ln_sq = {'ve': self.now('ve')}
                passes = [('xnch', 'xnT')] + ([('xncl', 'xnTl')] if is_b else [])
                for srcb, dstb in passes:
                    for kt in range(8):
                        self.trans(self.ps['pT'][:, kt * 128:(kt + 1) * 128],
                                   sb[srcb][:, kt * 128:(kt + 1) * 128],
                                   waits=([('ve', tnx['ve']), ('ve', tk['pT_free']['ve']),
                                           ('te', tk_xnT_free['te'])] if kt == 0 else ()),
                                   inc=[('te', 1)] if kt == 7 else ())
                    tkte = self.now('te')
                    tk_ln_tr = {'te': tkte}
                    dst = sb[dstb][:, :].rearrange('p (kt t) -> p kt t', t=512)[:, :, c * 128:(c + 1) * 128]
                    tk['pT_free'] = self.ve(lambda nc, e, dst=dst: nc.vector.tensor_copy(
                        dst, self.ps['pT'][:, :].rearrange('p (kt t) -> p kt t', t=128)),
                        waits=[('te', tkte)], inc=[('ve', 1)])
            tk_ln = {'ve': self.now('ve')}

            jorder = [34] + list(range(16, 34)) + list(range(16))
            t_wld = {}
            def load_wi(ii):
                j = jorder[ii]
                jw = PROJ - (NJ - 1) * 128 if j == NJ - 1 else 128
                bpar = ii % 4
                t_wld[j] = self.dma('sy', sb['b_wi'][:, bpar * 1024: bpar * 1024 + 8 * jw]
                                    .rearrange('p (kt jj) -> p kt jj', jj=jw),
                                    self.s_wi[dirw][j, :, :, 0:jw],
                                    waits=[('te', tk['wi_free'][bpar]['te'])] +
                                    (self._sst if self.fpo else []),
                                    sem=f'wi{bpar}')
            load_wi(0); load_wi(1); load_wi(2); load_wi(3)
            par_evac = [dict(self.slot_free[0]), dict(self.slot_free[1])]
            tk_conv_done = {}
            for ii, j in enumerate(jorder):
                jw = PROJ - (NJ - 1) * 128 if j == NJ - 1 else 128
                ppar = ii % 2
                bsl = ii % 4
                pd = self.slots[ppar] if jw == 128 else self.ps['pS%d' % ppar][0:32, :]
                for kt in range(8):
                    w0 = []
                    if kt == 0:
                        w0 = [(f'wi{bsl}', t_wld[j][f'wi{bsl}']), ('ve', tk_ln['ve']),
                              ('se', par_evac[ppar]['se']), ('ve', par_evac[ppar]['ve'])]
                    self.mm(pd, sb['b_wi'][:, bsl * 1024 + kt * jw: bsl * 1024 + (kt + 1) * jw],
                            sb['xnT'][:, kt * 512:(kt + 1) * 512],
                            start=(kt == 0), stop=(kt == 7),
                            waits=w0, inc=[('te', 1)] if kt == 7 else ())
                tk['wi_free'][bsl] = {'te': self.now('te')}
                if ii + 4 < NJ:
                    load_wi(ii + 4)
                tkte = self.now('te')
                if j < 16:
                    tz = self.se(lambda nc, e, j=j, ppar=ppar: nc.scalar.activation(
                        sb['zsil'][:, j * 512:(j + 1) * 512], self.slots[ppar], AF.Silu),
                        waits=[('te', tkte), ('te', tk_zsil_free['te']),
                               ('ve', tk_zsil_free['ve'])], inc=[('se', 1)])
                    par_evac[ppar] = {'se': tz['se'], 've': 0}
                elif j < 34:
                    ct = j - 16
                    cpar = ct % 2
                    if not is_b:
                        data_off = cpar * 520 + 3; stash_off = cpar * 520
                        save_off = cpar * 520 + 512; tap = [0, 1, 2, 3]
                    else:
                        data_off = cpar * 520; stash_off = cpar * 520 + 512
                        save_off = cpar * 520; tap = [3, 2, 1, 0]
                    tx = self.se(lambda nc, e, ppar=ppar, o=data_off: nc.scalar.activation(
                        sb['tmpc'][:, o:o + 512], self.slots[ppar], AF.Copy),
                        waits=[('te', tkte),
                               ('ve', tk_conv_done.get(ct - 2, {'ve': 0})['ve'])],
                        inc=[('se', 1)])
                    par_evac[ppar] = {'se': tx['se'], 've': 0}
                    self.ve(lambda nc, e, ct=ct, so=stash_off: nc.vector.tensor_copy(
                        sb['tmpc'][:, so:so + 3], sb['cstash'][:, ct * 3:ct * 3 + 3]),
                        waits=[('se', tx['se'])])
                    base = cpar * 520
                    self.ve(lambda nc, e, b=base, ct=ct, t0=tap[0], cpar=cpar: nc.vector.tensor_scalar(
                        sb['convt'][:, cpar * 512:(cpar + 1) * 512], sb['tmpc'][:, b + t0: b + t0 + 512],
                        sb['cw'][:, ct:ct + 1], None, ALU.mult))
                    for tau in (1, 2, 3):
                        self.ve(lambda nc, e, b=base, ct=ct, tau=tau, to=tap[tau], cpar=cpar:
                                nc.vector.scalar_tensor_tensor(
                                    sb['convt'][:, cpar * 512:(cpar + 1) * 512],
                                    sb['tmpc'][:, b + to: b + to + 512],
                                    sb['cw'][:, tau * 18 + ct: tau * 18 + ct + 1],
                                    sb['convt'][:, cpar * 512:(cpar + 1) * 512], ALU.mult, ALU.add))
                    tcv = self.ve(lambda nc, e, ct=ct, so=save_off: nc.vector.tensor_copy(
                        sb['cstash'][:, ct * 3:ct * 3 + 3], sb['tmpc'][:, so:so + 3]),
                        inc=[('ve', 1)])
                    tk_conv_done[ct] = tcv
                    self.se(lambda nc, e, ct=ct, cpar=cpar: nc.scalar.activation(
                        sb['xbcs'][:, ct * 512:(ct + 1) * 512], sb['convt'][:, cpar * 512:(cpar + 1) * 512],
                        AF.Silu, bias=sb['cb'][:, ct:ct + 1]),
                        waits=[('ve', tcv['ve']), ('te', tk_xbcs_free['te']),
                               ('gp', tk_xbcs_free['gp'])], inc=[('se', 1)])
                else:
                    self.se(lambda nc, e, ppar=ppar: nc.scalar.activation(
                        sb['lgda'][:, :], self.ps['pS%d' % ppar][0:32, :], AF.Abs,
                        bias=sb['dtbc'][:, :]),
                        waits=[('te', tkte), ('ve', tk_dt_free['ve']),
                               ('co', tk_dt_free['co']), ('dp', t_dirprep['dp'])],
                        inc=[('se', 1)])
                    self.se(lambda nc, e: nc.scalar.activation(
                        sb['cum'][:, :], sb['lgda'][:, :], AF.Exp, scale=-1.0))
                    self.se(lambda nc, e: nc.scalar.activation(
                        sb['cum'][:, :], sb['cum'][:, :], AF.Ln, bias=sb['onesf'][0:32, 0:1]))
                    tdt = self.se(lambda nc, e, ppar=ppar: nc.scalar.activation(
                        sb['dt_'][:, :], self.ps['pS%d' % ppar][0:32, :], AF.Relu,
                        bias=sb['dtbc'][:, :]), inc=[('se', 1)])
                    par_evac[ppar] = {'se': tdt['se'], 've': 0}
                    self._t_dtadd = self.ve(lambda nc, e: nc.vector.tensor_add(
                        sb['dt_'][:, :], sb['dt_'][:, :], sb['cum'][:, :]),
                        waits=[('se', tdt['se'])], inc=[('ve', 1)])
                if j == 33:
                    conv_se = self.now('se')
            self._conv_last = tk_conv_done[17]['ve']
            tk_sweep = {'se': self.now('se'), 've': self.now('ve'), 'te': self.now('te')}
            self.slot_free[0] = {'ve': 0, 'se': tk_sweep['se']}
            self.slot_free[1] = {'ve': 0, 'se': tk_sweep['se']}

            tlgda = self.ve(lambda nc, e: nc.vector.tensor_scalar(
                sb['lgda'][:, :], sb['dt_'][:, :], sb['negA'][:, :], None, ALU.mult),
                waits=[('ve', self._t_dtadd['ve']), ('ve', t_na['ve']),
                       ('se', getattr(self, '_dtp_se', 0))], inc=[('ve', 1)])
            tscan = tlgda
            for c in range(NCH):
                tscan = self.ve(lambda nc, e, c=c: nc.vector.tensor_tensor_scan(
                    sb['cum'][:, c * 128:(c + 1) * 128], sb['lgda'][:, c * 128:(c + 1) * 128],
                    sb['lgda'][:, c * 128:(c + 1) * 128], 0.0, ALU.add, ALU.bypass),
                    waits=[('ve', tscan['ve'])], inc=[('ve', 1)])
            if is_b:
                self.ve(lambda nc, e: nc.vector.tensor_sub(sb['cump'][:, :], sb['cum'][:, :], sb['lgda'][:, :]),
                        waits=[('ve', tscan['ve'])])
            tlg = self.se(lambda nc, e: nc.scalar.activation(sb['lgdt'][:, :], sb['dt_'][:, :], AF.Ln),
                          waits=[('ve', self._t_dtadd['ve'])], inc=[('se', 1)])
            CC = sb['cump'] if is_b else sb['cum']
            self.ve(lambda nc, e, CC=CC: nc.vector.tensor_copy(sb['cumhi'][:, :], CC[:, :]),
                    waits=[('se', tlg['se']), ('pk', self.now('pk')), ('sc', self.now('sc')),
                           ('ve', tscan['ve'])])
            self.ve(lambda nc, e, CC=CC: nc.vector.tensor_tensor(
                sb['cumlo'][:, :], CC[:, :], sb['cumhi'][:, :], ALU.subtract))
            self.ve(lambda nc, e: nc.vector.tensor_copy(sb['ldthi'][:, :], sb['lgdt'][:, :]))
            self.ve(lambda nc, e: nc.vector.tensor_tensor(
                sb['ldtlo'][:, :], sb['lgdt'][:, :], sb['ldthi'][:, :], ALU.subtract))
            if is_b:
                self.ve(lambda nc, e: nc.vector.tensor_add(sb['wvin'][:, :], sb['lgdt'][:, :], sb['cump'][:, :]),
                        waits=[('se', tlg['se'])], inc=[('ve', 1)])
            else:
                self.ve(lambda nc, e: nc.vector.tensor_sub(sb['wvin'][:, :], sb['lgdt'][:, :], sb['cum'][:, :]),
                        waits=[('se', tlg['se'])], inc=[('ve', 1)])
            for c in range(NCH):
                kw = dict(bias=sb['cum'][:, c * 128 + 127:c * 128 + 128]) if not is_b else {}
                self.se(lambda nc, e, c=c, kw=kw: nc.scalar.activation(
                    sb['wv'][:, c * 128:(c + 1) * 128], sb['wvin'][:, c * 128:(c + 1) * 128],
                    AF.Exp, **kw),
                    waits=[('ve', self.now('ve')), ('te', tk_wv_users['te'])], inc=[('se', 1)])
            tpq = self.se(lambda nc, e: nc.scalar.activation(
                sb['pqb'][:, :], sb['cum'][:, 127:512:128], AF.Exp),
                waits=[('ve', tscan['ve']), ('sc', self.now('sc'))], inc=[('se', 1)])
            if is_b:
                for c in range(NCH):
                    def vc2(fn, waits=()):
                        return self.ve(fn, waits=list(waits) + [('ve', self.now('ve'))],
                                       inc=[('ve', 1)])
                    vc2(lambda nc, e, c=c: nc.vector.tensor_copy(
                        sb['clsc'][:, 0:1], sb['cum'][:, c * 128 + 127:c * 128 + 128]),
                        waits=[('se', tpq['se'])])
                    vc2(lambda nc, e, c=c: nc.vector.tensor_scalar(
                        sb['clh'][:, c * 128:(c + 1) * 128], sb['onesf'][:, 0:128],
                        sb['clsc'][:, 0:1], None, ALU.mult))
                    vc2(lambda nc, e, c=c: nc.vector.tensor_tensor(
                        sb['clsc'][:, 1:2], sb['clsc'][:, 0:1],
                        sb['clh'][:, c * 128:c * 128 + 1], ALU.subtract))
                    vc2(lambda nc, e, c=c: nc.vector.tensor_scalar(
                        sb['cll'][:, c * 128:(c + 1) * 128], sb['onesf'][:, 0:128],
                        sb['clsc'][:, 1:2], None, ALU.mult))
            tk_dtp = {'ve': self.now('ve'), 'se': self.now('se')}
            self._dtp_se = tk_dtp['se']
            wcr = [('ve', tk_dtp['ve']), ('se', tk_dtp['se']), ('te', tk_cr_free['te'])]
            if self.fpo:
                wcr = wcr + [('em', self.now('em'))]
            for qq, srcb in enumerate(('cumhi', 'cumlo', 'ldthi', 'ldtlo')):
                self.dma('gp', sb['pk32'][qq:128:4, :], sb[srcb][:, :],
                         waits=wcr if (qq == 0 or self.fpo) else (), sem='pk')
            tk_pk = {'pk': self.now('pk')}
            crstores = [(0, 'cumhi'), (1, 'cumlo')] + ([(2, 'clh'), (3, 'cll')] if is_b else [])
            wsc = [('ve', tk_dtp['ve']), ('se', tk_dtp['se']),
                   ('cr', self.now('cr')), ('co', self.now('co'))] if self.fpo else []
            for row, srcb in crstores:
                self.dma('gp', self.s_cr[row, :, :], sb[srcb][:, :], waits=wsc, sem='sc')
            self.dma('gp', self.s_cr[4, :, 0:4], sb['pqb'][:, :], waits=wsc, sem='sc')
            tk_crst = {'sc': self.now('sc')}
            t_coll = self.dma('gp', sb['pqrow'][0:1, :].rearrange('a (p c) -> a p c', c=4),
                              self.s_cr[4, :, 0:4].unsqueeze(0),
                              waits=[('sc', tk_crst['sc'])] +
                              ([('te', tk_cr_free['te'])] if self.fpo else []), sem='co')
            tk_coll = {'co': self.now('co')}
            tk_dt_free = {'ve': tk_dtp['ve'], 'co': tk_coll['co']}

            for ci, c in enumerate(chunks):
                gc = g * NCH + c
                par = gc % 2
                crpairs = [(0, 'cumhi'), (1, 'cumlo')] + ([(2, 'clh'), (3, 'cll')] if is_b else [])
                for row, srcb in crpairs:
                    self.dma('gp', sb['crc'][row:row + 1, :].rearrange('a (h t) -> a h t', t=128),
                             self.s_cr[row, :, c * 128:(c + 1) * 128].unsqueeze(0),
                             waits=[('te', tk_crc_users['te']), ('sc', tk_crst['sc'])]
                             if (row == 0 or self.fpo) else (),
                             sem='cr')
                tk_coll_c = {'cr': self.now('cr')}
                for bb8 in range(8):
                    self.dma('gp', sb['emc'][:, bb8 * 128:(bb8 + 1) * 128],
                             sb['pk32'][bb8 * 16:(bb8 + 1) * 16, c * 128:(c + 1) * 128],
                             waits=[('te', tk_crc_users['te']), ('pk', tk_pk['pk'])]
                             if (bb8 == 0 or self.fpo) else (),
                             sem='em')
                tk_em = {'em': self.now('em')}
                Btile = sb['xbcs'][:, 16 * 512 + c * 128: 16 * 512 + c * 128 + 128]
                Ctile = sb['xbcs'][:, 17 * 512 + c * 128: 17 * 512 + c * 128 + 128]
                for half in range(2):
                    for k8 in range(8):
                        cti = half * 8 + k8
                        self.trans(self.ps['pT'][:, k8 * 128:(k8 + 1) * 128],
                                   sb['xbcs'][:, cti * 512 + c * 128: cti * 512 + c * 128 + 128],
                                   waits=([('se', conv_se), ('ve', tk['pT_free']['ve']),
                                           ('gp', self.now('gp'))] if k8 == 0 else ()),
                                   inc=[('te', 1)] if k8 == 7 else ())
                    tkte = self.now('te')
                    tk['pT_free'] = self.ve(lambda nc, e, half=half: nc.vector.tensor_copy(
                        sb['xt'][:, half * 1024:(half + 1) * 1024], self.ps['pT'][:, :]),
                        waits=[('te', tkte)], inc=[('ve', 1)])
                self.trans(self.ps['pT'][:, 0:128], Btile,
                           waits=[('ve', tk['pT_free']['ve'])], inc=[('te', 1)])
                self.trans(self.ps['pT'][:, 128:160], sb['wv'][0:32, c * 128:(c + 1) * 128],
                           waits=[('se', tk_dtp['se'])], inc=[('te', 1)])
                tkte = self.now('te')
                tk_wv_users = {'te': tkte}
                self.ve(lambda nc, e: nc.vector.tensor_copy(sb['Bt'][:, :], self.ps['pT'][:, 0:128]),
                        waits=[('te', tkte)])
                tk['pT_free'] = self.ve(lambda nc, e: nc.vector.tensor_copy(
                    sb['wct'][:, :], self.ps['pT'][:, 128:160]), inc=[('ve', 1)])
                t_xt = tk['pT_free']
                isl = self.next_slot()
                gsl = self.slots[isl]
                self.mm(gsl[:, 0:128], Btile, Ctile, start=True, stop=True,
                        waits=[('ve', self.slot_free[isl]['ve']), ('se', self.slot_free[isl]['se']),
                               ('se', conv_se)], inc=[('te', 1)])
                tgm = self.ve(lambda nc, e, gsl=gsl, m=('triB' if is_b else 'triF'):
                              nc.vector.tensor_tensor(sb['gmask'][:, :], gsl[:, 0:128],
                                                      sb[m][:, :], ALU.mult),
                              waits=[('te', self.now('te'))], inc=[('ve', 1)])
                self.slot_free[isl] = {'ve': tgm['ve'], 'se': 0}
                grep = sb['gmask'][:, :].unsqueeze(1).broadcast_to([128, 16, 128])
                crep = Ctile.unsqueeze(1).broadcast_to([128, 16, 128])
                se_exp_tok = {'se': 0}
                tok_mt, tok_mv = [], []
                for half in range(2):
                    mtb = sb['mt' if half == 0 else 'mt2']
                    lvb = sb['lv' if half == 0 else 'lv2']
                    lv_tok = None
                    for bb in range(4):
                        b = half * 4 + bb
                        i_e = self.next_slot()
                        pe = self.slots[i_e]
                        crs = sb['crc'][0:2, b * 512:(b + 1) * 512]
                        self.mm(pe, sb['ones2n' if is_b else 'ones2'][:, :], crs,
                                start=True, stop=False,
                                waits=[('cr', tk_coll_c['cr']), ('em', tk_em['em']),
                                       ('ve', self.slot_free[i_e]['ve']),
                                       ('se', self.slot_free[i_e]['se'])])
                        self.mm(pe, sb['emc'][0:16, b * 128: b * 128 + 128],
                                sb['indB' if is_b else 'indF'][:, :], start=False, stop=True,
                                inc=[('te', 1)])
                        tke = self.now('te')
                        i_v = self.next_slot()
                        pv = self.slots[i_v]
                        if is_b:
                            self.mm(pv, sb['evB'][:, :], sb['crc'][0:4, b * 512:(b + 1) * 512],
                                    start=True, stop=True,
                                    waits=[('ve', self.slot_free[i_v]['ve']),
                                           ('se', self.slot_free[i_v]['se'])], inc=[('te', 1)])
                        else:
                            self.mm(pv, sb['ones2'][:, :], crs, start=True, stop=True,
                                    waits=[('ve', self.slot_free[i_v]['ve']),
                                           ('se', self.slot_free[i_v]['se'])], inc=[('te', 1)])
                        tkv = self.now('te')
                        e_tok = self.ve(lambda nc, e, pe=pe: nc.vector.tensor_scalar_min(
                            pe, pe, 30.0),
                            waits=[('te', tke)], inc=[('ve', 1)])
                        if self.debug and is_b and gc == 30 and half == 0 and bb == 0:
                            self.dbg_dump([('decl', pe)])
                        w_exp = [('ve', e_tok['ve'])]
                        if bb == 0:
                            w_exp.append(('te', tk['ymm_free']['te']))
                        se_exp_tok = self.se(lambda nc, e, bb=bb, mtb=mtb, pe=pe: nc.scalar.activation(
                            mtb[:, bb * 512:(bb + 1) * 512],
                            pe, AF.Exp),
                            waits=w_exp, inc=[('se', 1)])
                        self.slot_free[i_e] = {'ve': 0, 'se': se_exp_tok['se']}
                        lv_tok = self.se(lambda nc, e, pv=pv, bb=bb, lvb=lvb: nc.scalar.activation(
                            lvb[:, bb * 512:(bb + 1) * 512], pv, AF.Exp),
                            waits=[('te', tkv)], inc=[('se', 1)])
                        self.slot_free[i_v] = {'ve': 0, 'se': lv_tok['se']}
                    self.ve(lambda nc, e, grep=grep, mtb=mtb: nc.vector.tensor_tensor(
                        mtb[:, :].rearrange('p (h t) -> p h t', t=128),
                        mtb[:, :].rearrange('p (h t) -> p h t', t=128), grep, ALU.mult),
                        waits=[('se', se_exp_tok['se']), ('ve', tgm['ve'])])
                    tok_mt.append(self.ve(lambda nc, e, half=half, mtb=mtb: nc.vector.tensor_tensor(
                        mtb[:, :], mtb[:, :],
                        sb['ddiag'][:, half * 2048:(half + 1) * 2048], ALU.add),
                        waits=[('dd', t_ddg['dd'])], inc=[('ve', 1)]))
                    tok_mv.append(self.ve(lambda nc, e, crep=crep, lvb=lvb: nc.vector.tensor_tensor(
                        lvb[:, :].rearrange('p (h t) -> p h t', t=128),
                        lvb[:, :].rearrange('p (h t) -> p h t', t=128), crep, ALU.mult),
                        waits=[('se', lv_tok['se']), ('se', conv_se)], inc=[('ve', 1)]))
                tk_crc_users = {'te': self.now('te')}
                if self.debug and is_b and gc == 30:
                    self.dbg_dump([('dlexp', sb['mt2'][:, :]), ('dlv', sb['lv'][:, :]),
                                   ('dmt', sb['mt'][:, :]),
                                   ('dhprev', sb['hst'][:, :])])
                for half in range(2):
                    mtb = sb['mt' if half == 0 else 'mt2']
                    lvb = sb['lv' if half == 0 else 'lv2']
                    tmt, tmv = tok_mt[half], tok_mv[half]
                    for jj in range(8):
                        j = half * 8 + jj
                        h0, h1 = 2 * j, 2 * j + 1
                        l0, l1 = 2 * jj, 2 * jj + 1
                        slot = self.ps['pY'][:, jj * 128:(jj + 1) * 128]
                        w0 = [('ve', tmt['ve']), ('ve', tmv['ve']), ('ve', tk['pY_free']['ve']),
                              ('ve', tk_hup['ve']), ('ve', t_xt['ve'])] if jj == 0 else ()
                        self.mm(slot[0:64, :], sb['xt'][:, h0 * 64:(h0 + 1) * 64],
                                mtb[:, l0 * 128:(l0 + 1) * 128], True, False, waits=w0, tp=(0, 0))
                        self.mm(slot[0:64, :], sb['hst'][:, h0 * 64:(h0 + 1) * 64],
                                lvb[:, l0 * 128:(l0 + 1) * 128], False, True, tp=(0, 0))
                        self.mm(slot[64:128, :], sb['xt'][:, h1 * 64:(h1 + 1) * 64],
                                mtb[:, l1 * 128:(l1 + 1) * 128], True, False, tp=(0, 64))
                        self.mm(slot[64:128, :], sb['hst'][:, h1 * 64:(h1 + 1) * 64],
                                lvb[:, l1 * 128:(l1 + 1) * 128], False, True, tp=(0, 64),
                                inc=[('te', 1)] if jj == 7 else ())
                    tk_ymm = {'te': self.now('te')}
                    tk['ymm_free'] = tk_ymm
                    tk['hbf_users'] = tk_ymm
                    zslc = sb['zsil'][:, :].rearrange('p (j t) -> p j t', t=512)[
                        :, half * 8:(half + 1) * 8, c * 128:(c + 1) * 128]
                    tk['pY_free'] = self.ve(lambda nc, e, zslc=zslc, par=par, half=half:
                                            nc.vector.tensor_tensor(
                        sb['yg'][:, par * 2048 + half * 1024: par * 2048 + (half + 1) * 1024]
                        .rearrange('p (j t) -> p j t', t=128),
                        self.ps['pY'][:, :].rearrange('p (j t) -> p j t', t=128),
                        zslc, ALU.mult),
                        waits=[('te', tk_ymm['te']), ('te', tk['yg_free'][par]['te']),
                               ('se', tk_sweep['se'])], inc=[('ve', 1)])
                tk_zsil_users = tk['pY_free']
                i_p = self.next_slot()
                psl = self.slots[i_p]
                self.mm(psl[:, 0:32], sb['ones2'][0:1, :], sb['pqrow'][0:1, :].rearrange('a (p c) -> a p c', c=4)[:, :, c], True, True,
                        waits=[('co', tk_coll['co']), ('ve', self.slot_free[i_p]['ve']),
                               ('se', self.slot_free[i_p]['se'])], inc=[('te', 1)])
                tpqb = self.ve(lambda nc, e, psl=psl: nc.vector.tensor_copy(
                    sb['pqbc'][:, :], psl[:, 0:32]),
                    waits=[('te', self.now('te'))], inc=[('ve', 1)])
                self.slot_free[i_p] = {'ve': tpqb['ve'], 'se': 0}
                tk_xw = self.ve(lambda nc, e: nc.vector.tensor_tensor(
                    sb['mt'][:, :].rearrange('p (h q) -> p h q', q=64),
                    sb['xt'][:, :].rearrange('p (h q) -> p h q', q=64),
                    sb['wct'][:, :].unsqueeze(2).broadcast_to([128, NH, 64]), ALU.mult),
                    waits=[('ve', t_xt['ve']), ('te', tk_ymm['te'])], inc=[('ve', 1)])
                tk_mt_gp = {'gp': 0}
                for qq in range(4):
                    i_h = self.next_slot()
                    hs = self.slots[i_h]
                    w0 = [('ve', self.slot_free[i_h]['ve']), ('se', self.slot_free[i_h]['se'])]
                    if qq == 0:
                        w0.append(('ve', tk_xw['ve']))
                    self.mm(hs, sb['Bt'][:, :], sb['mt'][:, qq * 512:(qq + 1) * 512], True, True,
                            waits=w0, inc=[('te', 1)])
                    tkh = self.now('te')
                    hsl = sb['hst'][:, qq * 512:(qq + 1) * 512]
                    self.ve(lambda nc, e, hsl=hsl, qq=qq: nc.vector.tensor_tensor(
                        hsl.rearrange('p (h q) -> p h q', q=64),
                        hsl.rearrange('p (h q) -> p h q', q=64),
                        sb['pqbc'][:, qq * 8:(qq + 1) * 8].unsqueeze(2).broadcast_to([128, 8, 64]),
                        ALU.mult),
                        waits=[('te', tkh), ('ve', tpqb['ve'])])
                    tup = self.ve(lambda nc, e, hsl=hsl, hs=hs: nc.vector.tensor_tensor(
                        hsl, hsl, hs, ALU.add), inc=[('ve', 1)])
                    self.slot_free[i_h] = {'ve': tup['ve'], 'se': 0}
                tk['ymm_free'] = {'te': self.now('te')}
                tk_hup = {'ve': self.now('ve')}
                for dh in range(2):
                    for kt in range(16):
                        w0 = [('ve', tk['pY_free']['ve']), ('wo', t_wo['wo'])] \
                            if (dh == 0 and kt == 0) else ()
                        self.mm(self.ps['pY'][:, dh * 512:(dh + 1) * 512],
                                sb['yg'][:, par * 2048 + kt * 128: par * 2048 + (kt + 1) * 128],
                                sb['b_wo'][:, kt * 1024 + dh * 512: kt * 1024 + dh * 512 + 512],
                                start=(kt == 0), stop=(kt == 15),
                                waits=w0, inc=[('te', 1)] if (kt == 15 and dh == 1) else ())
                tk_wo = {'te': self.now('te')}
                tk['yg_free'][par] = tk_wo
                if self.debug and is_b and gc == 30:
                    self.dbg_dump([('dxnT', sb['xnT'][:, :]), ('dzsil', sb['zsil'][:, :]),
                                   ('dxbcs', sb['xbcs'][:, :]), ('ddt', sb['dt_'][:, :]),
                                   ('dcum', sb['cum'][:, :]),
                                   ('dgm', sb['gmask'][:, :]),
                                   ('dyg', sb['yg'][:, 0:2048]),
                                   ('demega', sb['pk32'][:, :]), ('dcrc', sb['crc'][:, :]),
                                   ('dwv', sb['wv'][:, :]), ('dxt', sb['xt'][:, :]),
                                   ('dhst', sb['hst'][:, :])])
                if not is_b:
                    tev = self.ve(lambda nc, e, par=par: nc.vector.tensor_copy(
                        sb['yfc'][:, par * 1024:(par + 1) * 1024], self.ps['pY'][:, :]),
                        waits=[('te', tk_wo['te']), (f'yf{par}', tk['yfc_free'][par]['st'])],
                        inc=[('ve', 1)])
                    tk['pY_free'] = tev
                    tst = self.dma('sy', self.s_yf[gc * 128:(gc + 1) * 128, :],
                                   sb['yfc'][:, par * 1024:(par + 1) * 1024],
                                   waits=[('ve', tev['ve'])], sem=f'yf{par}')
                    tk['yfc_free'][par] = {'st': tst[f'yf{par}'], 've': tev['ve']}
                else:
                    tldy = self.dma('sy', sb['yfc'][:, par * 1024:(par + 1) * 1024],
                                    self.s_yf[gc * 128:(gc + 1) * 128, :],
                                    waits=[('ve', tk['yfc_free'][par]['ve'])] +
                                    ([(f'yf{par}', self.now(f'yf{par}'))] if self.fpo else []),
                                    sem=f'yl{par}')
                    tev = self.ve(lambda nc, e, par=par: nc.vector.tensor_tensor(
                        sb['xm'][:, par * 1024:(par + 1) * 1024], self.ps['pY'][:, :],
                        sb['yfc'][:, par * 1024:(par + 1) * 1024], ALU.add),
                        waits=[('te', tk_wo['te']), (f'yl{par}', tldy[f'yl{par}']),
                               ('ve', tk['xm_free'][par]['ve'])],
                        inc=[('ve', 1)])
                    tk['pY_free'] = tev
                    tk['yfc_free'][par] = {'st': tk['yfc_free'][par]['st'], 've': tev['ve']}
                    tk['xm_full'][par] = tev
                    if ci % 2 == 1:
                        self._final_pair(g, [chunks[ci - 1], c])
            tk_zsil_free = {'te': self.now('te'), 've': tk_zsil_users['ve']}
            tk_xbcs_free = {'te': self.now('te'), 'gp': self.now('gp')}
            tk_cr_free = {'te': self.now('te')}
            tk_xnT_free = {'te': self.now('te')}
        self.tk['group_done'] = {'te': self.now('te'), 've': self.now('ve'),
                                 'se': self.now('se'), 'gp': self.now('gp')}

    def _final_pair(self, g, ccs):
        sb, tk, d = self.sb, self.tk, self.d
        for idx, c in enumerate(ccs):
            gc = g * NCH + c
            par = gc % 2
            txb = self.ve(lambda nc, e, par=par: nc.vector.tensor_copy(
                sb['xmb'][:, :], sb['xm'][:, par * 1024:(par + 1) * 1024]),
                waits=[('ve', tk['xm_full'][par]['ve'])], inc=[('ve', 1)])
            for kt in range(8):
                self.trans(self.ps['pT'][:, kt * 128:(kt + 1) * 128],
                           sb['xmb'][:, kt * 128:(kt + 1) * 128],
                           waits=([('ve', txb['ve']), ('ve', tk['pT_free']['ve'])] if kt == 0 else ()),
                           inc=[('te', 1)] if kt == 7 else ())
            tk['pT_free'] = self.ve(lambda nc, e: nc.vector.tensor_copy(
                sb['xmT'][:, :], self.ps['pT'][:, :]),
                waits=[('te', self.now('te'))], inc=[('ve', 1)])
            txt = tk['pT_free']
            for dh in range(2):
                i_w = self.next_slot()
                pw = self.slots[i_w]
                for kt in range(8):
                    sl = kt % 4
                    twm = self.dma('gp', sb['b_m'][:, sl * 512: (sl + 1) * 512],
                                   d['wm'].rearrange('(kt p) n -> kt p n', p=128)[kt, :, dh * 512:(dh + 1) * 512],
                                   waits=[('te', tk['bm_free'][sl]['te'])], sem=f'wm{sl}')
                    w0 = [(f'wm{sl}', twm[f'wm{sl}'])]
                    if kt == 0:
                        w0 += [('ve', self.slot_free[i_w]['ve']), ('se', self.slot_free[i_w]['se']),
                               ('ve', txt['ve'])]
                    self.mm(pw, sb['xmT'][:, kt * 128:(kt + 1) * 128],
                            sb['b_m'][:, sl * 512:(sl + 1) * 512],
                            start=(kt == 0), stop=False, waits=w0,
                            inc=[('te', 1)])
                    tk['bm_free'][sl] = {'te': self.now('te')}
                self.mm(pw, sb['ones2'][:, :], sb['bmr'][:, dh * 512:(dh + 1) * 512],
                        start=False, stop=True, inc=[('te', 1)])
                tth = self.se(lambda nc, e, pw=pw, idx=idx, dh=dh: nc.scalar.activation(
                    sb['t1'][:, idx * 1024 + dh * 512: idx * 1024 + (dh + 1) * 512], pw, AF.Tanh),
                    waits=[('te', self.now('te')),
                           (f'do{idx}', tk['outb_free'][idx].get('do', 0))], inc=[('se', 1)])
                self.slot_free[i_w] = {'ve': 0, 'se': tth['se']}
        tk_t1 = {'se': self.now('se')}
        for dh in range(2):
            for kt in range(8):
                sl = kt % 4
                pevw = [('pev', self.now('pev'))] if self.fpo else []
                tmh = self.dma('sy', sb['b_m'][:, sl * 512: (sl + 1) * 512],
                               self.s_m[0, kt, dh],
                               waits=[('te', tk['bm_free'][sl]['te'])] + pevw,
                               sem=f'fm{sl}')
                tml = self.dma('sy', sb['b_m'][:, 2048 + sl * 512: 2048 + (sl + 1) * 512],
                               self.s_m[1, kt, dh],
                               waits=[('te', tk['bm_free'][4 + sl]['te'])] + pevw,
                               sem=f'fl{sl}')
                for idx, c in enumerate(ccs):
                    w0 = [(f'fm{sl}', tmh[f'fm{sl}']), (f'fl{sl}', tml[f'fl{sl}'])]
                    if kt == 0:
                        w0.append(('ve', tk['pY_free']['ve']))
                    self.mm(self.ps['pY'][:, idx * 512:(idx + 1) * 512],
                            sb['xnT'][:, kt * 512 + c * 128: kt * 512 + c * 128 + 128],
                            sb['b_m'][:, sl * 512:(sl + 1) * 512],
                            start=(kt == 0), stop=False, waits=w0)
                    self.mm(self.ps['pY'][:, idx * 512:(idx + 1) * 512],
                            sb['xnTl'][:, kt * 512 + c * 128: kt * 512 + c * 128 + 128],
                            sb['b_m'][:, sl * 512:(sl + 1) * 512], False, False,
                            inc=[('te', 1)] if idx == 1 else ())
                    if idx == 1:
                        tk['bm_free'][sl] = {'te': self.now('te')}
                    self.mm(self.ps['pY'][:, idx * 512:(idx + 1) * 512],
                            sb['xnT'][:, kt * 512 + c * 128: kt * 512 + c * 128 + 128],
                            sb['b_m'][:, 2048 + sl * 512: 2048 + (sl + 1) * 512], False, False,
                            inc=[('te', 1)] if idx == 1 else ())
                    if idx == 1:
                        tk['bm_free'][4 + sl] = {'te': self.now('te')}
            for idx, c in enumerate(ccs):
                self.mm(self.ps['pY'][:, idx * 512:(idx + 1) * 512],
                        sb['ones2'][:, :], sb['fbr'][:, dh * 512:(dh + 1) * 512],
                        False, True, inc=[('te', 1)])
            tkff = {'te': self.now('te')}
            tlast = None
            for idx, c in enumerate(ccs):
                tlast = self.ve(lambda nc, e, idx=idx, dh=dh: nc.vector.scalar_tensor_tensor(
                    sb['t1'][:, idx * 1024 + dh * 512: idx * 1024 + (dh + 1) * 512],
                    sb['t1'][:, idx * 1024 + dh * 512: idx * 1024 + (dh + 1) * 512],
                    0.1, self.ps['pY'][:, idx * 512:(idx + 1) * 512], ALU.mult, ALU.add),
                    waits=[('te', tkff['te']), ('se', tk_t1['se'])], inc=[('ve', 1)] if idx == 1 else ())
            tk['pY_free'] = {'ve': self.now('ve')}
        for idx, c in enumerate(ccs):
            gc = g * NCH + c
            par = gc % 2
            self.ve(lambda nc, e, idx=idx, par=par: nc.vector.tensor_tensor(
                sb['t1'][:, idx * 1024:(idx + 1) * 1024],
                sb['t1'][:, idx * 1024:(idx + 1) * 1024],
                sb['xm'][:, par * 1024:(par + 1) * 1024], ALU.add), inc=[('ve', 1)])
            tk['xm_free'][par] = {'ve': self.now('ve')}
            tld = self.dma('sy', sb['xq'][:, par * 1024:(par + 1) * 1024],
                           d['x'][gc * 128:(gc + 1) * 128, :],
                           waits=[('ve', tk['xq_free'][par]['ve'])], sem=f'xq{par}')
            tfin = self.ve(lambda nc, e, idx=idx, par=par: nc.vector.tensor_tensor(
                sb['t1'][:, idx * 1024:(idx + 1) * 1024],
                sb['t1'][:, idx * 1024:(idx + 1) * 1024],
                sb['xq'][:, par * 1024:(par + 1) * 1024], ALU.add),
                waits=[(f'xq{par}', tld[f'xq{par}'])], inc=[('ve', 1)])
            tk['xq_free'][par] = tfin
            tst = self.dma('sy', self.out[gc * 128:(gc + 1) * 128, :],
                           sb['t1'][:, idx * 1024:(idx + 1) * 1024],
                           waits=[('ve', tfin['ve'])], sem=f'do{idx}')
            tk['outb_free'][idx] = {'do': tst[f'do{idx}']}

    def _emit_block(self):
        nc = self.nc
        sems = self.sems

        def runner(name, eng):
            for fn, waits, incs in self.q[name]:
                for s, v in waits:
                    eng.wait_ge(sems[s], v)
                if fn is None:
                    continue
                inst = fn(nc, eng)
                for s, a in incs:
                    inst.then_inc(sems[s], a)

        with nc.Block() as blk:
            @blk.tensor
            def _(eng):
                runner('te', eng)

            @blk.vector
            def _(eng):
                runner('ve', eng)

            @blk.scalar
            def _(eng):
                runner('se', eng)

            @blk.gpsimd
            def _(eng):
                runner('gp', eng)

            @blk.sync
            def _(eng):
                runner('sy', eng)


def make_inputs(inp):
    c = host_consts()
    m = {}
    m['x'] = None
    for s in ('f', 'b'):
        m[f'wi_{s}'] = np.ascontiguousarray(inp[f'L0_attn_Wi_{s}'], np.float32)
        m[f'wo_{s}'] = np.ascontiguousarray(inp[f'L0_attn_Wo_{s}'], np.float32)
        m[f'convw_{s}'] = np.ascontiguousarray(inp[f'L0_attn_conv_{s}'], np.float32)
        m[f'convb_{s}'] = np.ascontiguousarray(inp[f'L0_attn_convb_{s}'], np.float32)
        m[f'alog_{s}'] = np.ascontiguousarray(inp[f'L0_attn_Alog_{s}'], np.float32)
        m[f'dtb_{s}'] = np.ascontiguousarray(inp[f'L0_attn_dtb_{s}'], np.float32)
        dd = np.zeros((128, 4096), np.float32)
        Dv = np.asarray(inp[f'L0_attn_D_{s}'], np.float32)
        for h in range(32):
            for ss in range(128):
                dd[ss, h * 128 + ss] = Dv[h]
        m[f'ddg_{s}'] = dd
    m['wm'] = np.ascontiguousarray(inp['L0_attn_Wm'], np.float32)
    m['bm'] = np.ascontiguousarray(inp['L0_attn_bm'], np.float32)
    m['wfr'] = np.ascontiguousarray(inp['L0_ffn_w_freq_real'], np.float32)
    m['wfi'] = np.ascontiguousarray(inp['L0_ffn_w_freq_imag'], np.float32)
    m['fbias'] = np.ascontiguousarray(inp['L0_ffn_bias'], np.float32)
    for k, v in c.items():
        m[k] = v
    return m


def kernel(**inp):
    x = np.asarray(inp['L0_attn_x'], np.float32)
    base = make_inputs(inp)
    b = Builder()
    nc = b.build()
    in_maps = []
    for core in range(8):
        mm_ = dict(base)
        mm_['x'] = np.ascontiguousarray(x[core])
        in_maps.append(mm_)
    res = run_bass_kernel_spmd(nc, in_maps, core_ids=list(range(8)))
    out = np.stack([r['out'] for r in res.results], axis=0)
    return out


# version 15
# speedup vs baseline: 1.7993x; 1.0005x over previous
import os
import numpy as np
import ml_dtypes
import concourse.bass as bass
import concourse.mybir as mybir
from concourse.bass_utils import run_bass_kernel_spmd

F32 = mybir.dt.float32
BF16 = mybir.dt.bfloat16
AF = mybir.ActivationFunctionType
ALU = mybir.AluOpType
BF = ml_dtypes.bfloat16

DM, DI, NH = 1024, 2048, 32
CONVD = DI + 2 * 128
PROJ = 2 * DI + 2 * 128 + NH
NCH, NG, L = 4, 8, 4096
NJ = 35


def _hilo(x):
    h = x.astype(BF).astype(np.float32)
    return h.astype(BF), (x - h).astype(BF)


def host_consts():
    j = np.arange(DM)
    ang = 2 * np.pi * np.outer(j, j) / DM
    Fr = np.cos(ang).astype(np.float32)
    Fi = (-np.sin(ang)).astype(np.float32)
    c = {}
    c['cFr_h'], c['cFr_l'] = _hilo(Fr)
    c['cGi_h'], c['cGi_l'] = _hilo(-Fi)
    c['cFi_h'], c['cFi_l'] = _hilo(Fi)
    c['cFrd_h'], c['cFrd_l'] = _hilo(Fr / DM)
    c['cFid_h'], c['cFid_l'] = _hilo(Fi / DM)
    c['cIdent'] = np.eye(128).astype(BF)
    tri = np.tril(np.ones((128, 128), np.float32))
    c['cTriF'] = tri.T.astype(BF)
    c['cTriB'] = tri.astype(BF)
    indF = np.zeros((16, 512), np.float32)
    indB = np.zeros((16, 512), np.float32)
    for i in range(4):
        blk = slice(i * 128, (i + 1) * 128)
        indF[i * 4 + 0, blk] = -1.0; indF[i * 4 + 1, blk] = -1.0
        indF[i * 4 + 2, blk] = 1.0; indF[i * 4 + 3, blk] = 1.0
        indB[i * 4 + 0, blk] = 1.0; indB[i * 4 + 1, blk] = 1.0
        indB[i * 4 + 2, blk] = 1.0; indB[i * 4 + 3, blk] = 1.0
    c['cIndF'] = indF.astype(BF)
    c['cIndB'] = indB.astype(BF)
    c['cOnes2'] = np.ones((2, 128), np.float32).astype(BF)
    c['cOnes2N'] = (-np.ones((2, 128), np.float32)).astype(BF)
    ev = np.ones((4, 128), np.float32); ev[0] = -1.0; ev[1] = -1.0
    c['cEvB'] = ev.astype(BF)
    c['cOnesF'] = np.ones((32, 128), np.float32)
    return c


class Builder:
    def __init__(self, debug=False):
        import contextlib
        self.fpo = os.environ.get('BASS_SIM_FP64') == '1'
        self._qlast = {}
        self.nc = bass.Bass("TRN2", target_bir_lowering=False, debug=False)
        self.ctx = contextlib.ExitStack()
        self.debug = debug
        self.q = {e: [] for e in ("te", "ve", "se", "gp", "sy")}
        self.cnt = {}
        self._declare()

    def emit(self, eng, fn, waits=(), inc=(), is_dma=False):
        if self.fpo and fn is not None and not is_dma and eng in ('te', 've', 'se', 'gp'):
            waits = list(waits) + [(eng, self.cnt[eng])]
            if not any(s == eng for s, _ in inc):
                inc = list(inc) + [(eng, 1)]
        incs = []
        for s, a in inc:
            self.cnt[s] += a
            incs.append((s, a))
        wts = tuple((s, v) for (s, v) in waits if v > 0)
        self.q[eng].append((fn, wts, tuple(incs)))
        return {s: self.cnt[s] for s, _ in incs}

    def now(self, s):
        return self.cnt[s]

    def dma(self, qeng, dst, src, waits=(), sem=None, slow=False):
        sem = sem or ('dsy' if qeng == 'sy' else 'dgp')
        if self.fpo:
            prev = self._qlast.get(qeng)
            if prev is not None:
                waits = list(waits) + [prev]
        tok = self.emit(qeng, lambda nc, e, dst=dst, src=src, slow=slow:
                        e.dma_start(out=dst, in_=src, allow_slow_non_contiguous=slow),
                        waits=waits, inc=[(sem, 16)], is_dma=True)
        if self.fpo:
            self._qlast[qeng] = (sem, tok[sem])
        return tok

    def mm(self, psd, lhsT, rhs, start, stop, waits=(), inc=(), tp=None):
        def f(nc, e, a=psd, b=lhsT, c=rhs, st=start, sp=stop, tp=tp):
            return nc.tensor.matmul(a, b, c, start=st, stop=sp, tile_position=tp)
        return self.emit('te', f, waits=waits, inc=inc)

    def trans(self, psd, src, waits=(), inc=()):
        ident = self.sb['ident']
        k = src.shape[0]
        return self.emit('te', lambda nc, e, a=psd, b=src, i=ident, k=k:
                         nc.tensor.transpose(a, b, i[0:k, 0:k]), waits=waits, inc=inc)

    def ve(self, fn, waits=(), inc=()):
        return self.emit('ve', fn, waits=waits, inc=inc)

    def se(self, fn, waits=(), inc=()):
        return self.emit('se', fn, waits=waits, inc=inc)

    def gp(self, fn, waits=(), inc=()):
        return self.emit('gp', fn, waits=waits, inc=inc)

    def _declare(self):
        nc = self.nc
        self.d = {}
        def din(name, shape, dt=F32):
            self.d[name] = nc.dram_tensor(name, shape, dt, kind="ExternalInput").ap()
        din('x', [L, DM])
        for s in ('f', 'b'):
            din(f'wi_{s}', [DM, PROJ]); din(f'wo_{s}', [DI, DM])
            din(f'convw_{s}', [4, CONVD]); din(f'convb_{s}', [CONVD])
            din(f'alog_{s}', [NH]); din(f'dtb_{s}', [NH])
            din(f'ddg_{s}', [128, 4096])
        din('wm', [DM, DM]); din('bm', [DM])
        din('wfr', [DM, DM]); din('wfi', [DM, DM]); din('fbias', [DM])
        for k, v in host_consts().items():
            din(k, list(v.shape), BF16 if v.dtype == BF else F32)
        self.out = nc.dram_tensor('out', [L, DM], F32, kind="ExternalOutput").ap()
        dbg = "ExternalOutput" if self.debug else "Internal"
        self.s_wi = {s: nc.dram_tensor(f's_wi_{s}', [NJ, 128, 8, 128], BF16, kind="Internal").ap()
                     for s in ('f', 'b')}
        self.s_yf = nc.dram_tensor('s_yf', [L, DM], BF16, kind=dbg).ap()
        self.s_w = nc.dram_tensor('s_w', [4, 8, 8, 128, 128], BF16, kind=dbg).ap()
        self.s_a = nc.dram_tensor('s_a', [4, 8, 8, 128, 128], BF16, kind=dbg).ap()
        self.s_m = nc.dram_tensor('s_m', [2, 8, 2, 128, 512], BF16, kind=dbg).ap()
        self.s_cr = nc.dram_tensor('s_cr', [5, 32, 512], BF16, kind="Internal").ap()
        if self.debug:
            self.dbgt = {}
            for nm, shape, dt in [('dxnT', [128, 4096], BF16), ('dzsil', [128, 8192], BF16),
                                  ('dxbcs', [128, 9216], BF16), ('ddt', [32, 512], F32),
                                  ('dcum', [32, 512], F32), ('dlexp', [128, 2048], BF16),
                                  ('dlv', [128, 2048], BF16), ('dmt', [128, 2048], BF16),
                                  ('dgm', [128, 128], BF16), ('dyg', [128, 2048], BF16),
                                  ('demega', [128, 512], BF16), ('dcrc', [4, 4096], BF16),
                                  ('dwv', [32, 512], BF16), ('dxt', [128, 2048], BF16),
                                  ('dhst', [128, 2048], BF16), ('decl', [128, 512], F32),
                                  ('dhprev', [128, 2048], BF16)]:
                self.dbgt[nm] = nc.dram_tensor(nm, shape, dt, kind='ExternalOutput').ap()

        self.sb = {}
        def sbuf(name, shape, dt):
            self.sb[name] = self.ctx.enter_context(nc.sbuf_tensor(name, shape, dt))
        for name, shape, dt in [
            ('b_wi', [128, 4096], BF16), ('b_wo', [128, 16384], BF16),
            ('b_m', [128, 4096], BF16),
            ('zsil', [128, 8192], BF16), ('xbcs', [128, 9216], BF16),
            ('tmpc', [128, 1040], BF16), ('cstash', [128, 54], BF16),
            ('cw', [128, 72], F32), ('cb', [128, 18], F32),
            ('yg', [128, 4096], BF16), ('xm', [128, 2048], BF16), ('yfc', [128, 2048], BF16),
            ('xnT', [128, 4096], BF16), ('xnTl', [128, 4096], BF16),
            ('xq', [128, 2048], F32),
            ('xnch', [128, 1024], BF16), ('xncl', [128, 1024], BF16),
            ('lv', [128, 2048], BF16),
            ('mt', [128, 2048], BF16), ('mt2', [128, 2048], BF16), ('lv2', [128, 2048], BF16),
            ('gmask', [128, 128], BF16), ('xt', [128, 2048], BF16),
            ('hst', [128, 2048], BF16),
            ('Bt', [128, 128], BF16), ('wct', [128, 32], F32), ('pqbc', [128, 32], F32),
            ('ddiag', [128, 4096], BF16),
            ('dt_', [32, 512], F32), ('lgda', [32, 512], F32), ('cum', [32, 512], F32),
            ('lgdt', [32, 512], F32),
            ('cumhi', [32, 512], BF16), ('cumlo', [32, 512], BF16),
            ('ldthi', [32, 512], BF16), ('ldtlo', [32, 512], BF16),
            ('wv', [32, 512], BF16), ('pqb', [32, 4], BF16),
            ('clh', [32, 512], BF16), ('cll', [32, 512], BF16),
            ('clsc', [32, 2], F32),
            ('emc', [16, 1024], BF16), ('crc', [4, 4096], BF16), ('pqrow', [1, 128], BF16),
            ('pk32', [128, 512], BF16),
            ('onesf', [32, 128], F32),
            ('t1', [128, 2048], F32),

            ('mean', [128, 1], F32), ('sqs', [128, 1], F32), ('rstd', [128, 1], F32),
            ('nmrs', [128, 1], F32), ('epsc', [128, 1], F32),
            ('negA', [32, 1], F32), ('aexp', [32, 1], F32), ('dtbc', [32, 1], F32),
            ('bmr', [2, 1024], BF16), ('fbr', [2, 1024], BF16),
            ('ident', [128, 128], BF16), ('triF', [128, 128], BF16), ('triB', [128, 128], BF16),
            ('indF', [16, 512], BF16), ('indB', [16, 512], BF16),
            ('ones2', [2, 128], BF16), ('ones2n', [2, 128], BF16), ('evB', [4, 128], BF16),
            ('wprep', [128, 1024], F32),
        ]:
            sbuf(name, shape, dt)
        self.sb['pstr'] = self.sb['zsil']
        self.sb['lstr'] = self.sb['xbcs']
        self.sb['wpo'] = self.sb['yg']
        self.sb['convt'] = self.sb['wprep']
        self.sb['cump'] = self.sb['lgda']
        self.sb['wvin'] = self.sb['dt_']
        self.sb['brow'] = None
        self.sb['xmb'] = self.sb['xnch']
        self.sb['xmT'] = self.sb['xncl']

        self.ps = {}
        def psumt(name, shape, dt):
            self.ps[name] = self.ctx.enter_context(nc.psum_tensor(name, shape, dt))
        psumt('pS0', [128, 512], F32); psumt('pS1', [128, 512], F32)
        psumt('pQ0', [128, 512], F32); psumt('pQ1', [128, 512], F32); psumt('pQ2', [128, 512], F32)
        psumt('pT', [128, 1024], BF16)
        psumt('pY', [128, 1024], F32)

        self.semnames = ['dsy', 'dgp', 'te', 've', 'se', 'gp',
                         'do0', 'do1',
                         'sst0', 'sst1', 'sld0', 'sld1', 'wld', 'wst', 'bld',
                         'ph0', 'ph1', 'ph2', 'ph3', 'pev', 'dp', 'dd', 'wo',
                         'xq0', 'xq1', 'wi0', 'wi1', 'wi2', 'wi3',
                         'pk', 'em', 'sc', 'co', 'cr', 'pc',
                         'yf0', 'yf1', 'yl0', 'yl1',
                         'wm0', 'wm1', 'wm2', 'wm3', 'fm0', 'fm1', 'fm2', 'fm3',
                         'fl0', 'fl1', 'fl2', 'fl3']
        self.sems = {n: self.ctx.enter_context(nc.semaphore(n)) for n in self.semnames}
        for n in self.semnames:
            self.cnt[n] = 0

        self.slots = [self.ps['pS0'][:, :], self.ps['pS1'][:, :], self.ps['pQ0'][:, :],
                      self.ps['pQ1'][:, :], self.ps['pQ2'][:, :]]
        self.slot_free = [{'ve': 0, 'se': 0} for _ in range(5)]
        self.pool_i = 0
        self.tk = dict(
            pT_free={'ve': 0}, pY_free={'ve': 0},
            yg_free=[{'te': 0}, {'te': 0}],
            yfc_free=[{'st': 0, 've': 0}, {'st': 0, 've': 0}],
            xm_free=[{'ve': 0}, {'ve': 0}], xm_full=[{'ve': 0}, {'ve': 0}],
            xq_free=[{'ve': 0}, {'ve': 0}], xq2_free=[{'ve': 0}, {'ve': 0}],
            outb_free=[{'do': 0}, {'do': 0}],
            wi_free=[{'te': 0}, {'te': 0}, {'te': 0}, {'te': 0}],
            bm_free=[{'te': 0} for _ in range(8)],
            group_done={'te': 0, 've': 0, 'se': 0, 'gp': 0},
            ymm_free={'te': 0}, hbf_users={'te': 0},
            t1_free=[{'ve': 0}, {'ve': 0}],
        )

    def next_slot(self):
        i = self.pool_i % 5
        self.pool_i += 1
        return i

    def dbg_dump(self, pairs):
        wd = [(n, self.now(n)) for n in self.semnames]
        t = None
        for nm, src in pairs:
            t = self.dma('sy', self.dbgt[nm], src, waits=wd)
            wd = ()
        for q in ('te', 've', 'se', 'gp'):
            self.emit(q, None, waits=[('dsy', t['dsy'])])

    def build(self):
        self._prep()
        self._phase0()
        self._dir(False)
        self._dir(True)
        self.emit('sy', None, waits=[('do0', self.now('do0')), ('do1', self.now('do1'))])
        self._emit_block()
        return self.nc

    def _prep(self):
        d, sb = self.d, self.sb
        for nm, tgt in [('cIdent', 'ident'), ('cTriF', 'triF'), ('cTriB', 'triB'),
                        ('cIndF', 'indF'), ('cIndB', 'indB'), ('cOnes2', 'ones2'),
                        ('cOnes2N', 'ones2n'), ('cEvB', 'evB'), ('cOnesF', 'onesf')]:
            self.dma('sy', sb[tgt][:, :], d[nm])


        for q in ('te', 've', 'se', 'gp'):
            self.emit(q, None, waits=[('dsy', self.now('dsy'))])
        self.ve(lambda nc, e: nc.vector.memset(sb['epsc'][:, :], 1e-5), inc=[('ve', 1)])
        if self.fpo:
            self.ve(lambda nc, e: nc.vector.memset(sb['pk32'][:, :], 0.0), inc=[('ve', 1)])
        prev = {}
        for s in ('f', 'b'):
            wi_r = d[f'wi_{s}'].rearrange('(kt p) c -> p kt c', p=128)
            for j in range(NJ):
                jw = PROJ - (NJ - 1) * 128 if j == NJ - 1 else 128
                par = j % 2
                key = (s, j - 2) if j >= 2 else (('f', NJ - 1 - par) if s == 'b' else None)
                wts = [(f'sst{par}', prev[key][f'sst{par}'])] if key in prev else ()
                tld = self.dma('gp', sb['wpo'][:, par * 1024: par * 1024 + 8 * jw]
                               .rearrange('p (kt jj) -> p kt jj', jj=jw),
                               wi_r[:, :, j * 128: j * 128 + jw], waits=wts, sem=f'sld{par}')
                prev[(s, j)] = self.dma(
                    'sy', self.s_wi[s][j, :, :, 0:jw],
                    sb['wpo'][:, par * 1024: par * 1024 + 8 * jw]
                    .rearrange('p (kt jj) -> p kt jj', jj=jw),
                    waits=[(f'sld{par}', tld[f'sld{par}'])], sem=f'sst{par}')
        self._sst = [('sst0', self.now('sst0')), ('sst1', self.now('sst1'))]
        tlast_ve = {'ve': self.now('ve')}
        tlast_st = {'wst': 0}
        first_hl = [('sst0', self.now('sst0')), ('sst1', self.now('sst1'))]
        for wnm, base in [('wfr', 0), ('wfi', 2)]:
            for kt in range(8):
                tld = self.dma('gp', sb['wprep'][:, :], d[wnm][kt * 128:(kt + 1) * 128, :],
                               waits=[('ve', tlast_ve['ve'])], sem='wld')
                self.ve(lambda nc, e: nc.vector.tensor_copy(sb['wpo'][:, 0:1024], sb['wprep'][:, :]),
                        waits=[('wld', tld['wld']), ('wst', tlast_st['wst'])] + first_hl)
                first_hl = []
                tlast_ve = self.ve(lambda nc, e: nc.vector.tensor_sub(
                    sb['wpo'][:, 1024:2048], sb['wprep'][:, :], sb['wpo'][:, 0:1024]),
                    inc=[('ve', 1)])
                self.dma('sy', self.s_w[base, kt].transpose([1, 0, 2]),
                         sb['wpo'][:, 0:1024].rearrange('p (ot c) -> p ot c', c=128),
                         waits=[('ve', tlast_ve['ve'])], sem='wst')
                tlast_st = self.dma('sy', self.s_w[base + 1, kt].transpose([1, 0, 2]),
                                    sb['wpo'][:, 1024:2048].rearrange('p (ot c) -> p ot c', c=128),
                                    waits=[('ve', tlast_ve['ve'])], sem='wst')
        for src, tgt in [('bm', 'bmr'), ('fbias', 'fbr')]:
            tld = self.dma('gp', sb['wprep'][0:1, :], d[src].rearrange('(a n) -> a n', a=1),
                           waits=[('ve', self.now('ve')), ('wld', self.now('wld')),
                                  ('bld', self.now('bld'))], sem='bld')
            self.ve(lambda nc, e, tgt=tgt: nc.vector.tensor_copy(
                sb[tgt][0:1, :], sb['wprep'][0:1, :]), waits=[('bld', tld['bld'])])
            tlo = self.ve(lambda nc, e, tgt=tgt: nc.vector.tensor_sub(
                sb['t1'][0:1, 0:1024], sb['wprep'][0:1, :], sb[tgt][0:1, :]), inc=[('ve', 1)])
            self.dma('gp', sb[tgt][1:2, :], sb['t1'][0:1, 0:1024],
                     waits=[('ve', tlo['ve'])], sem='bld')

    def _ph0_evac(self, pd, dst_h, dst_l, first=(), tiled=False):
        sb = self.sb
        self.ve(lambda nc, e, pd=pd: nc.vector.tensor_copy(sb['wpo'][:, 0:512], pd),
                waits=[('te', self.now('te')), ('pev', self.now('pev'))] + list(first))
        tev = self.ve(lambda nc, e, pd=pd: nc.vector.tensor_sub(
            sb['wpo'][:, 512:1024], pd, sb['wpo'][:, 0:512]), inc=[('ve', 1)])
        src_h, src_l = sb['wpo'][:, 0:512], sb['wpo'][:, 512:1024]
        if tiled:
            src_h = src_h.rearrange('p (mt c) -> p mt c', c=128)
            src_l = src_l.rearrange('p (mt c) -> p mt c', c=128)
            dst_h = dst_h.transpose([1, 0, 2])
            dst_l = dst_l.transpose([1, 0, 2])
        self.dma('sy', dst_h, src_h, waits=[('ve', tev['ve'])], sem='pev')
        self.dma('sy', dst_l, src_l, waits=[('ve', tev['ve'])], sem='pev')
        return tev

    def _phase0(self):
        self.emit('sy', None, waits=[('dsy', self.now('dsy')), ('sst0', self.now('sst0')),
                                     ('sst1', self.now('sst1')), ('wst', self.now('wst'))])
        d, sb = self.d, self.sb
        CONST1 = ['cFr_h', 'cFr_l', 'cGi_h', 'cGi_l', 'cFi_h', 'cFi_l']
        terms_r = [(0, 0), (1, 0), (0, 1), (2, 2), (3, 2), (2, 3)]
        terms_i = [(2, 0), (3, 0), (2, 1), (0, 4), (1, 4), (0, 5)]
        lhs_tok = [{'te': 0}, {'te': 0}, {'te': 0}, {'te': 0}]
        wst_tot = self.now('wst')
        first_ev = [('sst0', self.now('sst0')), ('sst1', self.now('sst1')),
                    ('wst', self.now('wst'))]

        def rhs_cache(q):
            col = q * 512
            if col < 16384:
                return sb['b_wo'][:, col:col + 512]
            return sb['zsil'][:, col - 16384:col - 16384 + 512]

        def load_cache(tiles):
            gen = [('te', self.now('te'))]
            for q, (cn, r0, c0) in enumerate(tiles):
                self.dma('sy', rhs_cache(q), d[cn][r0:r0 + 128, c0:c0 + 512],
                         waits=gen if (q == 0 or self.fpo) else (), sem='pc')
            return self.now('pc')

        for mh in range(2):
            pc_tok = load_cache([(cn, kt * 128, mh * 512)
                                 for kt in range(8) for cn in CONST1])
            for ot in range(8):
                i_r = self.next_slot()
                i_i = self.next_slot()
                pdr, pdi = self.slots[i_r], self.slots[i_i]
                for kt in range(8):
                    kp = kt % 4
                    t_lhs = None
                    for wi_i in range(4):
                        t_lhs = self.dma('sy', sb['lstr'][:, (kp * 4 + wi_i) * 128:(kp * 4 + wi_i + 1) * 128],
                                         self.s_w[wi_i, kt, ot],
                                         waits=([('te', lhs_tok[kp]['te'])] +
                                                ([('wst', wst_tot)] if self.fpo else []))
                                         if (wi_i == 0 or self.fpo) else (),
                                         sem=f'ph{kp}')
                    for ri, (pd, idx, terms) in enumerate(((pdr, i_r, terms_r), (pdi, i_i, terms_i))):
                        for ti, (wl, cc) in enumerate(terms):
                            w0 = []
                            if ri == 0 and ti == 0:
                                w0 = [(f'ph{kp}', t_lhs[f'ph{kp}']), ('pc', pc_tok)]
                                if kt == 0:
                                    w0 += [('ve', self.slot_free[i_r]['ve']), ('se', self.slot_free[i_r]['se']),
                                           ('ve', self.slot_free[i_i]['ve']), ('se', self.slot_free[i_i]['se'])]
                            last = (ri == 1 and ti == 5)
                            self.mm(pd, sb['lstr'][:, (kp * 4 + wl) * 128:(kp * 4 + wl + 1) * 128],
                                    rhs_cache(kt * 6 + cc),
                                    start=(kt == 0 and ti == 0), stop=(kt == 7 and ti == 5),
                                    waits=w0, inc=[('te', 1)] if last else ())
                    lhs_tok[kp] = {'te': self.now('te')}
                for ri, (pd, idx) in enumerate(((pdr, i_r), (pdi, i_i))):
                    tev = self._ph0_evac(
                        pd,
                        self.s_a[ri * 2, ot, mh * 4:(mh + 1) * 4],
                        self.s_a[ri * 2 + 1, ot, mh * 4:(mh + 1) * 4],
                        first=first_ev, tiled=True)
                    first_ev = []
                    self.slot_free[idx] = {'ve': tev['ve'], 'se': 0}
        self.emit('sy', None, waits=[('pev', self.now('pev')), ('ph0', self.now('ph0')),
                                     ('ph1', self.now('ph1')), ('ph2', self.now('ph2')),
                                     ('ph3', self.now('ph3'))])
        terms2 = [(0, 0), (1, 0), (0, 1), (2, 2), (3, 2), (2, 3)]
        pev_s1 = self.now('pev')
        for oh in range(2):
            pc_tok = load_cache([(cn, ot * 128, oh * 512)
                                 for ot in range(8)
                                 for cn in ('cFrd_h', 'cFrd_l', 'cFid_h', 'cFid_l')])
            for mtl in range(8):
                idx = self.next_slot()
                pd = self.slots[idx]
                for ot in range(8):
                    kp = ot % 4
                    t_lhs = None
                    for ai in range(4):
                        t_lhs = self.dma('sy', sb['lstr'][:, (kp * 4 + ai) * 128:(kp * 4 + ai + 1) * 128],
                                         self.s_a[ai, ot, mtl],
                                         waits=([('te', lhs_tok[kp]['te'])] +
                                                ([('pev', pev_s1)] if self.fpo else []))
                                         if (ai == 0 or self.fpo) else (),
                                         sem=f'ph{kp}')
                    for ti, (al, cc) in enumerate(terms2):
                        w0 = []
                        if ti == 0:
                            w0 = [(f'ph{kp}', t_lhs[f'ph{kp}']), ('pc', pc_tok)]
                            if ot == 0:
                                w0 += [('ve', self.slot_free[idx]['ve']), ('se', self.slot_free[idx]['se'])]
                        self.mm(pd, sb['lstr'][:, (kp * 4 + al) * 128:(kp * 4 + al + 1) * 128],
                                rhs_cache(ot * 4 + cc),
                                start=(ot == 0 and ti == 0), stop=(ot == 7 and ti == 5),
                                waits=w0, inc=[('te', 1)] if ti == 5 else ())
                    lhs_tok[kp] = {'te': self.now('te')}
                tev = self._ph0_evac(
                    pd,
                    self.s_m[0, mtl, oh],
                    self.s_m[1, mtl, oh])
                self.slot_free[idx] = {'ve': tev['ve'], 'se': 0}

    def _dir(self, is_b):
        d, sb, tk = self.d, self.sb, self.tk
        self.emit('sy', None, waits=[('dsy', self.now('dsy')), ('pev', self.now('pev')),
                                     ('yf0', self.now('yf0')), ('yf1', self.now('yf1'))])
        dirw = 'b' if is_b else 'f'
        gd = tk['group_done']
        wo_gate = self.now('te')
        t_wo = None
        for kt in range(16):
            t_wo = self.dma('gp', sb['b_wo'][:, kt * 1024:(kt + 1) * 1024],
                            d[f'wo_{dirw}'][kt * 128:(kt + 1) * 128, :],
                            waits=[('te', wo_gate)] if (kt == 0 or self.fpo) else (), sem='wo')

        gdw = [('te', gd['te']), ('ve', gd['ve']), ('se', gd['se'])]
        self.dma('gp', sb['cw'][:, :].rearrange('p (tau ct) -> p tau ct', ct=18),
                 d[f'convw_{dirw}'].rearrange('tau (ct p) -> p tau ct', p=128), slow=True,
                 waits=gdw, sem='dp')
        gdw2 = gdw if self.fpo else []
        self.dma('gp', sb['cb'][:, :], d[f'convb_{dirw}'].rearrange('(ct p) -> p ct', p=128),
                 slow=True, waits=gdw2, sem='dp')
        t_al = self.dma('gp', sb['aexp'][:, :], d[f'alog_{dirw}'].rearrange('(a b) -> a b', b=1),
                        slow=True, waits=gdw2, sem='dp')
        self.dma('gp', sb['dtbc'][:, :], d[f'dtb_{dirw}'].rearrange('(a b) -> a b', b=1),
                 slow=True, waits=gdw2, sem='dp')
        t_dd = self.dma('gp', sb['ddiag'][:, :], d[f'ddg_{dirw}'],
                        waits=[('te', gd['te']), ('ve', gd['ve'])], sem='dd')
        tse = self.se(lambda nc, e: nc.scalar.activation(sb['negA'][:, :], sb['aexp'][:, :], AF.Exp),
                      waits=[('dp', self.now('dp'))], inc=[('se', 1)])
        t_na = self.ve(lambda nc, e: nc.vector.tensor_scalar_mul(sb['negA'][:, :], sb['negA'][:, :], -1.0),
                       waits=[('se', tse['se'])], inc=[('ve', 1)])
        t_h0 = self.ve(lambda nc, e: nc.vector.memset(sb['hst'][:, :], 0.0),
                       waits=[('te', tk['group_done']['te'])], inc=[('ve', 1)])
        self.ve(lambda nc, e: nc.vector.memset(sb['cstash'][:, :], 0.0), inc=[('ve', 1)])
        t_dirprep = {'dp': self.now('dp'), 've': self.now('ve')}

        tk_hup = t_h0
        tk_zsil_free = {'te': tk['group_done']['te'], 've': 0}
        tk_xbcs_free = {'te': tk['group_done']['te'], 'gp': tk['group_done']['gp']}
        tk_dt_free = {'ve': 0, 'co': 0}
        tk_xnT_free = {'te': tk['group_done']['te']}
        tk_cr_free = {'te': 0}
        tk_wv_users = {'te': 0}
        tk_mt_gp = {'gp': 0}
        tk_crc_users = {'te': 0}
        t_ddg = t_dd

        groups = list(range(NG - 1, -1, -1)) if is_b else list(range(NG))
        for g in groups:
            chunks = list(range(NCH - 1, -1, -1)) if is_b else list(range(NCH))
            tk_ln_sq = {'ve': getattr(self, '_conv_last', 0)}
            tk_ln_tr = {'te': 0}
            for c in chunks:
                gc = g * NCH + c
                par = gc % 2
                xqs = sb['xq'][:, par * 1024:(par + 1) * 1024]
                tld = self.dma('sy', xqs, d['x'][gc * 128:(gc + 1) * 128, :],
                               waits=[('ve', tk['xq_free'][par]['ve']),
                                      ('dp', t_dirprep['dp'])], sem=f'xq{par}')
                tsqr = self.se(lambda nc, e, xqs=xqs: nc.scalar.activation(
                    sb['wprep'][:, :], xqs, AF.Square),
                    waits=[(f'xq{par}', tld[f'xq{par}']), ('ve', tk_ln_sq['ve'])],
                    inc=[('se', 1)])
                def vc(fn, waits=()):
                    t = self.ve(fn, waits=list(waits) + [('ve', self.now('ve'))], inc=[('ve', 1)])
                    return t
                vc(lambda nc, e, xqs=xqs: nc.vector.reduce_sum(
                    sb['mean'][:, :], xqs, axis=mybir.AxisListType.X),
                    waits=[(f'xq{par}', tld[f'xq{par}'])])
                vc(lambda nc, e: nc.vector.tensor_scalar_mul(sb['mean'][:, :], sb['mean'][:, :], 1.0 / DM))
                tk_ln_sq = vc(lambda nc, e: nc.vector.reduce_sum(
                    sb['sqs'][:, :], sb['wprep'][:, :], axis=mybir.AxisListType.X),
                    waits=[('se', tsqr['se'])])
                vc(lambda nc, e: nc.vector.tensor_scalar_mul(sb['sqs'][:, :], sb['sqs'][:, :], 1.0 / DM))
                vc(lambda nc, e: nc.vector.tensor_mul(sb['rstd'][:, :], sb['mean'][:, :], sb['mean'][:, :]))
                tvar = vc(lambda nc, e: nc.vector.tensor_sub(sb['sqs'][:, :], sb['sqs'][:, :], sb['rstd'][:, :]))
                tsq = self.se(lambda nc, e: nc.scalar.activation(sb['rstd'][:, :], sb['sqs'][:, :], AF.Sqrt,
                                                                 bias=sb['epsc'][:, :]),
                              waits=[('ve', tvar['ve'])], inc=[('se', 1)])
                vc(lambda nc, e: nc.vector.reciprocal(sb['rstd'][:, :], sb['rstd'][:, :]),
                   waits=[('se', tsq['se'])])
                vc(lambda nc, e: nc.vector.tensor_mul(sb['nmrs'][:, :], sb['mean'][:, :], sb['rstd'][:, :]))
                vc(lambda nc, e: nc.vector.tensor_scalar_mul(sb['nmrs'][:, :], sb['nmrs'][:, :], -1.0))
                tnx = vc(lambda nc, e, xqs=xqs: nc.vector.tensor_scalar(
                    sb['xnch'][:, :], xqs, sb['rstd'][:, :], sb['nmrs'][:, :], ALU.mult, ALU.add),
                    waits=[('te', tk_ln_tr['te'])])
                if is_b:
                    vc(lambda nc, e, xqs=xqs: nc.vector.scalar_tensor_tensor(
                        sb['wprep'][:, :], xqs, sb['rstd'][:, :], sb['xnch'][:, :],
                        ALU.mult, ALU.subtract))
                    tnx = vc(lambda nc, e: nc.vector.tensor_scalar(
                        sb['xncl'][:, :], sb['wprep'][:, :], sb['nmrs'][:, :], None, ALU.add))
                self.tk['xq_free'][par] = tnx
                tk_ln_sq = {'ve': self.now('ve')}
                passes = [('xnch', 'xnT')] + ([('xncl', 'xnTl')] if is_b else [])
                for srcb, dstb in passes:
                    for kt in range(8):
                        self.trans(self.ps['pT'][:, kt * 128:(kt + 1) * 128],
                                   sb[srcb][:, kt * 128:(kt + 1) * 128],
                                   waits=([('ve', tnx['ve']), ('ve', tk['pT_free']['ve']),
                                           ('te', tk_xnT_free['te'])] if kt == 0 else ()),
                                   inc=[('te', 1)] if kt == 7 else ())
                    tkte = self.now('te')
                    tk_ln_tr = {'te': tkte}
                    dst = sb[dstb][:, :].rearrange('p (kt t) -> p kt t', t=512)[:, :, c * 128:(c + 1) * 128]
                    tk['pT_free'] = self.ve(lambda nc, e, dst=dst: nc.vector.tensor_copy(
                        dst, self.ps['pT'][:, :].rearrange('p (kt t) -> p kt t', t=128)),
                        waits=[('te', tkte)], inc=[('ve', 1)])
            tk_ln = {'ve': self.now('ve')}

            jorder = [34] + list(range(16, 34)) + list(range(16))
            t_wld = {}
            def load_wi(ii):
                j = jorder[ii]
                jw = PROJ - (NJ - 1) * 128 if j == NJ - 1 else 128
                bpar = ii % 4
                t_wld[j] = self.dma('sy', sb['b_wi'][:, bpar * 1024: bpar * 1024 + 8 * jw]
                                    .rearrange('p (kt jj) -> p kt jj', jj=jw),
                                    self.s_wi[dirw][j, :, :, 0:jw],
                                    waits=[('te', tk['wi_free'][bpar]['te'])] +
                                    (self._sst if self.fpo else []),
                                    sem=f'wi{bpar}')
            load_wi(0); load_wi(1); load_wi(2); load_wi(3)
            par_evac = [dict(self.slot_free[0]), dict(self.slot_free[1])]
            tk_conv_done = {}
            for ii, j in enumerate(jorder):
                jw = PROJ - (NJ - 1) * 128 if j == NJ - 1 else 128
                ppar = ii % 2
                bsl = ii % 4
                pd = self.slots[ppar] if jw == 128 else self.ps['pS%d' % ppar][0:32, :]
                for kt in range(8):
                    w0 = []
                    if kt == 0:
                        w0 = [(f'wi{bsl}', t_wld[j][f'wi{bsl}']), ('ve', tk_ln['ve']),
                              ('se', par_evac[ppar]['se']), ('ve', par_evac[ppar]['ve'])]
                    self.mm(pd, sb['b_wi'][:, bsl * 1024 + kt * jw: bsl * 1024 + (kt + 1) * jw],
                            sb['xnT'][:, kt * 512:(kt + 1) * 512],
                            start=(kt == 0), stop=(kt == 7),
                            waits=w0, inc=[('te', 1)] if kt == 7 else ())
                tk['wi_free'][bsl] = {'te': self.now('te')}
                if ii + 4 < NJ:
                    load_wi(ii + 4)
                tkte = self.now('te')
                if j < 16:
                    tz = self.se(lambda nc, e, j=j, ppar=ppar: nc.scalar.activation(
                        sb['zsil'][:, j * 512:(j + 1) * 512], self.slots[ppar], AF.Silu),
                        waits=[('te', tkte), ('te', tk_zsil_free['te']),
                               ('ve', tk_zsil_free['ve'])], inc=[('se', 1)])
                    par_evac[ppar] = {'se': tz['se'], 've': 0}
                elif j < 34:
                    ct = j - 16
                    cpar = ct % 2
                    if not is_b:
                        data_off = cpar * 520 + 3; stash_off = cpar * 520
                        save_off = cpar * 520 + 512; tap = [0, 1, 2, 3]
                    else:
                        data_off = cpar * 520; stash_off = cpar * 520 + 512
                        save_off = cpar * 520; tap = [3, 2, 1, 0]
                    tx = self.se(lambda nc, e, ppar=ppar, o=data_off: nc.scalar.activation(
                        sb['tmpc'][:, o:o + 512], self.slots[ppar], AF.Copy),
                        waits=[('te', tkte),
                               ('ve', tk_conv_done.get(ct - 2, {'ve': 0})['ve'])],
                        inc=[('se', 1)])
                    par_evac[ppar] = {'se': tx['se'], 've': 0}
                    self.ve(lambda nc, e, ct=ct, so=stash_off: nc.vector.tensor_copy(
                        sb['tmpc'][:, so:so + 3], sb['cstash'][:, ct * 3:ct * 3 + 3]),
                        waits=[('se', tx['se'])])
                    base = cpar * 520
                    self.ve(lambda nc, e, b=base, ct=ct, t0=tap[0], cpar=cpar: nc.vector.tensor_scalar(
                        sb['convt'][:, cpar * 512:(cpar + 1) * 512], sb['tmpc'][:, b + t0: b + t0 + 512],
                        sb['cw'][:, ct:ct + 1], None, ALU.mult))
                    for tau in (1, 2, 3):
                        self.ve(lambda nc, e, b=base, ct=ct, tau=tau, to=tap[tau], cpar=cpar:
                                nc.vector.scalar_tensor_tensor(
                                    sb['convt'][:, cpar * 512:(cpar + 1) * 512],
                                    sb['tmpc'][:, b + to: b + to + 512],
                                    sb['cw'][:, tau * 18 + ct: tau * 18 + ct + 1],
                                    sb['convt'][:, cpar * 512:(cpar + 1) * 512], ALU.mult, ALU.add))
                    tcv = self.ve(lambda nc, e, ct=ct, so=save_off: nc.vector.tensor_copy(
                        sb['cstash'][:, ct * 3:ct * 3 + 3], sb['tmpc'][:, so:so + 3]),
                        inc=[('ve', 1)])
                    tk_conv_done[ct] = tcv
                    self.se(lambda nc, e, ct=ct, cpar=cpar: nc.scalar.activation(
                        sb['xbcs'][:, ct * 512:(ct + 1) * 512], sb['convt'][:, cpar * 512:(cpar + 1) * 512],
                        AF.Silu, bias=sb['cb'][:, ct:ct + 1]),
                        waits=[('ve', tcv['ve']), ('te', tk_xbcs_free['te']),
                               ('gp', tk_xbcs_free['gp'])], inc=[('se', 1)])
                else:
                    self.se(lambda nc, e, ppar=ppar: nc.scalar.activation(
                        sb['lgda'][:, :], self.ps['pS%d' % ppar][0:32, :], AF.Abs,
                        bias=sb['dtbc'][:, :]),
                        waits=[('te', tkte), ('ve', tk_dt_free['ve']),
                               ('co', tk_dt_free['co']), ('dp', t_dirprep['dp'])],
                        inc=[('se', 1)])
                    self.se(lambda nc, e: nc.scalar.activation(
                        sb['cum'][:, :], sb['lgda'][:, :], AF.Exp, scale=-1.0))
                    self.se(lambda nc, e: nc.scalar.activation(
                        sb['cum'][:, :], sb['cum'][:, :], AF.Ln, bias=sb['onesf'][0:32, 0:1]))
                    tdt = self.se(lambda nc, e, ppar=ppar: nc.scalar.activation(
                        sb['dt_'][:, :], self.ps['pS%d' % ppar][0:32, :], AF.Relu,
                        bias=sb['dtbc'][:, :]), inc=[('se', 1)])
                    par_evac[ppar] = {'se': tdt['se'], 've': 0}
                    self._t_dtadd = self.ve(lambda nc, e: nc.vector.tensor_add(
                        sb['dt_'][:, :], sb['dt_'][:, :], sb['cum'][:, :]),
                        waits=[('se', tdt['se'])], inc=[('ve', 1)])
                if j == 33:
                    conv_se = self.now('se')
            self._conv_last = tk_conv_done[17]['ve']
            tk_sweep = {'se': self.now('se'), 've': self.now('ve'), 'te': self.now('te')}
            self.slot_free[0] = {'ve': 0, 'se': tk_sweep['se']}
            self.slot_free[1] = {'ve': 0, 'se': tk_sweep['se']}

            tlgda = self.ve(lambda nc, e: nc.vector.tensor_scalar(
                sb['lgda'][:, :], sb['dt_'][:, :], sb['negA'][:, :], None, ALU.mult),
                waits=[('ve', self._t_dtadd['ve']), ('ve', t_na['ve']),
                       ('se', getattr(self, '_dtp_se', 0))], inc=[('ve', 1)])
            tscan = tlgda
            for c in range(NCH):
                tscan = self.ve(lambda nc, e, c=c: nc.vector.tensor_tensor_scan(
                    sb['cum'][:, c * 128:(c + 1) * 128], sb['lgda'][:, c * 128:(c + 1) * 128],
                    sb['lgda'][:, c * 128:(c + 1) * 128], 0.0, ALU.add, ALU.bypass),
                    waits=[('ve', tscan['ve'])], inc=[('ve', 1)])
            if is_b:
                self.ve(lambda nc, e: nc.vector.tensor_sub(sb['cump'][:, :], sb['cum'][:, :], sb['lgda'][:, :]),
                        waits=[('ve', tscan['ve'])])
            tlg = self.se(lambda nc, e: nc.scalar.activation(sb['lgdt'][:, :], sb['dt_'][:, :], AF.Ln),
                          waits=[('ve', self._t_dtadd['ve'])], inc=[('se', 1)])
            CC = sb['cump'] if is_b else sb['cum']
            self.ve(lambda nc, e, CC=CC: nc.vector.tensor_copy(sb['cumhi'][:, :], CC[:, :]),
                    waits=[('se', tlg['se']), ('pk', self.now('pk')), ('sc', self.now('sc')),
                           ('ve', tscan['ve'])])
            self.ve(lambda nc, e, CC=CC: nc.vector.tensor_tensor(
                sb['cumlo'][:, :], CC[:, :], sb['cumhi'][:, :], ALU.subtract))
            self.ve(lambda nc, e: nc.vector.tensor_copy(sb['ldthi'][:, :], sb['lgdt'][:, :]))
            self.ve(lambda nc, e: nc.vector.tensor_tensor(
                sb['ldtlo'][:, :], sb['lgdt'][:, :], sb['ldthi'][:, :], ALU.subtract))
            if is_b:
                self.ve(lambda nc, e: nc.vector.tensor_add(sb['wvin'][:, :], sb['lgdt'][:, :], sb['cump'][:, :]),
                        waits=[('se', tlg['se'])], inc=[('ve', 1)])
            else:
                self.ve(lambda nc, e: nc.vector.tensor_sub(sb['wvin'][:, :], sb['lgdt'][:, :], sb['cum'][:, :]),
                        waits=[('se', tlg['se'])], inc=[('ve', 1)])
            for c in range(NCH):
                kw = dict(bias=sb['cum'][:, c * 128 + 127:c * 128 + 128]) if not is_b else {}
                self.se(lambda nc, e, c=c, kw=kw: nc.scalar.activation(
                    sb['wv'][:, c * 128:(c + 1) * 128], sb['wvin'][:, c * 128:(c + 1) * 128],
                    AF.Exp, **kw),
                    waits=[('ve', self.now('ve')), ('te', tk_wv_users['te'])], inc=[('se', 1)])
            tpq = self.se(lambda nc, e: nc.scalar.activation(
                sb['pqb'][:, :], sb['cum'][:, 127:512:128], AF.Exp),
                waits=[('ve', tscan['ve']), ('sc', self.now('sc'))], inc=[('se', 1)])
            if is_b:
                for c in range(NCH):
                    def vc2(fn, waits=()):
                        return self.ve(fn, waits=list(waits) + [('ve', self.now('ve'))],
                                       inc=[('ve', 1)])
                    vc2(lambda nc, e, c=c: nc.vector.tensor_copy(
                        sb['clsc'][:, 0:1], sb['cum'][:, c * 128 + 127:c * 128 + 128]),
                        waits=[('se', tpq['se'])])
                    vc2(lambda nc, e, c=c: nc.vector.tensor_scalar(
                        sb['clh'][:, c * 128:(c + 1) * 128], sb['onesf'][:, 0:128],
                        sb['clsc'][:, 0:1], None, ALU.mult))
                    vc2(lambda nc, e, c=c: nc.vector.tensor_tensor(
                        sb['clsc'][:, 1:2], sb['clsc'][:, 0:1],
                        sb['clh'][:, c * 128:c * 128 + 1], ALU.subtract))
                    vc2(lambda nc, e, c=c: nc.vector.tensor_scalar(
                        sb['cll'][:, c * 128:(c + 1) * 128], sb['onesf'][:, 0:128],
                        sb['clsc'][:, 1:2], None, ALU.mult))
            tk_dtp = {'ve': self.now('ve'), 'se': self.now('se')}
            self._dtp_se = tk_dtp['se']
            wcr = [('ve', tk_dtp['ve']), ('se', tk_dtp['se']), ('te', tk_cr_free['te'])]
            if self.fpo:
                wcr = wcr + [('em', self.now('em'))]
            for qq, srcb in enumerate(('cumhi', 'cumlo', 'ldthi', 'ldtlo')):
                self.dma('gp', sb['pk32'][qq:128:4, :], sb[srcb][:, :],
                         waits=wcr if (qq == 0 or self.fpo) else (), sem='pk')
            tk_pk = {'pk': self.now('pk')}
            crstores = [(0, 'cumhi'), (1, 'cumlo')] + ([(2, 'clh'), (3, 'cll')] if is_b else [])
            wsc = [('ve', tk_dtp['ve']), ('se', tk_dtp['se']),
                   ('cr', self.now('cr')), ('co', self.now('co'))] if self.fpo else []
            for row, srcb in crstores:
                self.dma('gp', self.s_cr[row, :, :], sb[srcb][:, :], waits=wsc, sem='sc')
            self.dma('gp', self.s_cr[4, :, 0:4], sb['pqb'][:, :], waits=wsc, sem='sc')
            tk_crst = {'sc': self.now('sc')}
            t_coll = self.dma('gp', sb['pqrow'][0:1, :].rearrange('a (p c) -> a p c', c=4),
                              self.s_cr[4, :, 0:4].unsqueeze(0),
                              waits=[('sc', tk_crst['sc'])] +
                              ([('te', tk_cr_free['te'])] if self.fpo else []), sem='co')
            tk_coll = {'co': self.now('co')}
            tk_dt_free = {'ve': tk_dtp['ve'], 'co': tk_coll['co']}

            for ci, c in enumerate(chunks):
                gc = g * NCH + c
                par = gc % 2
                crpairs = [(0, 'cumhi'), (1, 'cumlo')] + ([(2, 'clh'), (3, 'cll')] if is_b else [])
                for row, srcb in crpairs:
                    self.dma('gp', sb['crc'][row:row + 1, :].rearrange('a (h t) -> a h t', t=128),
                             self.s_cr[row, :, c * 128:(c + 1) * 128].unsqueeze(0),
                             waits=[('te', tk_crc_users['te']), ('sc', tk_crst['sc'])]
                             if (row == 0 or self.fpo) else (),
                             sem='cr')
                tk_coll_c = {'cr': self.now('cr')}
                for bb8 in range(8):
                    self.dma('gp', sb['emc'][:, bb8 * 128:(bb8 + 1) * 128],
                             sb['pk32'][bb8 * 16:(bb8 + 1) * 16, c * 128:(c + 1) * 128],
                             waits=[('te', tk_crc_users['te']), ('pk', tk_pk['pk'])]
                             if (bb8 == 0 or self.fpo) else (),
                             sem='em')
                tk_em = {'em': self.now('em')}
                Btile = sb['xbcs'][:, 16 * 512 + c * 128: 16 * 512 + c * 128 + 128]
                Ctile = sb['xbcs'][:, 17 * 512 + c * 128: 17 * 512 + c * 128 + 128]
                for half in range(2):
                    for k8 in range(8):
                        cti = half * 8 + k8
                        self.trans(self.ps['pT'][:, k8 * 128:(k8 + 1) * 128],
                                   sb['xbcs'][:, cti * 512 + c * 128: cti * 512 + c * 128 + 128],
                                   waits=([('se', conv_se), ('ve', tk['pT_free']['ve']),
                                           ('gp', self.now('gp'))] if k8 == 0 else ()),
                                   inc=[('te', 1)] if k8 == 7 else ())
                    tkte = self.now('te')
                    tk['pT_free'] = self.ve(lambda nc, e, half=half: nc.vector.tensor_copy(
                        sb['xt'][:, half * 1024:(half + 1) * 1024], self.ps['pT'][:, :]),
                        waits=[('te', tkte)], inc=[('ve', 1)])
                self.trans(self.ps['pT'][:, 0:128], Btile,
                           waits=[('ve', tk['pT_free']['ve'])], inc=[('te', 1)])
                self.trans(self.ps['pT'][:, 128:160], sb['wv'][0:32, c * 128:(c + 1) * 128],
                           waits=[('se', tk_dtp['se'])], inc=[('te', 1)])
                tkte = self.now('te')
                tk_wv_users = {'te': tkte}
                self.ve(lambda nc, e: nc.vector.tensor_copy(sb['Bt'][:, :], self.ps['pT'][:, 0:128]),
                        waits=[('te', tkte)])
                tk['pT_free'] = self.ve(lambda nc, e: nc.vector.tensor_copy(
                    sb['wct'][:, :], self.ps['pT'][:, 128:160]), inc=[('ve', 1)])
                t_xt = tk['pT_free']
                isl = self.next_slot()
                gsl = self.slots[isl]
                self.mm(gsl[:, 0:128], Btile, Ctile, start=True, stop=True,
                        waits=[('ve', self.slot_free[isl]['ve']), ('se', self.slot_free[isl]['se']),
                               ('se', conv_se)], inc=[('te', 1)])
                tgm = self.ve(lambda nc, e, gsl=gsl, m=('triB' if is_b else 'triF'):
                              nc.vector.tensor_tensor(sb['gmask'][:, :], gsl[:, 0:128],
                                                      sb[m][:, :], ALU.mult),
                              waits=[('te', self.now('te'))], inc=[('ve', 1)])
                self.slot_free[isl] = {'ve': tgm['ve'], 'se': 0}
                grep = sb['gmask'][:, :].unsqueeze(1).broadcast_to([128, 16, 128])
                crep = Ctile.unsqueeze(1).broadcast_to([128, 16, 128])
                se_exp_tok = {'se': 0}
                tok_mt, tok_mv = [], []
                for half in range(2):
                    mtb = sb['mt' if half == 0 else 'mt2']
                    lvb = sb['lv' if half == 0 else 'lv2']
                    lv_tok = None
                    for bb in range(4):
                        b = half * 4 + bb
                        i_e = self.next_slot()
                        pe = self.slots[i_e]
                        crs = sb['crc'][0:2, b * 512:(b + 1) * 512]
                        self.mm(pe, sb['ones2n' if is_b else 'ones2'][:, :], crs,
                                start=True, stop=False,
                                waits=[('cr', tk_coll_c['cr']), ('em', tk_em['em']),
                                       ('ve', self.slot_free[i_e]['ve']),
                                       ('se', self.slot_free[i_e]['se'])])
                        self.mm(pe, sb['emc'][0:16, b * 128: b * 128 + 128],
                                sb['indB' if is_b else 'indF'][:, :], start=False, stop=True,
                                inc=[('te', 1)])
                        tke = self.now('te')
                        i_v = self.next_slot()
                        pv = self.slots[i_v]
                        if is_b:
                            self.mm(pv, sb['evB'][:, :], sb['crc'][0:4, b * 512:(b + 1) * 512],
                                    start=True, stop=True,
                                    waits=[('ve', self.slot_free[i_v]['ve']),
                                           ('se', self.slot_free[i_v]['se'])], inc=[('te', 1)])
                        else:
                            self.mm(pv, sb['ones2'][:, :], crs, start=True, stop=True,
                                    waits=[('ve', self.slot_free[i_v]['ve']),
                                           ('se', self.slot_free[i_v]['se'])], inc=[('te', 1)])
                        tkv = self.now('te')
                        e_tok = self.ve(lambda nc, e, pe=pe: nc.vector.tensor_scalar_min(
                            pe, pe, 30.0),
                            waits=[('te', tke)], inc=[('ve', 1)])
                        if self.debug and is_b and gc == 30 and half == 0 and bb == 0:
                            self.dbg_dump([('decl', pe)])
                        w_exp = [('ve', e_tok['ve'])]
                        if bb == 0:
                            w_exp.append(('te', tk['ymm_free']['te']))
                        se_exp_tok = self.se(lambda nc, e, bb=bb, mtb=mtb, pe=pe: nc.scalar.activation(
                            mtb[:, bb * 512:(bb + 1) * 512],
                            pe, AF.Exp),
                            waits=w_exp, inc=[('se', 1)])
                        self.slot_free[i_e] = {'ve': 0, 'se': se_exp_tok['se']}
                        lv_tok = self.se(lambda nc, e, pv=pv, bb=bb, lvb=lvb: nc.scalar.activation(
                            lvb[:, bb * 512:(bb + 1) * 512], pv, AF.Exp),
                            waits=[('te', tkv)], inc=[('se', 1)])
                        self.slot_free[i_v] = {'ve': 0, 'se': lv_tok['se']}
                    self.ve(lambda nc, e, grep=grep, mtb=mtb: nc.vector.tensor_tensor(
                        mtb[:, :].rearrange('p (h t) -> p h t', t=128),
                        mtb[:, :].rearrange('p (h t) -> p h t', t=128), grep, ALU.mult),
                        waits=[('se', se_exp_tok['se']), ('ve', tgm['ve'])])
                    tok_mt.append(self.ve(lambda nc, e, half=half, mtb=mtb: nc.vector.tensor_tensor(
                        mtb[:, :], mtb[:, :],
                        sb['ddiag'][:, half * 2048:(half + 1) * 2048], ALU.add),
                        waits=[('dd', t_ddg['dd'])], inc=[('ve', 1)]))
                    tok_mv.append(self.ve(lambda nc, e, crep=crep, lvb=lvb: nc.vector.tensor_tensor(
                        lvb[:, :].rearrange('p (h t) -> p h t', t=128),
                        lvb[:, :].rearrange('p (h t) -> p h t', t=128), crep, ALU.mult),
                        waits=[('se', lv_tok['se']), ('se', conv_se)], inc=[('ve', 1)]))
                tk_crc_users = {'te': self.now('te')}
                if self.debug and is_b and gc == 30:
                    self.dbg_dump([('dlexp', sb['mt2'][:, :]), ('dlv', sb['lv'][:, :]),
                                   ('dmt', sb['mt'][:, :]),
                                   ('dhprev', sb['hst'][:, :])])
                for half in range(2):
                    mtb = sb['mt' if half == 0 else 'mt2']
                    lvb = sb['lv' if half == 0 else 'lv2']
                    tmt, tmv = tok_mt[half], tok_mv[half]
                    for jj in range(8):
                        j = half * 8 + jj
                        h0, h1 = 2 * j, 2 * j + 1
                        l0, l1 = 2 * jj, 2 * jj + 1
                        slot = self.ps['pY'][:, jj * 128:(jj + 1) * 128]
                        w0 = [('ve', tmt['ve']), ('ve', tmv['ve']), ('ve', tk['pY_free']['ve']),
                              ('ve', tk_hup['ve']), ('ve', t_xt['ve'])] if jj == 0 else ()
                        self.mm(slot[0:64, :], sb['xt'][:, h0 * 64:(h0 + 1) * 64],
                                mtb[:, l0 * 128:(l0 + 1) * 128], True, False, waits=w0, tp=(0, 0))
                        self.mm(slot[0:64, :], sb['hst'][:, h0 * 64:(h0 + 1) * 64],
                                lvb[:, l0 * 128:(l0 + 1) * 128], False, True, tp=(0, 0))
                        self.mm(slot[64:128, :], sb['xt'][:, h1 * 64:(h1 + 1) * 64],
                                mtb[:, l1 * 128:(l1 + 1) * 128], True, False, tp=(0, 64))
                        self.mm(slot[64:128, :], sb['hst'][:, h1 * 64:(h1 + 1) * 64],
                                lvb[:, l1 * 128:(l1 + 1) * 128], False, True, tp=(0, 64),
                                inc=[('te', 1)] if jj == 7 else ())
                    tk_ymm = {'te': self.now('te')}
                    tk['ymm_free'] = tk_ymm
                    tk['hbf_users'] = tk_ymm
                    zslc = sb['zsil'][:, :].rearrange('p (j t) -> p j t', t=512)[
                        :, half * 8:(half + 1) * 8, c * 128:(c + 1) * 128]
                    tk['pY_free'] = self.ve(lambda nc, e, zslc=zslc, par=par, half=half:
                                            nc.vector.tensor_tensor(
                        sb['yg'][:, par * 2048 + half * 1024: par * 2048 + (half + 1) * 1024]
                        .rearrange('p (j t) -> p j t', t=128),
                        self.ps['pY'][:, :].rearrange('p (j t) -> p j t', t=128),
                        zslc, ALU.mult),
                        waits=[('te', tk_ymm['te']), ('te', tk['yg_free'][par]['te']),
                               ('se', tk_sweep['se'])], inc=[('ve', 1)])
                tk_zsil_users = tk['pY_free']
                i_p = self.next_slot()
                psl = self.slots[i_p]
                self.mm(psl[:, 0:32], sb['ones2'][0:1, :], sb['pqrow'][0:1, :].rearrange('a (p c) -> a p c', c=4)[:, :, c], True, True,
                        waits=[('co', tk_coll['co']), ('ve', self.slot_free[i_p]['ve']),
                               ('se', self.slot_free[i_p]['se'])], inc=[('te', 1)])
                tpqb = self.ve(lambda nc, e, psl=psl: nc.vector.tensor_copy(
                    sb['pqbc'][:, :], psl[:, 0:32]),
                    waits=[('te', self.now('te'))], inc=[('ve', 1)])
                self.slot_free[i_p] = {'ve': tpqb['ve'], 'se': 0}
                tk_xw = self.ve(lambda nc, e: nc.vector.tensor_tensor(
                    sb['mt'][:, :].rearrange('p (h q) -> p h q', q=64),
                    sb['xt'][:, :].rearrange('p (h q) -> p h q', q=64),
                    sb['wct'][:, :].unsqueeze(2).broadcast_to([128, NH, 64]), ALU.mult),
                    waits=[('ve', t_xt['ve']), ('te', tk_ymm['te'])], inc=[('ve', 1)])
                tk_mt_gp = {'gp': 0}
                for qq in range(4):
                    i_h = self.next_slot()
                    hs = self.slots[i_h]
                    w0 = [('ve', self.slot_free[i_h]['ve']), ('se', self.slot_free[i_h]['se'])]
                    if qq == 0:
                        w0.append(('ve', tk_xw['ve']))
                    self.mm(hs, sb['Bt'][:, :], sb['mt'][:, qq * 512:(qq + 1) * 512], True, True,
                            waits=w0, inc=[('te', 1)])
                    tkh = self.now('te')
                    hsl = sb['hst'][:, qq * 512:(qq + 1) * 512]
                    self.ve(lambda nc, e, hsl=hsl, qq=qq: nc.vector.tensor_tensor(
                        hsl.rearrange('p (h q) -> p h q', q=64),
                        hsl.rearrange('p (h q) -> p h q', q=64),
                        sb['pqbc'][:, qq * 8:(qq + 1) * 8].unsqueeze(2).broadcast_to([128, 8, 64]),
                        ALU.mult),
                        waits=[('te', tkh), ('ve', tpqb['ve'])])
                    tup = self.ve(lambda nc, e, hsl=hsl, hs=hs: nc.vector.tensor_tensor(
                        hsl, hsl, hs, ALU.add), inc=[('ve', 1)])
                    self.slot_free[i_h] = {'ve': tup['ve'], 'se': 0}
                tk['ymm_free'] = {'te': self.now('te')}
                tk_hup = {'ve': self.now('ve')}
                for dh in range(2):
                    for kt in range(16):
                        w0 = [('ve', tk['pY_free']['ve']), ('wo', t_wo['wo'])] \
                            if (dh == 0 and kt == 0) else ()
                        self.mm(self.ps['pY'][:, dh * 512:(dh + 1) * 512],
                                sb['yg'][:, par * 2048 + kt * 128: par * 2048 + (kt + 1) * 128],
                                sb['b_wo'][:, kt * 1024 + dh * 512: kt * 1024 + dh * 512 + 512],
                                start=(kt == 0), stop=(kt == 15),
                                waits=w0, inc=[('te', 1)] if (kt == 15 and dh == 1) else ())
                tk_wo = {'te': self.now('te')}
                tk['yg_free'][par] = tk_wo
                if self.debug and is_b and gc == 30:
                    self.dbg_dump([('dxnT', sb['xnT'][:, :]), ('dzsil', sb['zsil'][:, :]),
                                   ('dxbcs', sb['xbcs'][:, :]), ('ddt', sb['dt_'][:, :]),
                                   ('dcum', sb['cum'][:, :]),
                                   ('dgm', sb['gmask'][:, :]),
                                   ('dyg', sb['yg'][:, 0:2048]),
                                   ('demega', sb['pk32'][:, :]), ('dcrc', sb['crc'][:, :]),
                                   ('dwv', sb['wv'][:, :]), ('dxt', sb['xt'][:, :]),
                                   ('dhst', sb['hst'][:, :])])
                if not is_b:
                    tev = self.ve(lambda nc, e, par=par: nc.vector.tensor_copy(
                        sb['yfc'][:, par * 1024:(par + 1) * 1024], self.ps['pY'][:, :]),
                        waits=[('te', tk_wo['te']), (f'yf{par}', tk['yfc_free'][par]['st'])],
                        inc=[('ve', 1)])
                    tk['pY_free'] = tev
                    tst = self.dma('sy', self.s_yf[gc * 128:(gc + 1) * 128, :],
                                   sb['yfc'][:, par * 1024:(par + 1) * 1024],
                                   waits=[('ve', tev['ve'])], sem=f'yf{par}')
                    tk['yfc_free'][par] = {'st': tst[f'yf{par}'], 've': tev['ve']}
                else:
                    tldy = self.dma('sy', sb['yfc'][:, par * 1024:(par + 1) * 1024],
                                    self.s_yf[gc * 128:(gc + 1) * 128, :],
                                    waits=[('ve', tk['yfc_free'][par]['ve'])] +
                                    ([(f'yf{par}', self.now(f'yf{par}'))] if self.fpo else []),
                                    sem=f'yl{par}')
                    tev = self.ve(lambda nc, e, par=par: nc.vector.tensor_tensor(
                        sb['xm'][:, par * 1024:(par + 1) * 1024], self.ps['pY'][:, :],
                        sb['yfc'][:, par * 1024:(par + 1) * 1024], ALU.add),
                        waits=[('te', tk_wo['te']), (f'yl{par}', tldy[f'yl{par}']),
                               ('ve', tk['xm_free'][par]['ve'])],
                        inc=[('ve', 1)])
                    tk['pY_free'] = tev
                    tk['yfc_free'][par] = {'st': tk['yfc_free'][par]['st'], 've': tev['ve']}
                    tk['xm_full'][par] = tev
                    if ci % 2 == 1:
                        self._final_pair(g, [chunks[ci - 1], c])
            tk_zsil_free = {'te': self.now('te'), 've': tk_zsil_users['ve']}
            tk_xbcs_free = {'te': self.now('te'), 'gp': self.now('gp')}
            tk_cr_free = {'te': self.now('te')}
            tk_xnT_free = {'te': self.now('te')}
        self.tk['group_done'] = {'te': self.now('te'), 've': self.now('ve'),
                                 'se': self.now('se'), 'gp': self.now('gp')}

    def _final_pair(self, g, ccs):
        sb, tk, d = self.sb, self.tk, self.d
        for idx, c in enumerate(ccs):
            gc = g * NCH + c
            par = gc % 2
            txb = self.ve(lambda nc, e, par=par: nc.vector.tensor_copy(
                sb['xmb'][:, :], sb['xm'][:, par * 1024:(par + 1) * 1024]),
                waits=[('ve', tk['xm_full'][par]['ve'])], inc=[('ve', 1)])
            for kt in range(8):
                self.trans(self.ps['pT'][:, kt * 128:(kt + 1) * 128],
                           sb['xmb'][:, kt * 128:(kt + 1) * 128],
                           waits=([('ve', txb['ve']), ('ve', tk['pT_free']['ve'])] if kt == 0 else ()),
                           inc=[('te', 1)] if kt == 7 else ())
            tk['pT_free'] = self.ve(lambda nc, e: nc.vector.tensor_copy(
                sb['xmT'][:, :], self.ps['pT'][:, :]),
                waits=[('te', self.now('te'))], inc=[('ve', 1)])
            txt = tk['pT_free']
            for dh in range(2):
                i_w = self.next_slot()
                pw = self.slots[i_w]
                for kt in range(8):
                    sl = kt % 4
                    twm = self.dma('gp', sb['b_m'][:, sl * 512: (sl + 1) * 512],
                                   d['wm'].rearrange('(kt p) n -> kt p n', p=128)[kt, :, dh * 512:(dh + 1) * 512],
                                   waits=[('te', tk['bm_free'][sl]['te'])], sem=f'wm{sl}')
                    w0 = [(f'wm{sl}', twm[f'wm{sl}'])]
                    if kt == 0:
                        w0 += [('ve', self.slot_free[i_w]['ve']), ('se', self.slot_free[i_w]['se']),
                               ('ve', txt['ve'])]
                    self.mm(pw, sb['xmT'][:, kt * 128:(kt + 1) * 128],
                            sb['b_m'][:, sl * 512:(sl + 1) * 512],
                            start=(kt == 0), stop=False, waits=w0,
                            inc=[('te', 1)])
                    tk['bm_free'][sl] = {'te': self.now('te')}
                self.mm(pw, sb['ones2'][:, :], sb['bmr'][:, dh * 512:(dh + 1) * 512],
                        start=False, stop=True, inc=[('te', 1)])
                tth = self.se(lambda nc, e, pw=pw, idx=idx, dh=dh: nc.scalar.activation(
                    sb['t1'][:, idx * 1024 + dh * 512: idx * 1024 + (dh + 1) * 512], pw, AF.Tanh),
                    waits=[('te', self.now('te')),
                           (f'do{idx}', tk['outb_free'][idx].get('do', 0))], inc=[('se', 1)])
                self.slot_free[i_w] = {'ve': 0, 'se': tth['se']}
        tk_t1 = {'se': self.now('se')}
        for dh in range(2):
            for kt in range(8):
                sl = kt % 4
                pevw = [('pev', self.now('pev'))] if self.fpo else []
                tmh = self.dma('sy', sb['b_m'][:, sl * 512: (sl + 1) * 512],
                               self.s_m[0, kt, dh],
                               waits=[('te', tk['bm_free'][sl]['te'])] + pevw,
                               sem=f'fm{sl}')
                tml = self.dma('sy', sb['b_m'][:, 2048 + sl * 512: 2048 + (sl + 1) * 512],
                               self.s_m[1, kt, dh],
                               waits=[('te', tk['bm_free'][4 + sl]['te'])] + pevw,
                               sem=f'fl{sl}')
                for idx, c in enumerate(ccs):
                    w0 = [(f'fm{sl}', tmh[f'fm{sl}']), (f'fl{sl}', tml[f'fl{sl}'])]
                    if kt == 0:
                        w0.append(('ve', tk['pY_free']['ve']))
                    self.mm(self.ps['pY'][:, idx * 512:(idx + 1) * 512],
                            sb['xnT'][:, kt * 512 + c * 128: kt * 512 + c * 128 + 128],
                            sb['b_m'][:, sl * 512:(sl + 1) * 512],
                            start=(kt == 0), stop=False, waits=w0)
                    self.mm(self.ps['pY'][:, idx * 512:(idx + 1) * 512],
                            sb['xnTl'][:, kt * 512 + c * 128: kt * 512 + c * 128 + 128],
                            sb['b_m'][:, sl * 512:(sl + 1) * 512], False, False,
                            inc=[('te', 1)] if idx == 1 else ())
                    if idx == 1:
                        tk['bm_free'][sl] = {'te': self.now('te')}
                    self.mm(self.ps['pY'][:, idx * 512:(idx + 1) * 512],
                            sb['xnT'][:, kt * 512 + c * 128: kt * 512 + c * 128 + 128],
                            sb['b_m'][:, 2048 + sl * 512: 2048 + (sl + 1) * 512], False, False,
                            inc=[('te', 1)] if idx == 1 else ())
                    if idx == 1:
                        tk['bm_free'][4 + sl] = {'te': self.now('te')}
            for idx, c in enumerate(ccs):
                self.mm(self.ps['pY'][:, idx * 512:(idx + 1) * 512],
                        sb['ones2'][:, :], sb['fbr'][:, dh * 512:(dh + 1) * 512],
                        False, True, inc=[('te', 1)])
            tkff = {'te': self.now('te')}
            tlast = None
            for idx, c in enumerate(ccs):
                tlast = self.ve(lambda nc, e, idx=idx, dh=dh: nc.vector.scalar_tensor_tensor(
                    sb['t1'][:, idx * 1024 + dh * 512: idx * 1024 + (dh + 1) * 512],
                    sb['t1'][:, idx * 1024 + dh * 512: idx * 1024 + (dh + 1) * 512],
                    0.1, self.ps['pY'][:, idx * 512:(idx + 1) * 512], ALU.mult, ALU.add),
                    waits=[('te', tkff['te']), ('se', tk_t1['se'])], inc=[('ve', 1)] if idx == 1 else ())
            tk['pY_free'] = {'ve': self.now('ve')}
        for idx, c in enumerate(ccs):
            gc = g * NCH + c
            par = gc % 2
            self.ve(lambda nc, e, idx=idx, par=par: nc.vector.tensor_tensor(
                sb['t1'][:, idx * 1024:(idx + 1) * 1024],
                sb['t1'][:, idx * 1024:(idx + 1) * 1024],
                sb['xm'][:, par * 1024:(par + 1) * 1024], ALU.add), inc=[('ve', 1)])
            tk['xm_free'][par] = {'ve': self.now('ve')}
            tld = self.dma('sy', sb['xq'][:, par * 1024:(par + 1) * 1024],
                           d['x'][gc * 128:(gc + 1) * 128, :],
                           waits=[('ve', tk['xq_free'][par]['ve'])], sem=f'xq{par}')
            tfin = self.ve(lambda nc, e, idx=idx, par=par: nc.vector.tensor_tensor(
                sb['t1'][:, idx * 1024:(idx + 1) * 1024],
                sb['t1'][:, idx * 1024:(idx + 1) * 1024],
                sb['xq'][:, par * 1024:(par + 1) * 1024], ALU.add),
                waits=[(f'xq{par}', tld[f'xq{par}'])], inc=[('ve', 1)])
            tk['xq_free'][par] = tfin
            tst = self.dma('sy', self.out[gc * 128:(gc + 1) * 128, :],
                           sb['t1'][:, idx * 1024:(idx + 1) * 1024],
                           waits=[('ve', tfin['ve'])], sem=f'do{idx}')
            tk['outb_free'][idx] = {'do': tst[f'do{idx}']}

    def _emit_block(self):
        nc = self.nc
        sems = self.sems

        def runner(name, eng):
            for fn, waits, incs in self.q[name]:
                for s, v in waits:
                    eng.wait_ge(sems[s], v)
                if fn is None:
                    continue
                inst = fn(nc, eng)
                for s, a in incs:
                    inst.then_inc(sems[s], a)

        with nc.Block() as blk:
            @blk.tensor
            def _(eng):
                runner('te', eng)

            @blk.vector
            def _(eng):
                runner('ve', eng)

            @blk.scalar
            def _(eng):
                runner('se', eng)

            @blk.gpsimd
            def _(eng):
                runner('gp', eng)

            @blk.sync
            def _(eng):
                runner('sy', eng)


def make_inputs(inp):
    c = host_consts()
    m = {}
    m['x'] = None
    for s in ('f', 'b'):
        m[f'wi_{s}'] = np.ascontiguousarray(inp[f'L0_attn_Wi_{s}'], np.float32)
        m[f'wo_{s}'] = np.ascontiguousarray(inp[f'L0_attn_Wo_{s}'], np.float32)
        m[f'convw_{s}'] = np.ascontiguousarray(inp[f'L0_attn_conv_{s}'], np.float32)
        m[f'convb_{s}'] = np.ascontiguousarray(inp[f'L0_attn_convb_{s}'], np.float32)
        m[f'alog_{s}'] = np.ascontiguousarray(inp[f'L0_attn_Alog_{s}'], np.float32)
        m[f'dtb_{s}'] = np.ascontiguousarray(inp[f'L0_attn_dtb_{s}'], np.float32)
        dd = np.zeros((128, 4096), np.float32)
        Dv = np.asarray(inp[f'L0_attn_D_{s}'], np.float32)
        for h in range(32):
            for ss in range(128):
                dd[ss, h * 128 + ss] = Dv[h]
        m[f'ddg_{s}'] = dd
    m['wm'] = np.ascontiguousarray(inp['L0_attn_Wm'], np.float32)
    m['bm'] = np.ascontiguousarray(inp['L0_attn_bm'], np.float32)
    m['wfr'] = np.ascontiguousarray(inp['L0_ffn_w_freq_real'], np.float32)
    m['wfi'] = np.ascontiguousarray(inp['L0_ffn_w_freq_imag'], np.float32)
    m['fbias'] = np.ascontiguousarray(inp['L0_ffn_bias'], np.float32)
    for k, v in c.items():
        m[k] = v
    return m


def kernel(**inp):
    x = np.asarray(inp['L0_attn_x'], np.float32)
    base = make_inputs(inp)
    b = Builder()
    nc = b.build()
    in_maps = []
    for core in range(8):
        mm_ = dict(base)
        mm_['x'] = np.ascontiguousarray(x[core])
        in_maps.append(mm_)
    res = run_bass_kernel_spmd(nc, in_maps, core_ids=list(range(8)))
    out = np.stack([r['out'] for r in res.results], axis=0)
    return out


# version 16
# speedup vs baseline: 1.8055x; 1.0034x over previous
import os
import numpy as np
import ml_dtypes
import concourse.bass as bass
import concourse.mybir as mybir
from concourse.bass_utils import run_bass_kernel_spmd

F32 = mybir.dt.float32
BF16 = mybir.dt.bfloat16
AF = mybir.ActivationFunctionType
ALU = mybir.AluOpType
BF = ml_dtypes.bfloat16

DM, DI, NH = 1024, 2048, 32
CONVD = DI + 2 * 128
PROJ = 2 * DI + 2 * 128 + NH
NCH, NG, L = 4, 8, 4096
NJ = 35


def _hilo(x):
    h = x.astype(BF).astype(np.float32)
    return h.astype(BF), (x - h).astype(BF)


def host_consts():
    j = np.arange(DM)
    ang = 2 * np.pi * np.outer(j, j) / DM
    Fr = np.cos(ang).astype(np.float32)
    Fi = (-np.sin(ang)).astype(np.float32)
    c = {}
    c['cFr_h'], c['cFr_l'] = _hilo(Fr)
    c['cGi_h'], c['cGi_l'] = _hilo(-Fi)
    c['cFi_h'], c['cFi_l'] = _hilo(Fi)
    c['cFrd_h'], c['cFrd_l'] = _hilo(Fr / DM)
    c['cFid_h'], c['cFid_l'] = _hilo(Fi / DM)
    c['cIdent'] = np.eye(128).astype(BF)
    tri = np.tril(np.ones((128, 128), np.float32))
    c['cTriF'] = tri.T.astype(BF)
    c['cTriB'] = tri.astype(BF)
    indF = np.zeros((16, 512), np.float32)
    indB = np.zeros((16, 512), np.float32)
    for i in range(4):
        blk = slice(i * 128, (i + 1) * 128)
        indF[i * 4 + 0, blk] = -1.0; indF[i * 4 + 1, blk] = -1.0
        indF[i * 4 + 2, blk] = 1.0; indF[i * 4 + 3, blk] = 1.0
        indB[i * 4 + 0, blk] = 1.0; indB[i * 4 + 1, blk] = 1.0
        indB[i * 4 + 2, blk] = 1.0; indB[i * 4 + 3, blk] = 1.0
    c['cIndF'] = indF.astype(BF)
    c['cIndB'] = indB.astype(BF)
    c['cOnes2'] = np.ones((2, 128), np.float32).astype(BF)
    c['cOnes2N'] = (-np.ones((2, 128), np.float32)).astype(BF)
    ev = np.ones((4, 128), np.float32); ev[0] = -1.0; ev[1] = -1.0
    c['cEvB'] = ev.astype(BF)
    c['cOnesF'] = np.ones((32, 128), np.float32)
    return c


class Builder:
    def __init__(self, debug=False):
        import contextlib
        self.fpo = os.environ.get('BASS_SIM_FP64') == '1'
        self._qlast = {}
        self.nc = bass.Bass("TRN2", target_bir_lowering=False, debug=False)
        self.ctx = contextlib.ExitStack()
        self.debug = debug
        self.q = {e: [] for e in ("te", "ve", "se", "gp", "sy")}
        self.cnt = {}
        self._declare()

    def emit(self, eng, fn, waits=(), inc=(), is_dma=False):
        if self.fpo and fn is not None and not is_dma and eng in ('te', 've', 'se', 'gp'):
            waits = list(waits) + [(eng, self.cnt[eng])]
            if not any(s == eng for s, _ in inc):
                inc = list(inc) + [(eng, 1)]
        incs = []
        for s, a in inc:
            self.cnt[s] += a
            incs.append((s, a))
        wts = tuple((s, v) for (s, v) in waits if v > 0)
        self.q[eng].append((fn, wts, tuple(incs)))
        return {s: self.cnt[s] for s, _ in incs}

    def now(self, s):
        return self.cnt[s]

    def dma(self, qeng, dst, src, waits=(), sem=None, slow=False):
        sem = sem or ('dsy' if qeng == 'sy' else 'dgp')
        if self.fpo:
            prev = self._qlast.get(qeng)
            if prev is not None:
                waits = list(waits) + [prev]
        tok = self.emit(qeng, lambda nc, e, dst=dst, src=src, slow=slow:
                        e.dma_start(out=dst, in_=src, allow_slow_non_contiguous=slow),
                        waits=waits, inc=[(sem, 16)], is_dma=True)
        if self.fpo:
            self._qlast[qeng] = (sem, tok[sem])
        return tok

    def mm(self, psd, lhsT, rhs, start, stop, waits=(), inc=(), tp=None):
        def f(nc, e, a=psd, b=lhsT, c=rhs, st=start, sp=stop, tp=tp):
            return nc.tensor.matmul(a, b, c, start=st, stop=sp, tile_position=tp)
        return self.emit('te', f, waits=waits, inc=inc)

    def trans(self, psd, src, waits=(), inc=()):
        ident = self.sb['ident']
        k = src.shape[0]
        return self.emit('te', lambda nc, e, a=psd, b=src, i=ident, k=k:
                         nc.tensor.transpose(a, b, i[0:k, 0:k]), waits=waits, inc=inc)

    def ve(self, fn, waits=(), inc=()):
        return self.emit('ve', fn, waits=waits, inc=inc)

    def se(self, fn, waits=(), inc=()):
        return self.emit('se', fn, waits=waits, inc=inc)

    def gp(self, fn, waits=(), inc=()):
        return self.emit('gp', fn, waits=waits, inc=inc)

    def _declare(self):
        nc = self.nc
        self.d = {}
        def din(name, shape, dt=F32):
            self.d[name] = nc.dram_tensor(name, shape, dt, kind="ExternalInput").ap()
        din('x', [L, DM])
        for s in ('f', 'b'):
            din(f'wi_{s}', [DM, PROJ]); din(f'wo_{s}', [DI, DM])
            din(f'convw_{s}', [4, CONVD]); din(f'convb_{s}', [CONVD])
            din(f'alog_{s}', [NH]); din(f'dtb_{s}', [NH])
            din(f'ddg_{s}', [128, 4096])
        din('wm', [DM, DM]); din('bm', [DM])
        din('wfr', [DM, DM]); din('wfi', [DM, DM]); din('fbias', [DM])
        for k, v in host_consts().items():
            din(k, list(v.shape), BF16 if v.dtype == BF else F32)
        self.out = nc.dram_tensor('out', [L, DM], F32, kind="ExternalOutput").ap()
        dbg = "ExternalOutput" if self.debug else "Internal"
        self.s_wi = {s: nc.dram_tensor(f's_wi_{s}', [NJ, 128, 8, 128], BF16, kind="Internal").ap()
                     for s in ('f', 'b')}
        self.s_yf = nc.dram_tensor('s_yf', [L, DM], BF16, kind=dbg).ap()
        self.s_w = nc.dram_tensor('s_w', [4, 8, 8, 128, 128], BF16, kind=dbg).ap()
        self.s_a = nc.dram_tensor('s_a', [4, 8, 8, 128, 128], BF16, kind=dbg).ap()
        self.s_m = nc.dram_tensor('s_m', [2, 8, 2, 128, 512], BF16, kind=dbg).ap()
        self.s_cr = nc.dram_tensor('s_cr', [5, 32, 512], BF16, kind="Internal").ap()
        if self.debug:
            self.dbgt = {}
            for nm, shape, dt in [('dxnT', [128, 4096], BF16), ('dzsil', [128, 8192], BF16),
                                  ('dxbcs', [128, 9216], BF16), ('ddt', [32, 512], F32),
                                  ('dcum', [32, 512], F32), ('dlexp', [128, 2048], BF16),
                                  ('dlv', [128, 2048], BF16), ('dmt', [128, 2048], BF16),
                                  ('dgm', [128, 128], BF16), ('dyg', [128, 2048], BF16),
                                  ('demega', [128, 512], BF16), ('dcrc', [4, 4096], BF16),
                                  ('dwv', [32, 512], BF16), ('dxt', [128, 2048], BF16),
                                  ('dhst', [128, 2048], BF16), ('decl', [128, 512], F32),
                                  ('dhprev', [128, 2048], BF16)]:
                self.dbgt[nm] = nc.dram_tensor(nm, shape, dt, kind='ExternalOutput').ap()

        self.sb = {}
        def sbuf(name, shape, dt):
            self.sb[name] = self.ctx.enter_context(nc.sbuf_tensor(name, shape, dt))
        for name, shape, dt in [
            ('b_wi', [128, 4096], BF16), ('b_wo', [128, 16384], BF16),
            ('b_m', [128, 4096], BF16),
            ('zsil', [128, 8192], BF16), ('xbcs', [128, 9216], BF16),
            ('tmpc', [128, 1040], BF16), ('cstash', [128, 54], BF16),
            ('cw', [128, 72], F32), ('cb', [128, 18], F32),
            ('yg', [128, 4096], BF16), ('xm', [128, 2048], BF16), ('yfc', [128, 2048], BF16),
            ('xnT', [128, 4096], BF16), ('xnTl', [128, 4096], BF16),
            ('xq', [128, 2048], F32),
            ('xnch', [128, 1024], BF16), ('xncl', [128, 1024], BF16),
            ('lv', [128, 2048], BF16),
            ('mt', [128, 2048], BF16), ('mt2', [128, 2048], BF16), ('lv2', [128, 2048], BF16),
            ('gmask', [128, 128], BF16), ('xt', [128, 2048], BF16),
            ('hst', [128, 2048], BF16),
            ('Bt', [128, 128], BF16), ('wct', [128, 32], F32), ('pqbc', [128, 32], F32),
            ('ddiag', [128, 4096], BF16),
            ('dt_', [32, 512], F32), ('lgda', [32, 512], F32), ('cum', [32, 512], F32),
            ('lgdt', [32, 512], F32),
            ('cumhi', [32, 512], BF16), ('cumlo', [32, 512], BF16),
            ('ldthi', [32, 512], BF16), ('ldtlo', [32, 512], BF16),
            ('wv', [32, 512], BF16), ('pqb', [32, 4], BF16),
            ('clh', [32, 512], BF16), ('cll', [32, 512], BF16),
            ('clsc', [32, 2], F32),
            ('emc', [16, 1024], BF16), ('crc', [4, 4096], BF16), ('pqrow', [1, 128], BF16),
            ('pk32', [128, 512], BF16),
            ('onesf', [32, 128], F32),
            ('t1', [128, 2048], F32),

            ('mean', [128, 1], F32), ('sqs', [128, 1], F32), ('rstd', [128, 1], F32),
            ('nmrs', [128, 1], F32), ('epsc', [128, 1], F32),
            ('negA', [32, 1], F32), ('aexp', [32, 1], F32), ('dtbc', [32, 1], F32),
            ('bmr', [2, 1024], BF16), ('fbr', [2, 1024], BF16),
            ('ident', [128, 128], BF16), ('triF', [128, 128], BF16), ('triB', [128, 128], BF16),
            ('indF', [16, 512], BF16), ('indB', [16, 512], BF16),
            ('ones2', [2, 128], BF16), ('ones2n', [2, 128], BF16), ('evB', [4, 128], BF16),
            ('wprep', [128, 1024], F32),
        ]:
            sbuf(name, shape, dt)
        self.sb['pstr'] = self.sb['zsil']
        self.sb['lstr'] = self.sb['xbcs']
        self.sb['wpo'] = self.sb['yg']
        self.sb['convt'] = self.sb['wprep']
        self.sb['cump'] = self.sb['lgda']
        self.sb['wvin'] = self.sb['dt_']
        self.sb['brow'] = None
        self.sb['xmb'] = self.sb['xnch']
        self.sb['xmT'] = self.sb['xncl']

        self.ps = {}
        def psumt(name, shape, dt):
            self.ps[name] = self.ctx.enter_context(nc.psum_tensor(name, shape, dt))
        psumt('pS0', [128, 512], F32); psumt('pS1', [128, 512], F32)
        psumt('pQ0', [128, 512], F32); psumt('pQ1', [128, 512], F32); psumt('pQ2', [128, 512], F32)
        psumt('pT', [128, 1024], BF16)
        psumt('pY', [128, 1024], F32)

        self.semnames = ['dsy', 'dgp', 'te', 've', 'se', 'gp',
                         'do0', 'do1',
                         'sst0', 'sst1', 'sld0', 'sld1', 'wld', 'wst', 'bld',
                         'ph0', 'ph1', 'ph2', 'ph3', 'pev', 'dp', 'dd', 'wo',
                         'xq0', 'xq1', 'wi0', 'wi1', 'wi2', 'wi3',
                         'pk', 'em', 'sc', 'co', 'cr', 'pc',
                         'yf0', 'yf1', 'yl0', 'yl1',
                         'wm0', 'wm1', 'wm2', 'wm3', 'fm0', 'fm1', 'fm2', 'fm3',
                         'fl0', 'fl1', 'fl2', 'fl3']
        self.sems = {n: self.ctx.enter_context(nc.semaphore(n)) for n in self.semnames}
        for n in self.semnames:
            self.cnt[n] = 0

        self.slots = [self.ps['pS0'][:, :], self.ps['pS1'][:, :], self.ps['pQ0'][:, :],
                      self.ps['pQ1'][:, :], self.ps['pQ2'][:, :]]
        self.slot_free = [{'ve': 0, 'se': 0} for _ in range(5)]
        self.pool_i = 0
        self.tk = dict(
            pT_free={'ve': 0}, pY_free={'ve': 0},
            yg_free=[{'te': 0}, {'te': 0}],
            yfc_free=[{'st': 0, 've': 0}, {'st': 0, 've': 0}],
            xm_free=[{'ve': 0}, {'ve': 0}], xm_full=[{'ve': 0}, {'ve': 0}],
            xq_free=[{'ve': 0}, {'ve': 0}], xq2_free=[{'ve': 0}, {'ve': 0}],
            outb_free=[{'do': 0}, {'do': 0}],
            wi_free=[{'te': 0}, {'te': 0}, {'te': 0}, {'te': 0}],
            bm_free=[{'te': 0} for _ in range(8)],
            group_done={'te': 0, 've': 0, 'se': 0, 'gp': 0},
            ymm_free={'te': 0}, hbf_users={'te': 0},
            t1_free=[{'ve': 0}, {'ve': 0}],
        )

    def next_slot(self):
        i = self.pool_i % 5
        self.pool_i += 1
        return i

    def dbg_dump(self, pairs):
        wd = [(n, self.now(n)) for n in self.semnames]
        t = None
        for nm, src in pairs:
            t = self.dma('sy', self.dbgt[nm], src, waits=wd)
            wd = ()
        for q in ('te', 've', 'se', 'gp'):
            self.emit(q, None, waits=[('dsy', t['dsy'])])

    def build(self):
        self._prep()
        self._phase0()
        self._dir(False)
        self._dir(True)
        self.emit('sy', None, waits=[('do0', self.now('do0')), ('do1', self.now('do1'))])
        self._emit_block()
        return self.nc

    def _prep(self):
        d, sb = self.d, self.sb
        for nm, tgt in [('cIdent', 'ident'), ('cTriF', 'triF'), ('cTriB', 'triB'),
                        ('cIndF', 'indF'), ('cIndB', 'indB'), ('cOnes2', 'ones2'),
                        ('cOnes2N', 'ones2n'), ('cEvB', 'evB'), ('cOnesF', 'onesf')]:
            self.dma('sy', sb[tgt][:, :], d[nm])


        for q in ('te', 've', 'se', 'gp'):
            self.emit(q, None, waits=[('dsy', self.now('dsy'))])
        self.ve(lambda nc, e: nc.vector.memset(sb['epsc'][:, :], 1e-5), inc=[('ve', 1)])
        if self.fpo:
            self.ve(lambda nc, e: nc.vector.memset(sb['pk32'][:, :], 0.0), inc=[('ve', 1)])
        prev = {}
        for s in ('f', 'b'):
            wi_r = d[f'wi_{s}'].rearrange('(kt p) c -> p kt c', p=128)
            for j in range(NJ):
                jw = PROJ - (NJ - 1) * 128 if j == NJ - 1 else 128
                par = j % 2
                key = (s, j - 2) if j >= 2 else (('f', NJ - 1 - par) if s == 'b' else None)
                wts = [(f'sst{par}', prev[key][f'sst{par}'])] if key in prev else ()
                tld = self.dma('gp', sb['wpo'][:, par * 1024: par * 1024 + 8 * jw]
                               .rearrange('p (kt jj) -> p kt jj', jj=jw),
                               wi_r[:, :, j * 128: j * 128 + jw], waits=wts, sem=f'sld{par}')
                prev[(s, j)] = self.dma(
                    'sy', self.s_wi[s][j, :, :, 0:jw],
                    sb['wpo'][:, par * 1024: par * 1024 + 8 * jw]
                    .rearrange('p (kt jj) -> p kt jj', jj=jw),
                    waits=[(f'sld{par}', tld[f'sld{par}'])], sem=f'sst{par}')
        self._sst = [('sst0', self.now('sst0')), ('sst1', self.now('sst1'))]
        tlast_ve = {'ve': self.now('ve')}
        tlast_st = {'wst': 0}
        first_hl = [('sst0', self.now('sst0')), ('sst1', self.now('sst1'))]
        for wnm, base in [('wfr', 0), ('wfi', 2)]:
            for kt in range(8):
                tld = self.dma('gp', sb['wprep'][:, :], d[wnm][kt * 128:(kt + 1) * 128, :],
                               waits=[('ve', tlast_ve['ve'])], sem='wld')
                self.ve(lambda nc, e: nc.vector.tensor_copy(sb['wpo'][:, 0:1024], sb['wprep'][:, :]),
                        waits=[('wld', tld['wld']), ('wst', tlast_st['wst'])] + first_hl)
                first_hl = []
                tlast_ve = self.ve(lambda nc, e: nc.vector.tensor_sub(
                    sb['wpo'][:, 1024:2048], sb['wprep'][:, :], sb['wpo'][:, 0:1024]),
                    inc=[('ve', 1)])
                self.dma('sy', self.s_w[base, kt].transpose([1, 0, 2]),
                         sb['wpo'][:, 0:1024].rearrange('p (ot c) -> p ot c', c=128),
                         waits=[('ve', tlast_ve['ve'])], sem='wst')
                tlast_st = self.dma('sy', self.s_w[base + 1, kt].transpose([1, 0, 2]),
                                    sb['wpo'][:, 1024:2048].rearrange('p (ot c) -> p ot c', c=128),
                                    waits=[('ve', tlast_ve['ve'])], sem='wst')
        for src, tgt in [('bm', 'bmr'), ('fbias', 'fbr')]:
            tld = self.dma('gp', sb['wprep'][0:1, :], d[src].rearrange('(a n) -> a n', a=1),
                           waits=[('ve', self.now('ve')), ('wld', self.now('wld')),
                                  ('bld', self.now('bld'))], sem='bld')
            self.ve(lambda nc, e, tgt=tgt: nc.vector.tensor_copy(
                sb[tgt][0:1, :], sb['wprep'][0:1, :]), waits=[('bld', tld['bld'])])
            tlo = self.ve(lambda nc, e, tgt=tgt: nc.vector.tensor_sub(
                sb['t1'][0:1, 0:1024], sb['wprep'][0:1, :], sb[tgt][0:1, :]), inc=[('ve', 1)])
            self.dma('gp', sb[tgt][1:2, :], sb['t1'][0:1, 0:1024],
                     waits=[('ve', tlo['ve'])], sem='bld')

    def _ph0_evac(self, pd, dst_h, dst_l, first=(), tiled=False):
        sb = self.sb
        self.ve(lambda nc, e, pd=pd: nc.vector.tensor_copy(sb['wpo'][:, 0:512], pd),
                waits=[('te', self.now('te')), ('pev', self.now('pev'))] + list(first))
        tev = self.ve(lambda nc, e, pd=pd: nc.vector.tensor_sub(
            sb['wpo'][:, 512:1024], pd, sb['wpo'][:, 0:512]), inc=[('ve', 1)])
        src_h, src_l = sb['wpo'][:, 0:512], sb['wpo'][:, 512:1024]
        if tiled:
            src_h = src_h.rearrange('p (mt c) -> p mt c', c=128)
            src_l = src_l.rearrange('p (mt c) -> p mt c', c=128)
            dst_h = dst_h.transpose([1, 0, 2])
            dst_l = dst_l.transpose([1, 0, 2])
        self.dma('sy', dst_h, src_h, waits=[('ve', tev['ve'])], sem='pev')
        self.dma('sy', dst_l, src_l, waits=[('ve', tev['ve'])], sem='pev')
        return tev

    def _phase0(self):
        self.emit('sy', None, waits=[('dsy', self.now('dsy')), ('sst0', self.now('sst0')),
                                     ('sst1', self.now('sst1')), ('wst', self.now('wst'))])
        d, sb = self.d, self.sb
        CONST1 = ['cFr_h', 'cFr_l', 'cGi_h', 'cGi_l', 'cFi_h', 'cFi_l']
        terms_r = [(0, 0), (1, 0), (0, 1), (2, 2), (3, 2), (2, 3)]
        terms_i = [(2, 0), (3, 0), (2, 1), (0, 4), (1, 4), (0, 5)]
        lhs_tok = [{'te': 0}, {'te': 0}, {'te': 0}, {'te': 0}]
        wst_tot = self.now('wst')
        first_ev = [('sst0', self.now('sst0')), ('sst1', self.now('sst1')),
                    ('wst', self.now('wst'))]

        def rhs_cache(q):
            col = q * 512
            if col < 16384:
                return sb['b_wo'][:, col:col + 512]
            return sb['zsil'][:, col - 16384:col - 16384 + 512]

        def load_cache(tiles):
            gen = [('te', self.now('te'))]
            for q, (cn, r0, c0) in enumerate(tiles):
                self.dma('sy', rhs_cache(q), d[cn][r0:r0 + 128, c0:c0 + 512],
                         waits=gen if (q == 0 or self.fpo) else (), sem='pc')
            return self.now('pc')

        for mh in range(2):
            pc_tok = load_cache([(cn, kt * 128, mh * 512)
                                 for kt in range(8) for cn in CONST1])
            for ot in range(8):
                i_r = self.next_slot()
                i_i = self.next_slot()
                pdr, pdi = self.slots[i_r], self.slots[i_i]
                for kt in range(8):
                    kp = kt % 4
                    t_lhs = None
                    for wi_i in range(4):
                        t_lhs = self.dma('sy', sb['lstr'][:, (kp * 4 + wi_i) * 128:(kp * 4 + wi_i + 1) * 128],
                                         self.s_w[wi_i, kt, ot],
                                         waits=([('te', lhs_tok[kp]['te'])] +
                                                ([('wst', wst_tot)] if self.fpo else []))
                                         if (wi_i == 0 or self.fpo) else (),
                                         sem=f'ph{kp}')
                    for ri, (pd, idx, terms) in enumerate(((pdr, i_r, terms_r), (pdi, i_i, terms_i))):
                        for ti, (wl, cc) in enumerate(terms):
                            w0 = []
                            if ri == 0 and ti == 0:
                                w0 = [(f'ph{kp}', t_lhs[f'ph{kp}']), ('pc', pc_tok)]
                                if kt == 0:
                                    w0 += [('ve', self.slot_free[i_r]['ve']), ('se', self.slot_free[i_r]['se']),
                                           ('ve', self.slot_free[i_i]['ve']), ('se', self.slot_free[i_i]['se'])]
                            last = (ri == 1 and ti == 5)
                            self.mm(pd, sb['lstr'][:, (kp * 4 + wl) * 128:(kp * 4 + wl + 1) * 128],
                                    rhs_cache(kt * 6 + cc),
                                    start=(kt == 0 and ti == 0), stop=(kt == 7 and ti == 5),
                                    waits=w0, inc=[('te', 1)] if last else ())
                    lhs_tok[kp] = {'te': self.now('te')}
                for ri, (pd, idx) in enumerate(((pdr, i_r), (pdi, i_i))):
                    tev = self._ph0_evac(
                        pd,
                        self.s_a[ri * 2, ot, mh * 4:(mh + 1) * 4],
                        self.s_a[ri * 2 + 1, ot, mh * 4:(mh + 1) * 4],
                        first=first_ev, tiled=True)
                    first_ev = []
                    self.slot_free[idx] = {'ve': tev['ve'], 'se': 0}
        self.emit('sy', None, waits=[('pev', self.now('pev')), ('ph0', self.now('ph0')),
                                     ('ph1', self.now('ph1')), ('ph2', self.now('ph2')),
                                     ('ph3', self.now('ph3'))])
        terms2 = [(0, 0), (1, 0), (0, 1), (2, 2), (3, 2), (2, 3)]
        pev_s1 = self.now('pev')
        for oh in range(2):
            pc_tok = load_cache([(cn, ot * 128, oh * 512)
                                 for ot in range(8)
                                 for cn in ('cFrd_h', 'cFrd_l', 'cFid_h', 'cFid_l')])
            for mtl in range(8):
                idx = self.next_slot()
                pd = self.slots[idx]
                for ot in range(8):
                    kp = ot % 4
                    t_lhs = None
                    for ai in range(4):
                        t_lhs = self.dma('sy', sb['lstr'][:, (kp * 4 + ai) * 128:(kp * 4 + ai + 1) * 128],
                                         self.s_a[ai, ot, mtl],
                                         waits=([('te', lhs_tok[kp]['te'])] +
                                                ([('pev', pev_s1)] if self.fpo else []))
                                         if (ai == 0 or self.fpo) else (),
                                         sem=f'ph{kp}')
                    for ti, (al, cc) in enumerate(terms2):
                        w0 = []
                        if ti == 0:
                            w0 = [(f'ph{kp}', t_lhs[f'ph{kp}']), ('pc', pc_tok)]
                            if ot == 0:
                                w0 += [('ve', self.slot_free[idx]['ve']), ('se', self.slot_free[idx]['se'])]
                        self.mm(pd, sb['lstr'][:, (kp * 4 + al) * 128:(kp * 4 + al + 1) * 128],
                                rhs_cache(ot * 4 + cc),
                                start=(ot == 0 and ti == 0), stop=(ot == 7 and ti == 5),
                                waits=w0, inc=[('te', 1)] if ti == 5 else ())
                    lhs_tok[kp] = {'te': self.now('te')}
                tev = self._ph0_evac(
                    pd,
                    self.s_m[0, mtl, oh],
                    self.s_m[1, mtl, oh])
                self.slot_free[idx] = {'ve': tev['ve'], 'se': 0}

    def _dir(self, is_b):
        d, sb, tk = self.d, self.sb, self.tk
        self.emit('sy', None, waits=[('dsy', self.now('dsy')), ('pev', self.now('pev')),
                                     ('yf0', self.now('yf0')), ('yf1', self.now('yf1'))])
        dirw = 'b' if is_b else 'f'
        gd = tk['group_done']
        wo_gate = self.now('te')
        t_wo = None
        for kt in range(16):
            t_wo = self.dma('gp', sb['b_wo'][:, kt * 1024:(kt + 1) * 1024],
                            d[f'wo_{dirw}'][kt * 128:(kt + 1) * 128, :],
                            waits=[('te', wo_gate)] if (kt == 0 or self.fpo) else (), sem='wo')

        gdw = [('te', gd['te']), ('ve', gd['ve']), ('se', gd['se'])]
        self.dma('gp', sb['cw'][:, :].rearrange('p (tau ct) -> p tau ct', ct=18),
                 d[f'convw_{dirw}'].rearrange('tau (ct p) -> p tau ct', p=128), slow=True,
                 waits=gdw, sem='dp')
        gdw2 = gdw if self.fpo else []
        self.dma('gp', sb['cb'][:, :], d[f'convb_{dirw}'].rearrange('(ct p) -> p ct', p=128),
                 slow=True, waits=gdw2, sem='dp')
        t_al = self.dma('gp', sb['aexp'][:, :], d[f'alog_{dirw}'].rearrange('(a b) -> a b', b=1),
                        slow=True, waits=gdw2, sem='dp')
        self.dma('gp', sb['dtbc'][:, :], d[f'dtb_{dirw}'].rearrange('(a b) -> a b', b=1),
                 slow=True, waits=gdw2, sem='dp')
        t_dd = self.dma('gp', sb['ddiag'][:, :], d[f'ddg_{dirw}'],
                        waits=[('te', gd['te']), ('ve', gd['ve'])], sem='dd')
        tse = self.se(lambda nc, e: nc.scalar.activation(sb['negA'][:, :], sb['aexp'][:, :], AF.Exp),
                      waits=[('dp', self.now('dp'))], inc=[('se', 1)])
        t_na = self.ve(lambda nc, e: nc.vector.tensor_scalar_mul(sb['negA'][:, :], sb['negA'][:, :], -1.0),
                       waits=[('se', tse['se'])], inc=[('ve', 1)])
        t_h0 = self.ve(lambda nc, e: nc.vector.memset(sb['hst'][:, :], 0.0),
                       waits=[('te', tk['group_done']['te'])], inc=[('ve', 1)])
        self.ve(lambda nc, e: nc.vector.memset(sb['cstash'][:, :], 0.0), inc=[('ve', 1)])
        t_dirprep = {'dp': self.now('dp'), 've': self.now('ve')}

        tk_hup = t_h0
        tk_zsil_free = {'te': tk['group_done']['te'], 've': 0}
        tk_xbcs_free = {'te': tk['group_done']['te'], 'gp': tk['group_done']['gp']}
        tk_dt_free = {'ve': 0, 'co': 0}
        tk_xnT_free = {'te': tk['group_done']['te']}
        tk_cr_free = {'te': 0}
        tk_wv_users = {'te': 0}
        tk_mt_gp = {'gp': 0}
        tk_crc_users = {'te': 0}
        t_ddg = t_dd

        groups = list(range(NG - 1, -1, -1)) if is_b else list(range(NG))
        for g in groups:
            chunks = list(range(NCH - 1, -1, -1)) if is_b else list(range(NCH))
            tk_ln_sq = {'ve': getattr(self, '_conv_last', 0)}
            tk_ln_tr = {'te': 0}
            for c in chunks:
                gc = g * NCH + c
                par = gc % 2
                xqs = sb['xq'][:, par * 1024:(par + 1) * 1024]
                tld = self.dma('sy', xqs, d['x'][gc * 128:(gc + 1) * 128, :],
                               waits=[('ve', tk['xq_free'][par]['ve']),
                                      ('dp', t_dirprep['dp'])], sem=f'xq{par}')
                tsqr = self.se(lambda nc, e, xqs=xqs: nc.scalar.activation(
                    sb['wprep'][:, :], xqs, AF.Square),
                    waits=[(f'xq{par}', tld[f'xq{par}']), ('ve', tk_ln_sq['ve'])],
                    inc=[('se', 1)])
                def vc(fn, waits=()):
                    t = self.ve(fn, waits=list(waits) + [('ve', self.now('ve'))], inc=[('ve', 1)])
                    return t
                vc(lambda nc, e, xqs=xqs: nc.vector.reduce_sum(
                    sb['mean'][:, :], xqs, axis=mybir.AxisListType.X),
                    waits=[(f'xq{par}', tld[f'xq{par}'])])
                vc(lambda nc, e: nc.vector.tensor_scalar_mul(sb['mean'][:, :], sb['mean'][:, :], 1.0 / DM))
                tk_ln_sq = vc(lambda nc, e: nc.vector.reduce_sum(
                    sb['sqs'][:, :], sb['wprep'][:, :], axis=mybir.AxisListType.X),
                    waits=[('se', tsqr['se'])])
                vc(lambda nc, e: nc.vector.tensor_scalar_mul(sb['sqs'][:, :], sb['sqs'][:, :], 1.0 / DM))
                vc(lambda nc, e: nc.vector.tensor_mul(sb['rstd'][:, :], sb['mean'][:, :], sb['mean'][:, :]))
                tvar = vc(lambda nc, e: nc.vector.tensor_sub(sb['sqs'][:, :], sb['sqs'][:, :], sb['rstd'][:, :]))
                tsq = self.se(lambda nc, e: nc.scalar.activation(sb['rstd'][:, :], sb['sqs'][:, :], AF.Sqrt,
                                                                 bias=sb['epsc'][:, :]),
                              waits=[('ve', tvar['ve'])], inc=[('se', 1)])
                vc(lambda nc, e: nc.vector.reciprocal(sb['rstd'][:, :], sb['rstd'][:, :]),
                   waits=[('se', tsq['se'])])
                vc(lambda nc, e: nc.vector.tensor_mul(sb['nmrs'][:, :], sb['mean'][:, :], sb['rstd'][:, :]))
                vc(lambda nc, e: nc.vector.tensor_scalar_mul(sb['nmrs'][:, :], sb['nmrs'][:, :], -1.0))
                tnx = vc(lambda nc, e, xqs=xqs: nc.vector.tensor_scalar(
                    sb['xnch'][:, :], xqs, sb['rstd'][:, :], sb['nmrs'][:, :], ALU.mult, ALU.add),
                    waits=[('te', tk_ln_tr['te'])])
                if is_b:
                    vc(lambda nc, e, xqs=xqs: nc.vector.scalar_tensor_tensor(
                        sb['wprep'][:, :], xqs, sb['rstd'][:, :], sb['xnch'][:, :],
                        ALU.mult, ALU.subtract))
                    tnx = vc(lambda nc, e: nc.vector.tensor_scalar(
                        sb['xncl'][:, :], sb['wprep'][:, :], sb['nmrs'][:, :], None, ALU.add))
                self.tk['xq_free'][par] = tnx
                tk_ln_sq = {'ve': self.now('ve')}
                passes = [('xnch', 'xnT')] + ([('xncl', 'xnTl')] if is_b else [])
                for srcb, dstb in passes:
                    for kt in range(8):
                        self.trans(self.ps['pT'][:, kt * 128:(kt + 1) * 128],
                                   sb[srcb][:, kt * 128:(kt + 1) * 128],
                                   waits=([('ve', tnx['ve']), ('ve', tk['pT_free']['ve']),
                                           ('te', tk_xnT_free['te'])] if kt == 0 else ()),
                                   inc=[('te', 1)] if kt == 7 else ())
                    tkte = self.now('te')
                    tk_ln_tr = {'te': tkte}
                    dst = sb[dstb][:, :].rearrange('p (kt t) -> p kt t', t=512)[:, :, c * 128:(c + 1) * 128]
                    tk['pT_free'] = self.ve(lambda nc, e, dst=dst: nc.vector.tensor_copy(
                        dst, self.ps['pT'][:, :].rearrange('p (kt t) -> p kt t', t=128)),
                        waits=[('te', tkte)], inc=[('ve', 1)])
            tk_ln = {'ve': self.now('ve')}

            jorder = [34] + list(range(16, 34)) + list(range(16))
            t_wld = {}
            def load_wi(ii):
                j = jorder[ii]
                jw = PROJ - (NJ - 1) * 128 if j == NJ - 1 else 128
                bpar = ii % 4
                t_wld[j] = self.dma('sy', sb['b_wi'][:, bpar * 1024: bpar * 1024 + 8 * jw]
                                    .rearrange('p (kt jj) -> p kt jj', jj=jw),
                                    self.s_wi[dirw][j, :, :, 0:jw],
                                    waits=[('te', tk['wi_free'][bpar]['te'])] +
                                    (self._sst if self.fpo else []),
                                    sem=f'wi{bpar}')
            load_wi(0); load_wi(1); load_wi(2); load_wi(3)
            par_evac = [dict(self.slot_free[i]) for i in range(4)]
            tk_conv_done = {}
            for ii, j in enumerate(jorder):
                jw = PROJ - (NJ - 1) * 128 if j == NJ - 1 else 128
                ppar = ii % 4
                bsl = ii % 4
                pd = self.slots[ppar] if jw == 128 else self.slots[ppar][0:32, :]
                for kt in range(8):
                    w0 = []
                    if kt == 0:
                        w0 = [(f'wi{bsl}', t_wld[j][f'wi{bsl}']), ('ve', tk_ln['ve']),
                              ('se', par_evac[ppar]['se']), ('ve', par_evac[ppar]['ve'])]
                    self.mm(pd, sb['b_wi'][:, bsl * 1024 + kt * jw: bsl * 1024 + (kt + 1) * jw],
                            sb['xnT'][:, kt * 512:(kt + 1) * 512],
                            start=(kt == 0), stop=(kt == 7),
                            waits=w0, inc=[('te', 1)] if kt == 7 else ())
                tk['wi_free'][bsl] = {'te': self.now('te')}
                if ii + 4 < NJ:
                    load_wi(ii + 4)
                tkte = self.now('te')
                if j < 16:
                    tz = self.se(lambda nc, e, j=j, ppar=ppar: nc.scalar.activation(
                        sb['zsil'][:, j * 512:(j + 1) * 512], self.slots[ppar][:, :], AF.Silu),
                        waits=[('te', tkte), ('te', tk_zsil_free['te']),
                               ('ve', tk_zsil_free['ve'])], inc=[('se', 1)])
                    par_evac[ppar] = {'se': tz['se'], 've': 0}
                elif j < 34:
                    ct = j - 16
                    cpar = ct % 2
                    if not is_b:
                        data_off = cpar * 520 + 3; stash_off = cpar * 520
                        save_off = cpar * 520 + 512; tap = [0, 1, 2, 3]
                    else:
                        data_off = cpar * 520; stash_off = cpar * 520 + 512
                        save_off = cpar * 520; tap = [3, 2, 1, 0]
                    tx = self.se(lambda nc, e, ppar=ppar, o=data_off: nc.scalar.activation(
                        sb['tmpc'][:, o:o + 512], self.slots[ppar], AF.Copy),
                        waits=[('te', tkte),
                               ('ve', tk_conv_done.get(ct - 2, {'ve': 0})['ve'])],
                        inc=[('se', 1)])
                    par_evac[ppar] = {'se': tx['se'], 've': 0}
                    self.ve(lambda nc, e, ct=ct, so=stash_off: nc.vector.tensor_copy(
                        sb['tmpc'][:, so:so + 3], sb['cstash'][:, ct * 3:ct * 3 + 3]),
                        waits=[('se', tx['se'])])
                    base = cpar * 520
                    self.ve(lambda nc, e, b=base, ct=ct, t0=tap[0], cpar=cpar: nc.vector.tensor_scalar(
                        sb['convt'][:, cpar * 512:(cpar + 1) * 512], sb['tmpc'][:, b + t0: b + t0 + 512],
                        sb['cw'][:, ct:ct + 1], None, ALU.mult))
                    for tau in (1, 2, 3):
                        self.ve(lambda nc, e, b=base, ct=ct, tau=tau, to=tap[tau], cpar=cpar:
                                nc.vector.scalar_tensor_tensor(
                                    sb['convt'][:, cpar * 512:(cpar + 1) * 512],
                                    sb['tmpc'][:, b + to: b + to + 512],
                                    sb['cw'][:, tau * 18 + ct: tau * 18 + ct + 1],
                                    sb['convt'][:, cpar * 512:(cpar + 1) * 512], ALU.mult, ALU.add))
                    tcv = self.ve(lambda nc, e, ct=ct, so=save_off: nc.vector.tensor_copy(
                        sb['cstash'][:, ct * 3:ct * 3 + 3], sb['tmpc'][:, so:so + 3]),
                        inc=[('ve', 1)])
                    tk_conv_done[ct] = tcv
                    self.se(lambda nc, e, ct=ct, cpar=cpar: nc.scalar.activation(
                        sb['xbcs'][:, ct * 512:(ct + 1) * 512], sb['convt'][:, cpar * 512:(cpar + 1) * 512],
                        AF.Silu, bias=sb['cb'][:, ct:ct + 1]),
                        waits=[('ve', tcv['ve']), ('te', tk_xbcs_free['te']),
                               ('gp', tk_xbcs_free['gp'])], inc=[('se', 1)])
                else:
                    self.se(lambda nc, e, ppar=ppar: nc.scalar.activation(
                        sb['lgda'][:, :], self.slots[ppar][0:32, :], AF.Abs,
                        bias=sb['dtbc'][:, :]),
                        waits=[('te', tkte), ('ve', tk_dt_free['ve']),
                               ('co', tk_dt_free['co']), ('dp', t_dirprep['dp'])],
                        inc=[('se', 1)])
                    self.se(lambda nc, e: nc.scalar.activation(
                        sb['cum'][:, :], sb['lgda'][:, :], AF.Exp, scale=-1.0))
                    self.se(lambda nc, e: nc.scalar.activation(
                        sb['cum'][:, :], sb['cum'][:, :], AF.Ln, bias=sb['onesf'][0:32, 0:1]))
                    tdt = self.se(lambda nc, e, ppar=ppar: nc.scalar.activation(
                        sb['dt_'][:, :], self.slots[ppar][0:32, :], AF.Relu,
                        bias=sb['dtbc'][:, :]), inc=[('se', 1)])
                    par_evac[ppar] = {'se': tdt['se'], 've': 0}
                    self._t_dtadd = self.ve(lambda nc, e: nc.vector.tensor_add(
                        sb['dt_'][:, :], sb['dt_'][:, :], sb['cum'][:, :]),
                        waits=[('se', tdt['se'])], inc=[('ve', 1)])
                if j == 33:
                    conv_se = self.now('se')
            self._conv_last = tk_conv_done[17]['ve']
            tk_sweep = {'se': self.now('se'), 've': self.now('ve'), 'te': self.now('te')}
            for _i in range(4):
                self.slot_free[_i] = {'ve': 0, 'se': tk_sweep['se']}

            tlgda = self.ve(lambda nc, e: nc.vector.tensor_scalar(
                sb['lgda'][:, :], sb['dt_'][:, :], sb['negA'][:, :], None, ALU.mult),
                waits=[('ve', self._t_dtadd['ve']), ('ve', t_na['ve']),
                       ('se', getattr(self, '_dtp_se', 0))], inc=[('ve', 1)])
            tscan = tlgda
            for c in range(NCH):
                tscan = self.ve(lambda nc, e, c=c: nc.vector.tensor_tensor_scan(
                    sb['cum'][:, c * 128:(c + 1) * 128], sb['lgda'][:, c * 128:(c + 1) * 128],
                    sb['lgda'][:, c * 128:(c + 1) * 128], 0.0, ALU.add, ALU.bypass),
                    waits=[('ve', tscan['ve'])], inc=[('ve', 1)])
            if is_b:
                self.ve(lambda nc, e: nc.vector.tensor_sub(sb['cump'][:, :], sb['cum'][:, :], sb['lgda'][:, :]),
                        waits=[('ve', tscan['ve'])])
            tlg = self.se(lambda nc, e: nc.scalar.activation(sb['lgdt'][:, :], sb['dt_'][:, :], AF.Ln),
                          waits=[('ve', self._t_dtadd['ve'])], inc=[('se', 1)])
            CC = sb['cump'] if is_b else sb['cum']
            self.ve(lambda nc, e, CC=CC: nc.vector.tensor_copy(sb['cumhi'][:, :], CC[:, :]),
                    waits=[('se', tlg['se']), ('pk', self.now('pk')), ('sc', self.now('sc')),
                           ('ve', tscan['ve'])])
            self.ve(lambda nc, e, CC=CC: nc.vector.tensor_tensor(
                sb['cumlo'][:, :], CC[:, :], sb['cumhi'][:, :], ALU.subtract))
            self.ve(lambda nc, e: nc.vector.tensor_copy(sb['ldthi'][:, :], sb['lgdt'][:, :]))
            self.ve(lambda nc, e: nc.vector.tensor_tensor(
                sb['ldtlo'][:, :], sb['lgdt'][:, :], sb['ldthi'][:, :], ALU.subtract))
            if is_b:
                self.ve(lambda nc, e: nc.vector.tensor_add(sb['wvin'][:, :], sb['lgdt'][:, :], sb['cump'][:, :]),
                        waits=[('se', tlg['se'])], inc=[('ve', 1)])
            else:
                self.ve(lambda nc, e: nc.vector.tensor_sub(sb['wvin'][:, :], sb['lgdt'][:, :], sb['cum'][:, :]),
                        waits=[('se', tlg['se'])], inc=[('ve', 1)])
            for c in range(NCH):
                kw = dict(bias=sb['cum'][:, c * 128 + 127:c * 128 + 128]) if not is_b else {}
                self.se(lambda nc, e, c=c, kw=kw: nc.scalar.activation(
                    sb['wv'][:, c * 128:(c + 1) * 128], sb['wvin'][:, c * 128:(c + 1) * 128],
                    AF.Exp, **kw),
                    waits=[('ve', self.now('ve')), ('te', tk_wv_users['te'])], inc=[('se', 1)])
            tpq = self.se(lambda nc, e: nc.scalar.activation(
                sb['pqb'][:, :], sb['cum'][:, 127:512:128], AF.Exp),
                waits=[('ve', tscan['ve']), ('sc', self.now('sc'))], inc=[('se', 1)])
            if is_b:
                for c in range(NCH):
                    def vc2(fn, waits=()):
                        return self.ve(fn, waits=list(waits) + [('ve', self.now('ve'))],
                                       inc=[('ve', 1)])
                    vc2(lambda nc, e, c=c: nc.vector.tensor_copy(
                        sb['clsc'][:, 0:1], sb['cum'][:, c * 128 + 127:c * 128 + 128]),
                        waits=[('se', tpq['se'])])
                    vc2(lambda nc, e, c=c: nc.vector.tensor_scalar(
                        sb['clh'][:, c * 128:(c + 1) * 128], sb['onesf'][:, 0:128],
                        sb['clsc'][:, 0:1], None, ALU.mult))
                    vc2(lambda nc, e, c=c: nc.vector.tensor_tensor(
                        sb['clsc'][:, 1:2], sb['clsc'][:, 0:1],
                        sb['clh'][:, c * 128:c * 128 + 1], ALU.subtract))
                    vc2(lambda nc, e, c=c: nc.vector.tensor_scalar(
                        sb['cll'][:, c * 128:(c + 1) * 128], sb['onesf'][:, 0:128],
                        sb['clsc'][:, 1:2], None, ALU.mult))
            tk_dtp = {'ve': self.now('ve'), 'se': self.now('se')}
            self._dtp_se = tk_dtp['se']
            wcr = [('ve', tk_dtp['ve']), ('se', tk_dtp['se']), ('te', tk_cr_free['te'])]
            if self.fpo:
                wcr = wcr + [('em', self.now('em'))]
            for qq, srcb in enumerate(('cumhi', 'cumlo', 'ldthi', 'ldtlo')):
                self.dma('gp', sb['pk32'][qq:128:4, :], sb[srcb][:, :],
                         waits=wcr if (qq == 0 or self.fpo) else (), sem='pk')
            tk_pk = {'pk': self.now('pk')}
            crstores = [(0, 'cumhi'), (1, 'cumlo')] + ([(2, 'clh'), (3, 'cll')] if is_b else [])
            wsc = [('ve', tk_dtp['ve']), ('se', tk_dtp['se']),
                   ('cr', self.now('cr')), ('co', self.now('co'))] if self.fpo else []
            for row, srcb in crstores:
                self.dma('gp', self.s_cr[row, :, :], sb[srcb][:, :], waits=wsc, sem='sc')
            self.dma('gp', self.s_cr[4, :, 0:4], sb['pqb'][:, :], waits=wsc, sem='sc')
            tk_crst = {'sc': self.now('sc')}
            t_coll = self.dma('gp', sb['pqrow'][0:1, :].rearrange('a (p c) -> a p c', c=4),
                              self.s_cr[4, :, 0:4].unsqueeze(0),
                              waits=[('sc', tk_crst['sc'])] +
                              ([('te', tk_cr_free['te'])] if self.fpo else []), sem='co')
            tk_coll = {'co': self.now('co')}
            tk_dt_free = {'ve': tk_dtp['ve'], 'co': tk_coll['co']}

            for ci, c in enumerate(chunks):
                gc = g * NCH + c
                par = gc % 2
                crpairs = [(0, 'cumhi'), (1, 'cumlo')] + ([(2, 'clh'), (3, 'cll')] if is_b else [])
                for row, srcb in crpairs:
                    self.dma('gp', sb['crc'][row:row + 1, :].rearrange('a (h t) -> a h t', t=128),
                             self.s_cr[row, :, c * 128:(c + 1) * 128].unsqueeze(0),
                             waits=[('te', tk_crc_users['te']), ('sc', tk_crst['sc'])]
                             if (row == 0 or self.fpo) else (),
                             sem='cr')
                tk_coll_c = {'cr': self.now('cr')}
                for bb8 in range(8):
                    self.dma('gp', sb['emc'][:, bb8 * 128:(bb8 + 1) * 128],
                             sb['pk32'][bb8 * 16:(bb8 + 1) * 16, c * 128:(c + 1) * 128],
                             waits=[('te', tk_crc_users['te']), ('pk', tk_pk['pk'])]
                             if (bb8 == 0 or self.fpo) else (),
                             sem='em')
                tk_em = {'em': self.now('em')}
                Btile = sb['xbcs'][:, 16 * 512 + c * 128: 16 * 512 + c * 128 + 128]
                Ctile = sb['xbcs'][:, 17 * 512 + c * 128: 17 * 512 + c * 128 + 128]
                for half in range(2):
                    for k8 in range(8):
                        cti = half * 8 + k8
                        self.trans(self.ps['pT'][:, k8 * 128:(k8 + 1) * 128],
                                   sb['xbcs'][:, cti * 512 + c * 128: cti * 512 + c * 128 + 128],
                                   waits=([('se', conv_se), ('ve', tk['pT_free']['ve']),
                                           ('gp', self.now('gp'))] if k8 == 0 else ()),
                                   inc=[('te', 1)] if k8 == 7 else ())
                    tkte = self.now('te')
                    tk['pT_free'] = self.ve(lambda nc, e, half=half: nc.vector.tensor_copy(
                        sb['xt'][:, half * 1024:(half + 1) * 1024], self.ps['pT'][:, :]),
                        waits=[('te', tkte)], inc=[('ve', 1)])
                self.trans(self.ps['pT'][:, 0:128], Btile,
                           waits=[('ve', tk['pT_free']['ve'])], inc=[('te', 1)])
                self.trans(self.ps['pT'][:, 128:160], sb['wv'][0:32, c * 128:(c + 1) * 128],
                           waits=[('se', tk_dtp['se'])], inc=[('te', 1)])
                tkte = self.now('te')
                tk_wv_users = {'te': tkte}
                self.ve(lambda nc, e: nc.vector.tensor_copy(sb['Bt'][:, :], self.ps['pT'][:, 0:128]),
                        waits=[('te', tkte)])
                tk['pT_free'] = self.ve(lambda nc, e: nc.vector.tensor_copy(
                    sb['wct'][:, :], self.ps['pT'][:, 128:160]), inc=[('ve', 1)])
                t_xt = tk['pT_free']
                isl = self.next_slot()
                gsl = self.slots[isl]
                self.mm(gsl[:, 0:128], Btile, Ctile, start=True, stop=True,
                        waits=[('ve', self.slot_free[isl]['ve']), ('se', self.slot_free[isl]['se']),
                               ('se', conv_se)], inc=[('te', 1)])
                tgm = self.ve(lambda nc, e, gsl=gsl, m=('triB' if is_b else 'triF'):
                              nc.vector.tensor_tensor(sb['gmask'][:, :], gsl[:, 0:128],
                                                      sb[m][:, :], ALU.mult),
                              waits=[('te', self.now('te'))], inc=[('ve', 1)])
                self.slot_free[isl] = {'ve': tgm['ve'], 'se': 0}
                grep = sb['gmask'][:, :].unsqueeze(1).broadcast_to([128, 16, 128])
                crep = Ctile.unsqueeze(1).broadcast_to([128, 16, 128])
                se_exp_tok = {'se': 0}
                tok_mt, tok_mv = [], []
                for half in range(2):
                    mtb = sb['mt' if half == 0 else 'mt2']
                    lvb = sb['lv' if half == 0 else 'lv2']
                    lv_tok = None
                    for bb in range(4):
                        b = half * 4 + bb
                        i_e = self.next_slot()
                        pe = self.slots[i_e]
                        crs = sb['crc'][0:2, b * 512:(b + 1) * 512]
                        self.mm(pe, sb['ones2n' if is_b else 'ones2'][:, :], crs,
                                start=True, stop=False,
                                waits=[('cr', tk_coll_c['cr']), ('em', tk_em['em']),
                                       ('ve', self.slot_free[i_e]['ve']),
                                       ('se', self.slot_free[i_e]['se'])])
                        self.mm(pe, sb['emc'][0:16, b * 128: b * 128 + 128],
                                sb['indB' if is_b else 'indF'][:, :], start=False, stop=True,
                                inc=[('te', 1)])
                        tke = self.now('te')
                        i_v = self.next_slot()
                        pv = self.slots[i_v]
                        if is_b:
                            self.mm(pv, sb['evB'][:, :], sb['crc'][0:4, b * 512:(b + 1) * 512],
                                    start=True, stop=True,
                                    waits=[('ve', self.slot_free[i_v]['ve']),
                                           ('se', self.slot_free[i_v]['se'])], inc=[('te', 1)])
                        else:
                            self.mm(pv, sb['ones2'][:, :], crs, start=True, stop=True,
                                    waits=[('ve', self.slot_free[i_v]['ve']),
                                           ('se', self.slot_free[i_v]['se'])], inc=[('te', 1)])
                        tkv = self.now('te')
                        e_tok = self.ve(lambda nc, e, pe=pe: nc.vector.tensor_scalar_min(
                            pe, pe, 30.0),
                            waits=[('te', tke)], inc=[('ve', 1)])
                        if self.debug and is_b and gc == 30 and half == 0 and bb == 0:
                            self.dbg_dump([('decl', pe)])
                        w_exp = [('ve', e_tok['ve'])]
                        if bb == 0:
                            w_exp.append(('te', tk['ymm_free']['te']))
                        se_exp_tok = self.se(lambda nc, e, bb=bb, mtb=mtb, pe=pe: nc.scalar.activation(
                            mtb[:, bb * 512:(bb + 1) * 512],
                            pe, AF.Exp),
                            waits=w_exp, inc=[('se', 1)])
                        self.slot_free[i_e] = {'ve': 0, 'se': se_exp_tok['se']}
                        lv_tok = self.se(lambda nc, e, pv=pv, bb=bb, lvb=lvb: nc.scalar.activation(
                            lvb[:, bb * 512:(bb + 1) * 512], pv, AF.Exp),
                            waits=[('te', tkv)], inc=[('se', 1)])
                        self.slot_free[i_v] = {'ve': 0, 'se': lv_tok['se']}
                    self.ve(lambda nc, e, grep=grep, mtb=mtb: nc.vector.tensor_tensor(
                        mtb[:, :].rearrange('p (h t) -> p h t', t=128),
                        mtb[:, :].rearrange('p (h t) -> p h t', t=128), grep, ALU.mult),
                        waits=[('se', se_exp_tok['se']), ('ve', tgm['ve'])])
                    tok_mt.append(self.ve(lambda nc, e, half=half, mtb=mtb: nc.vector.tensor_tensor(
                        mtb[:, :], mtb[:, :],
                        sb['ddiag'][:, half * 2048:(half + 1) * 2048], ALU.add),
                        waits=[('dd', t_ddg['dd'])], inc=[('ve', 1)]))
                    tok_mv.append(self.ve(lambda nc, e, crep=crep, lvb=lvb: nc.vector.tensor_tensor(
                        lvb[:, :].rearrange('p (h t) -> p h t', t=128),
                        lvb[:, :].rearrange('p (h t) -> p h t', t=128), crep, ALU.mult),
                        waits=[('se', lv_tok['se']), ('se', conv_se)], inc=[('ve', 1)]))
                tk_crc_users = {'te': self.now('te')}
                if self.debug and is_b and gc == 30:
                    self.dbg_dump([('dlexp', sb['mt2'][:, :]), ('dlv', sb['lv'][:, :]),
                                   ('dmt', sb['mt'][:, :]),
                                   ('dhprev', sb['hst'][:, :])])
                for half in range(2):
                    mtb = sb['mt' if half == 0 else 'mt2']
                    lvb = sb['lv' if half == 0 else 'lv2']
                    tmt, tmv = tok_mt[half], tok_mv[half]
                    for jj in range(8):
                        j = half * 8 + jj
                        h0, h1 = 2 * j, 2 * j + 1
                        l0, l1 = 2 * jj, 2 * jj + 1
                        slot = self.ps['pY'][:, jj * 128:(jj + 1) * 128]
                        w0 = [('ve', tmt['ve']), ('ve', tmv['ve']), ('ve', tk['pY_free']['ve']),
                              ('ve', tk_hup['ve']), ('ve', t_xt['ve'])] if jj == 0 else ()
                        self.mm(slot[0:64, :], sb['xt'][:, h0 * 64:(h0 + 1) * 64],
                                mtb[:, l0 * 128:(l0 + 1) * 128], True, False, waits=w0, tp=(0, 0))
                        self.mm(slot[0:64, :], sb['hst'][:, h0 * 64:(h0 + 1) * 64],
                                lvb[:, l0 * 128:(l0 + 1) * 128], False, True, tp=(0, 0))
                        self.mm(slot[64:128, :], sb['xt'][:, h1 * 64:(h1 + 1) * 64],
                                mtb[:, l1 * 128:(l1 + 1) * 128], True, False, tp=(0, 64))
                        self.mm(slot[64:128, :], sb['hst'][:, h1 * 64:(h1 + 1) * 64],
                                lvb[:, l1 * 128:(l1 + 1) * 128], False, True, tp=(0, 64),
                                inc=[('te', 1)] if jj == 7 else ())
                    tk_ymm = {'te': self.now('te')}
                    tk['ymm_free'] = tk_ymm
                    tk['hbf_users'] = tk_ymm
                    zslc = sb['zsil'][:, :].rearrange('p (j t) -> p j t', t=512)[
                        :, half * 8:(half + 1) * 8, c * 128:(c + 1) * 128]
                    tk['pY_free'] = self.ve(lambda nc, e, zslc=zslc, par=par, half=half:
                                            nc.vector.tensor_tensor(
                        sb['yg'][:, par * 2048 + half * 1024: par * 2048 + (half + 1) * 1024]
                        .rearrange('p (j t) -> p j t', t=128),
                        self.ps['pY'][:, :].rearrange('p (j t) -> p j t', t=128),
                        zslc, ALU.mult),
                        waits=[('te', tk_ymm['te']), ('te', tk['yg_free'][par]['te']),
                               ('se', tk_sweep['se'])], inc=[('ve', 1)])
                tk_zsil_users = tk['pY_free']
                i_p = self.next_slot()
                psl = self.slots[i_p]
                self.mm(psl[:, 0:32], sb['ones2'][0:1, :], sb['pqrow'][0:1, :].rearrange('a (p c) -> a p c', c=4)[:, :, c], True, True,
                        waits=[('co', tk_coll['co']), ('ve', self.slot_free[i_p]['ve']),
                               ('se', self.slot_free[i_p]['se'])], inc=[('te', 1)])
                tpqb = self.ve(lambda nc, e, psl=psl: nc.vector.tensor_copy(
                    sb['pqbc'][:, :], psl[:, 0:32]),
                    waits=[('te', self.now('te'))], inc=[('ve', 1)])
                self.slot_free[i_p] = {'ve': tpqb['ve'], 'se': 0}
                tk_xw = self.ve(lambda nc, e: nc.vector.tensor_tensor(
                    sb['mt'][:, :].rearrange('p (h q) -> p h q', q=64),
                    sb['xt'][:, :].rearrange('p (h q) -> p h q', q=64),
                    sb['wct'][:, :].unsqueeze(2).broadcast_to([128, NH, 64]), ALU.mult),
                    waits=[('ve', t_xt['ve']), ('te', tk_ymm['te'])], inc=[('ve', 1)])
                tk_mt_gp = {'gp': 0}
                for qq in range(4):
                    i_h = self.next_slot()
                    hs = self.slots[i_h]
                    w0 = [('ve', self.slot_free[i_h]['ve']), ('se', self.slot_free[i_h]['se'])]
                    if qq == 0:
                        w0.append(('ve', tk_xw['ve']))
                    self.mm(hs, sb['Bt'][:, :], sb['mt'][:, qq * 512:(qq + 1) * 512], True, True,
                            waits=w0, inc=[('te', 1)])
                    tkh = self.now('te')
                    hsl = sb['hst'][:, qq * 512:(qq + 1) * 512]
                    self.ve(lambda nc, e, hsl=hsl, qq=qq: nc.vector.tensor_tensor(
                        hsl.rearrange('p (h q) -> p h q', q=64),
                        hsl.rearrange('p (h q) -> p h q', q=64),
                        sb['pqbc'][:, qq * 8:(qq + 1) * 8].unsqueeze(2).broadcast_to([128, 8, 64]),
                        ALU.mult),
                        waits=[('te', tkh), ('ve', tpqb['ve'])])
                    tup = self.ve(lambda nc, e, hsl=hsl, hs=hs: nc.vector.tensor_tensor(
                        hsl, hsl, hs, ALU.add), inc=[('ve', 1)])
                    self.slot_free[i_h] = {'ve': tup['ve'], 'se': 0}
                tk['ymm_free'] = {'te': self.now('te')}
                tk_hup = {'ve': self.now('ve')}
                for dh in range(2):
                    for kt in range(16):
                        w0 = [('ve', tk['pY_free']['ve']), ('wo', t_wo['wo'])] \
                            if (dh == 0 and kt == 0) else ()
                        self.mm(self.ps['pY'][:, dh * 512:(dh + 1) * 512],
                                sb['yg'][:, par * 2048 + kt * 128: par * 2048 + (kt + 1) * 128],
                                sb['b_wo'][:, kt * 1024 + dh * 512: kt * 1024 + dh * 512 + 512],
                                start=(kt == 0), stop=(kt == 15),
                                waits=w0, inc=[('te', 1)] if (kt == 15 and dh == 1) else ())
                tk_wo = {'te': self.now('te')}
                tk['yg_free'][par] = tk_wo
                if self.debug and is_b and gc == 30:
                    self.dbg_dump([('dxnT', sb['xnT'][:, :]), ('dzsil', sb['zsil'][:, :]),
                                   ('dxbcs', sb['xbcs'][:, :]), ('ddt', sb['dt_'][:, :]),
                                   ('dcum', sb['cum'][:, :]),
                                   ('dgm', sb['gmask'][:, :]),
                                   ('dyg', sb['yg'][:, 0:2048]),
                                   ('demega', sb['pk32'][:, :]), ('dcrc', sb['crc'][:, :]),
                                   ('dwv', sb['wv'][:, :]), ('dxt', sb['xt'][:, :]),
                                   ('dhst', sb['hst'][:, :])])
                if not is_b:
                    tev = self.ve(lambda nc, e, par=par: nc.vector.tensor_copy(
                        sb['yfc'][:, par * 1024:(par + 1) * 1024], self.ps['pY'][:, :]),
                        waits=[('te', tk_wo['te']), (f'yf{par}', tk['yfc_free'][par]['st'])],
                        inc=[('ve', 1)])
                    tk['pY_free'] = tev
                    tst = self.dma('sy', self.s_yf[gc * 128:(gc + 1) * 128, :],
                                   sb['yfc'][:, par * 1024:(par + 1) * 1024],
                                   waits=[('ve', tev['ve'])], sem=f'yf{par}')
                    tk['yfc_free'][par] = {'st': tst[f'yf{par}'], 've': tev['ve']}
                else:
                    tldy = self.dma('sy', sb['yfc'][:, par * 1024:(par + 1) * 1024],
                                    self.s_yf[gc * 128:(gc + 1) * 128, :],
                                    waits=[('ve', tk['yfc_free'][par]['ve'])] +
                                    ([(f'yf{par}', self.now(f'yf{par}'))] if self.fpo else []),
                                    sem=f'yl{par}')
                    tev = self.ve(lambda nc, e, par=par: nc.vector.tensor_tensor(
                        sb['xm'][:, par * 1024:(par + 1) * 1024], self.ps['pY'][:, :],
                        sb['yfc'][:, par * 1024:(par + 1) * 1024], ALU.add),
                        waits=[('te', tk_wo['te']), (f'yl{par}', tldy[f'yl{par}']),
                               ('ve', tk['xm_free'][par]['ve'])],
                        inc=[('ve', 1)])
                    tk['pY_free'] = tev
                    tk['yfc_free'][par] = {'st': tk['yfc_free'][par]['st'], 've': tev['ve']}
                    tk['xm_full'][par] = tev
                    if ci % 2 == 1:
                        self._final_pair(g, [chunks[ci - 1], c])
            tk_zsil_free = {'te': self.now('te'), 've': tk_zsil_users['ve']}
            tk_xbcs_free = {'te': self.now('te'), 'gp': self.now('gp')}
            tk_cr_free = {'te': self.now('te')}
            tk_xnT_free = {'te': self.now('te')}
        self.tk['group_done'] = {'te': self.now('te'), 've': self.now('ve'),
                                 'se': self.now('se'), 'gp': self.now('gp')}

    def _final_pair(self, g, ccs):
        sb, tk, d = self.sb, self.tk, self.d
        for idx, c in enumerate(ccs):
            gc = g * NCH + c
            par = gc % 2
            txb = self.ve(lambda nc, e, par=par: nc.vector.tensor_copy(
                sb['xmb'][:, :], sb['xm'][:, par * 1024:(par + 1) * 1024]),
                waits=[('ve', tk['xm_full'][par]['ve'])], inc=[('ve', 1)])
            for kt in range(8):
                self.trans(self.ps['pT'][:, kt * 128:(kt + 1) * 128],
                           sb['xmb'][:, kt * 128:(kt + 1) * 128],
                           waits=([('ve', txb['ve']), ('ve', tk['pT_free']['ve'])] if kt == 0 else ()),
                           inc=[('te', 1)] if kt == 7 else ())
            tk['pT_free'] = self.ve(lambda nc, e: nc.vector.tensor_copy(
                sb['xmT'][:, :], self.ps['pT'][:, :]),
                waits=[('te', self.now('te'))], inc=[('ve', 1)])
            txt = tk['pT_free']
            for dh in range(2):
                i_w = self.next_slot()
                pw = self.slots[i_w]
                for kt in range(8):
                    sl = kt % 4
                    twm = self.dma('gp', sb['b_m'][:, sl * 512: (sl + 1) * 512],
                                   d['wm'].rearrange('(kt p) n -> kt p n', p=128)[kt, :, dh * 512:(dh + 1) * 512],
                                   waits=[('te', tk['bm_free'][sl]['te'])], sem=f'wm{sl}')
                    w0 = [(f'wm{sl}', twm[f'wm{sl}'])]
                    if kt == 0:
                        w0 += [('ve', self.slot_free[i_w]['ve']), ('se', self.slot_free[i_w]['se']),
                               ('ve', txt['ve'])]
                    self.mm(pw, sb['xmT'][:, kt * 128:(kt + 1) * 128],
                            sb['b_m'][:, sl * 512:(sl + 1) * 512],
                            start=(kt == 0), stop=False, waits=w0,
                            inc=[('te', 1)])
                    tk['bm_free'][sl] = {'te': self.now('te')}
                self.mm(pw, sb['ones2'][:, :], sb['bmr'][:, dh * 512:(dh + 1) * 512],
                        start=False, stop=True, inc=[('te', 1)])
                tth = self.se(lambda nc, e, pw=pw, idx=idx, dh=dh: nc.scalar.activation(
                    sb['t1'][:, idx * 1024 + dh * 512: idx * 1024 + (dh + 1) * 512], pw, AF.Tanh),
                    waits=[('te', self.now('te')),
                           (f'do{idx}', tk['outb_free'][idx].get('do', 0))], inc=[('se', 1)])
                self.slot_free[i_w] = {'ve': 0, 'se': tth['se']}
        tk_t1 = {'se': self.now('se')}
        for dh in range(2):
            for kt in range(8):
                sl = kt % 4
                pevw = [('pev', self.now('pev'))] if self.fpo else []
                tmh = self.dma('sy', sb['b_m'][:, sl * 512: (sl + 1) * 512],
                               self.s_m[0, kt, dh],
                               waits=[('te', tk['bm_free'][sl]['te'])] + pevw,
                               sem=f'fm{sl}')
                tml = self.dma('sy', sb['b_m'][:, 2048 + sl * 512: 2048 + (sl + 1) * 512],
                               self.s_m[1, kt, dh],
                               waits=[('te', tk['bm_free'][4 + sl]['te'])] + pevw,
                               sem=f'fl{sl}')
                for idx, c in enumerate(ccs):
                    w0 = [(f'fm{sl}', tmh[f'fm{sl}']), (f'fl{sl}', tml[f'fl{sl}'])]
                    if kt == 0:
                        w0.append(('ve', tk['pY_free']['ve']))
                    self.mm(self.ps['pY'][:, idx * 512:(idx + 1) * 512],
                            sb['xnT'][:, kt * 512 + c * 128: kt * 512 + c * 128 + 128],
                            sb['b_m'][:, sl * 512:(sl + 1) * 512],
                            start=(kt == 0), stop=False, waits=w0)
                    self.mm(self.ps['pY'][:, idx * 512:(idx + 1) * 512],
                            sb['xnTl'][:, kt * 512 + c * 128: kt * 512 + c * 128 + 128],
                            sb['b_m'][:, sl * 512:(sl + 1) * 512], False, False,
                            inc=[('te', 1)] if idx == 1 else ())
                    if idx == 1:
                        tk['bm_free'][sl] = {'te': self.now('te')}
                    self.mm(self.ps['pY'][:, idx * 512:(idx + 1) * 512],
                            sb['xnT'][:, kt * 512 + c * 128: kt * 512 + c * 128 + 128],
                            sb['b_m'][:, 2048 + sl * 512: 2048 + (sl + 1) * 512], False, False,
                            inc=[('te', 1)] if idx == 1 else ())
                    if idx == 1:
                        tk['bm_free'][4 + sl] = {'te': self.now('te')}
            for idx, c in enumerate(ccs):
                self.mm(self.ps['pY'][:, idx * 512:(idx + 1) * 512],
                        sb['ones2'][:, :], sb['fbr'][:, dh * 512:(dh + 1) * 512],
                        False, True, inc=[('te', 1)])
            tkff = {'te': self.now('te')}
            tlast = None
            for idx, c in enumerate(ccs):
                tlast = self.ve(lambda nc, e, idx=idx, dh=dh: nc.vector.scalar_tensor_tensor(
                    sb['t1'][:, idx * 1024 + dh * 512: idx * 1024 + (dh + 1) * 512],
                    sb['t1'][:, idx * 1024 + dh * 512: idx * 1024 + (dh + 1) * 512],
                    0.1, self.ps['pY'][:, idx * 512:(idx + 1) * 512], ALU.mult, ALU.add),
                    waits=[('te', tkff['te']), ('se', tk_t1['se'])], inc=[('ve', 1)] if idx == 1 else ())
            tk['pY_free'] = {'ve': self.now('ve')}
        for idx, c in enumerate(ccs):
            gc = g * NCH + c
            par = gc % 2
            self.ve(lambda nc, e, idx=idx, par=par: nc.vector.tensor_tensor(
                sb['t1'][:, idx * 1024:(idx + 1) * 1024],
                sb['t1'][:, idx * 1024:(idx + 1) * 1024],
                sb['xm'][:, par * 1024:(par + 1) * 1024], ALU.add), inc=[('ve', 1)])
            tk['xm_free'][par] = {'ve': self.now('ve')}
            tld = self.dma('sy', sb['xq'][:, par * 1024:(par + 1) * 1024],
                           d['x'][gc * 128:(gc + 1) * 128, :],
                           waits=[('ve', tk['xq_free'][par]['ve'])], sem=f'xq{par}')
            tfin = self.ve(lambda nc, e, idx=idx, par=par: nc.vector.tensor_tensor(
                sb['t1'][:, idx * 1024:(idx + 1) * 1024],
                sb['t1'][:, idx * 1024:(idx + 1) * 1024],
                sb['xq'][:, par * 1024:(par + 1) * 1024], ALU.add),
                waits=[(f'xq{par}', tld[f'xq{par}'])], inc=[('ve', 1)])
            tk['xq_free'][par] = tfin
            tst = self.dma('sy', self.out[gc * 128:(gc + 1) * 128, :],
                           sb['t1'][:, idx * 1024:(idx + 1) * 1024],
                           waits=[('ve', tfin['ve'])], sem=f'do{idx}')
            tk['outb_free'][idx] = {'do': tst[f'do{idx}']}

    def _emit_block(self):
        nc = self.nc
        sems = self.sems

        def runner(name, eng):
            for fn, waits, incs in self.q[name]:
                for s, v in waits:
                    eng.wait_ge(sems[s], v)
                if fn is None:
                    continue
                inst = fn(nc, eng)
                for s, a in incs:
                    inst.then_inc(sems[s], a)

        with nc.Block() as blk:
            @blk.tensor
            def _(eng):
                runner('te', eng)

            @blk.vector
            def _(eng):
                runner('ve', eng)

            @blk.scalar
            def _(eng):
                runner('se', eng)

            @blk.gpsimd
            def _(eng):
                runner('gp', eng)

            @blk.sync
            def _(eng):
                runner('sy', eng)


def make_inputs(inp):
    c = host_consts()
    m = {}
    m['x'] = None
    for s in ('f', 'b'):
        m[f'wi_{s}'] = np.ascontiguousarray(inp[f'L0_attn_Wi_{s}'], np.float32)
        m[f'wo_{s}'] = np.ascontiguousarray(inp[f'L0_attn_Wo_{s}'], np.float32)
        m[f'convw_{s}'] = np.ascontiguousarray(inp[f'L0_attn_conv_{s}'], np.float32)
        m[f'convb_{s}'] = np.ascontiguousarray(inp[f'L0_attn_convb_{s}'], np.float32)
        m[f'alog_{s}'] = np.ascontiguousarray(inp[f'L0_attn_Alog_{s}'], np.float32)
        m[f'dtb_{s}'] = np.ascontiguousarray(inp[f'L0_attn_dtb_{s}'], np.float32)
        dd = np.zeros((128, 4096), np.float32)
        Dv = np.asarray(inp[f'L0_attn_D_{s}'], np.float32)
        for h in range(32):
            for ss in range(128):
                dd[ss, h * 128 + ss] = Dv[h]
        m[f'ddg_{s}'] = dd
    m['wm'] = np.ascontiguousarray(inp['L0_attn_Wm'], np.float32)
    m['bm'] = np.ascontiguousarray(inp['L0_attn_bm'], np.float32)
    m['wfr'] = np.ascontiguousarray(inp['L0_ffn_w_freq_real'], np.float32)
    m['wfi'] = np.ascontiguousarray(inp['L0_ffn_w_freq_imag'], np.float32)
    m['fbias'] = np.ascontiguousarray(inp['L0_ffn_bias'], np.float32)
    for k, v in c.items():
        m[k] = v
    return m


def kernel(**inp):
    x = np.asarray(inp['L0_attn_x'], np.float32)
    base = make_inputs(inp)
    b = Builder()
    nc = b.build()
    in_maps = []
    for core in range(8):
        mm_ = dict(base)
        mm_['x'] = np.ascontiguousarray(x[core])
        in_maps.append(mm_)
    res = run_bass_kernel_spmd(nc, in_maps, core_ids=list(range(8)))
    out = np.stack([r['out'] for r in res.results], axis=0)
    return out


# version 17
# speedup vs baseline: 1.8120x; 1.0036x over previous
import os
import numpy as np
import ml_dtypes
import concourse.bass as bass
import concourse.mybir as mybir
from concourse.bass_utils import run_bass_kernel_spmd

F32 = mybir.dt.float32
BF16 = mybir.dt.bfloat16
AF = mybir.ActivationFunctionType
ALU = mybir.AluOpType
BF = ml_dtypes.bfloat16

DM, DI, NH = 1024, 2048, 32
CONVD = DI + 2 * 128
PROJ = 2 * DI + 2 * 128 + NH
NCH, NG, L = 4, 8, 4096
NJ = 35


def _hilo(x):
    h = x.astype(BF).astype(np.float32)
    return h.astype(BF), (x - h).astype(BF)


def host_consts():
    j = np.arange(DM)
    ang = 2 * np.pi * np.outer(j, j) / DM
    Fr = np.cos(ang).astype(np.float32)
    Fi = (-np.sin(ang)).astype(np.float32)
    c = {}
    c['cFr_h'], c['cFr_l'] = _hilo(Fr)
    c['cGi_h'], c['cGi_l'] = _hilo(-Fi)
    c['cFi_h'], c['cFi_l'] = _hilo(Fi)
    c['cFrd_h'], c['cFrd_l'] = _hilo(Fr / DM)
    c['cFid_h'], c['cFid_l'] = _hilo(Fi / DM)
    c['cIdent'] = np.eye(128).astype(BF)
    tri = np.tril(np.ones((128, 128), np.float32))
    c['cTriF'] = tri.T.astype(BF)
    c['cTriB'] = tri.astype(BF)
    indF = np.zeros((16, 512), np.float32)
    indB = np.zeros((16, 512), np.float32)
    for i in range(4):
        blk = slice(i * 128, (i + 1) * 128)
        indF[i * 4 + 0, blk] = -1.0; indF[i * 4 + 1, blk] = -1.0
        indF[i * 4 + 2, blk] = 1.0; indF[i * 4 + 3, blk] = 1.0
        indB[i * 4 + 0, blk] = 1.0; indB[i * 4 + 1, blk] = 1.0
        indB[i * 4 + 2, blk] = 1.0; indB[i * 4 + 3, blk] = 1.0
    c['cIndF'] = indF.astype(BF)
    c['cIndB'] = indB.astype(BF)
    c['cOnes2'] = np.ones((2, 128), np.float32).astype(BF)
    c['cOnes2N'] = (-np.ones((2, 128), np.float32)).astype(BF)
    ev = np.ones((4, 128), np.float32); ev[0] = -1.0; ev[1] = -1.0
    c['cEvB'] = ev.astype(BF)
    c['cOnesF'] = np.ones((32, 128), np.float32)
    return c


class Builder:
    def __init__(self, debug=False):
        import contextlib
        self.fpo = os.environ.get('BASS_SIM_FP64') == '1'
        self._qlast = {}
        self.nc = bass.Bass("TRN2", target_bir_lowering=False, debug=False)
        self.ctx = contextlib.ExitStack()
        self.debug = debug
        self.q = {e: [] for e in ("te", "ve", "se", "gp", "sy")}
        self.cnt = {}
        self._declare()

    def emit(self, eng, fn, waits=(), inc=(), is_dma=False):
        if self.fpo and fn is not None and not is_dma and eng in ('te', 've', 'se', 'gp'):
            waits = list(waits) + [(eng, self.cnt[eng])]
            if not any(s == eng for s, _ in inc):
                inc = list(inc) + [(eng, 1)]
        incs = []
        for s, a in inc:
            self.cnt[s] += a
            incs.append((s, a))
        wts = tuple((s, v) for (s, v) in waits if v > 0)
        self.q[eng].append((fn, wts, tuple(incs)))
        return {s: self.cnt[s] for s, _ in incs}

    def now(self, s):
        return self.cnt[s]

    def dma(self, qeng, dst, src, waits=(), sem=None, slow=False):
        sem = sem or ('dsy' if qeng == 'sy' else 'dgp')
        if self.fpo:
            prev = self._qlast.get(qeng)
            if prev is not None:
                waits = list(waits) + [prev]
        tok = self.emit(qeng, lambda nc, e, dst=dst, src=src, slow=slow:
                        e.dma_start(out=dst, in_=src, allow_slow_non_contiguous=slow),
                        waits=waits, inc=[(sem, 16)], is_dma=True)
        if self.fpo:
            self._qlast[qeng] = (sem, tok[sem])
        return tok

    def mm(self, psd, lhsT, rhs, start, stop, waits=(), inc=(), tp=None):
        def f(nc, e, a=psd, b=lhsT, c=rhs, st=start, sp=stop, tp=tp):
            return nc.tensor.matmul(a, b, c, start=st, stop=sp, tile_position=tp)
        return self.emit('te', f, waits=waits, inc=inc)

    def trans(self, psd, src, waits=(), inc=()):
        ident = self.sb['ident']
        k = src.shape[0]
        return self.emit('te', lambda nc, e, a=psd, b=src, i=ident, k=k:
                         nc.tensor.transpose(a, b, i[0:k, 0:k]), waits=waits, inc=inc)

    def ve(self, fn, waits=(), inc=()):
        return self.emit('ve', fn, waits=waits, inc=inc)

    def se(self, fn, waits=(), inc=()):
        return self.emit('se', fn, waits=waits, inc=inc)

    def gp(self, fn, waits=(), inc=()):
        return self.emit('gp', fn, waits=waits, inc=inc)

    def _declare(self):
        nc = self.nc
        self.d = {}
        def din(name, shape, dt=F32):
            self.d[name] = nc.dram_tensor(name, shape, dt, kind="ExternalInput").ap()
        din('x', [L, DM])
        for s in ('f', 'b'):
            din(f'wi_{s}', [DM, PROJ]); din(f'wo_{s}', [DI, DM])
            din(f'convw_{s}', [4, CONVD]); din(f'convb_{s}', [CONVD])
            din(f'alog_{s}', [NH]); din(f'dtb_{s}', [NH])
            din(f'ddg_{s}', [128, 4096])
        din('wm', [DM, DM]); din('bm', [DM])
        din('wfr', [DM, DM]); din('wfi', [DM, DM]); din('fbias', [DM])
        for k, v in host_consts().items():
            din(k, list(v.shape), BF16 if v.dtype == BF else F32)
        self.out = nc.dram_tensor('out', [L, DM], F32, kind="ExternalOutput").ap()
        dbg = "ExternalOutput" if self.debug else "Internal"
        self.s_wi = {s: nc.dram_tensor(f's_wi_{s}', [NJ, 128, 8, 128], BF16, kind="Internal").ap()
                     for s in ('f', 'b')}
        self.s_yf = nc.dram_tensor('s_yf', [L, DM], BF16, kind=dbg).ap()
        self.s_w = nc.dram_tensor('s_w', [4, 8, 8, 128, 128], BF16, kind=dbg).ap()
        self.s_a = nc.dram_tensor('s_a', [4, 8, 8, 128, 128], BF16, kind=dbg).ap()
        self.s_m = nc.dram_tensor('s_m', [2, 8, 2, 128, 512], BF16, kind=dbg).ap()
        self.s_cr = nc.dram_tensor('s_cr', [5, 32, 512], BF16, kind="Internal").ap()
        if self.debug:
            self.dbgt = {}
            for nm, shape, dt in [('dxnT', [128, 4096], BF16), ('dzsil', [128, 8192], BF16),
                                  ('dxbcs', [128, 9216], BF16), ('ddt', [32, 512], F32),
                                  ('dcum', [32, 512], F32), ('dlexp', [128, 2048], BF16),
                                  ('dlv', [128, 2048], BF16), ('dmt', [128, 2048], BF16),
                                  ('dgm', [128, 128], BF16), ('dyg', [128, 2048], BF16),
                                  ('demega', [128, 512], BF16), ('dcrc', [4, 4096], BF16),
                                  ('dwv', [32, 512], BF16), ('dxt', [128, 2048], BF16),
                                  ('dhst', [128, 2048], BF16), ('decl', [128, 512], F32),
                                  ('dhprev', [128, 2048], BF16)]:
                self.dbgt[nm] = nc.dram_tensor(nm, shape, dt, kind='ExternalOutput').ap()

        self.sb = {}
        def sbuf(name, shape, dt):
            self.sb[name] = self.ctx.enter_context(nc.sbuf_tensor(name, shape, dt))
        for name, shape, dt in [
            ('b_wi', [128, 4096], BF16), ('b_wo', [128, 16384], BF16),
            ('b_m', [128, 4096], BF16),
            ('zsil', [128, 8192], BF16), ('xbcs', [128, 9216], BF16),
            ('tmpc', [128, 1040], BF16), ('cstash', [128, 54], BF16),
            ('cw', [128, 72], F32), ('cb', [128, 18], F32),
            ('yg', [128, 4096], BF16), ('xm', [128, 2048], BF16), ('yfc', [128, 2048], BF16),
            ('xnT', [128, 4096], BF16), ('xnTl', [128, 4096], BF16),
            ('xq', [128, 2048], F32),
            ('xnch', [128, 1024], BF16), ('xncl', [128, 1024], BF16),
            ('lv', [128, 2048], BF16),
            ('mt', [128, 2048], BF16), ('mt2', [128, 2048], BF16), ('lv2', [128, 2048], BF16),
            ('gmask', [128, 128], BF16), ('xt', [128, 2048], BF16),
            ('hst', [128, 2048], BF16),
            ('Bt', [128, 128], BF16), ('wct', [128, 32], F32), ('pqbc', [128, 32], F32),
            ('ddiag', [128, 4096], BF16),
            ('dt_', [32, 512], F32), ('lgda', [32, 512], F32), ('cum', [32, 512], F32),
            ('lgdt', [32, 512], F32),
            ('cumhi', [32, 512], BF16), ('cumlo', [32, 512], BF16),
            ('ldthi', [32, 512], BF16), ('ldtlo', [32, 512], BF16),
            ('wv', [32, 512], BF16), ('pqb', [32, 4], BF16),
            ('clh', [32, 512], BF16), ('cll', [32, 512], BF16),
            ('clsc', [32, 2], F32),
            ('emc', [16, 1024], BF16), ('crc', [4, 4096], BF16), ('pqrow', [1, 128], BF16),
            ('pk32', [128, 512], BF16),
            ('onesf', [32, 128], F32),
            ('t1', [128, 2048], F32),

            ('mean', [128, 1], F32), ('sqs', [128, 1], F32), ('rstd', [128, 1], F32),
            ('nmrs', [128, 1], F32), ('epsc', [128, 1], F32),
            ('negA', [32, 1], F32), ('aexp', [32, 1], F32), ('dtbc', [32, 1], F32),
            ('bmr', [2, 1024], BF16), ('fbr', [2, 1024], BF16),
            ('ident', [128, 128], BF16), ('triF', [128, 128], BF16), ('triB', [128, 128], BF16),
            ('indF', [16, 512], BF16), ('indB', [16, 512], BF16),
            ('ones2', [2, 128], BF16), ('ones2n', [2, 128], BF16), ('evB', [4, 128], BF16),
            ('wprep', [128, 1024], F32),
        ]:
            sbuf(name, shape, dt)
        self.sb['pstr'] = self.sb['zsil']
        self.sb['lstr'] = self.sb['xbcs']
        self.sb['wpo'] = self.sb['yg']
        self.sb['convt'] = self.sb['wprep']
        self.sb['cump'] = self.sb['lgda']
        self.sb['wvin'] = self.sb['dt_']
        self.sb['brow'] = None
        self.sb['xmb'] = self.sb['xnch']
        self.sb['xmT'] = self.sb['xncl']

        self.ps = {}
        def psumt(name, shape, dt):
            self.ps[name] = self.ctx.enter_context(nc.psum_tensor(name, shape, dt))
        psumt('pS0', [128, 512], F32); psumt('pS1', [128, 512], F32)
        psumt('pQ0', [128, 512], F32); psumt('pQ1', [128, 512], F32); psumt('pQ2', [128, 512], F32)
        psumt('pT', [128, 1024], BF16)
        psumt('pY', [128, 1024], F32)

        self.semnames = ['dsy', 'dgp', 'te', 've', 'se', 'gp',
                         'do0', 'do1',
                         'sst0', 'sst1', 'sld0', 'sld1', 'wld', 'wst', 'bld',
                         'ph0', 'ph1', 'ph2', 'ph3', 'pev', 'dp', 'dd', 'wo',
                         'xq0', 'xq1', 'wi0', 'wi1', 'wi2', 'wi3',
                         'pk', 'em', 'sc', 'co', 'cr', 'pc',
                         'yf0', 'yf1', 'yl0', 'yl1',
                         'wm0', 'wm1', 'wm2', 'wm3', 'fm0', 'fm1', 'fm2', 'fm3',
                         'fl0', 'fl1', 'fl2', 'fl3']
        self.sems = {n: self.ctx.enter_context(nc.semaphore(n)) for n in self.semnames}
        for n in self.semnames:
            self.cnt[n] = 0

        self.slots = [self.ps['pS0'][:, :], self.ps['pS1'][:, :], self.ps['pQ0'][:, :],
                      self.ps['pQ1'][:, :], self.ps['pQ2'][:, :]]
        self.slot_free = [{'ve': 0, 'se': 0} for _ in range(5)]
        self.pool_i = 0
        self.tk = dict(
            pT_free={'ve': 0}, pY_free={'ve': 0},
            yg_free=[{'te': 0}, {'te': 0}],
            yfc_free=[{'st': 0, 've': 0}, {'st': 0, 've': 0}],
            xm_free=[{'ve': 0}, {'ve': 0}], xm_full=[{'ve': 0}, {'ve': 0}],
            xq_free=[{'ve': 0}, {'ve': 0}], xq2_free=[{'ve': 0}, {'ve': 0}],
            outb_free=[{'do': 0}, {'do': 0}],
            wi_free=[{'te': 0}, {'te': 0}, {'te': 0}, {'te': 0}],
            bm_free=[{'te': 0} for _ in range(8)],
            group_done={'te': 0, 've': 0, 'se': 0, 'gp': 0},
            ymm_free={'te': 0}, hbf_users={'te': 0},
            t1_free=[{'ve': 0}, {'ve': 0}],
        )

    def next_slot(self):
        i = self.pool_i % 5
        self.pool_i += 1
        return i

    def dbg_dump(self, pairs):
        wd = [(n, self.now(n)) for n in self.semnames]
        t = None
        for nm, src in pairs:
            t = self.dma('sy', self.dbgt[nm], src, waits=wd)
            wd = ()
        for q in ('te', 've', 'se', 'gp'):
            self.emit(q, None, waits=[('dsy', t['dsy'])])

    def build(self):
        self._prep()
        self._phase0()
        self._dir(False)
        self._dir(True)
        self.emit('sy', None, waits=[('do0', self.now('do0')), ('do1', self.now('do1'))])
        self._emit_block()
        return self.nc

    def _prep(self):
        d, sb = self.d, self.sb
        for nm, tgt in [('cIdent', 'ident'), ('cTriF', 'triF'), ('cTriB', 'triB'),
                        ('cIndF', 'indF'), ('cIndB', 'indB'), ('cOnes2', 'ones2'),
                        ('cOnes2N', 'ones2n'), ('cEvB', 'evB'), ('cOnesF', 'onesf')]:
            self.dma('sy', sb[tgt][:, :], d[nm])


        for q in ('te', 've', 'se', 'gp'):
            self.emit(q, None, waits=[('dsy', self.now('dsy'))])
        self.ve(lambda nc, e: nc.vector.memset(sb['epsc'][:, :], 1e-5), inc=[('ve', 1)])
        if self.fpo:
            self.ve(lambda nc, e: nc.vector.memset(sb['pk32'][:, :], 0.0), inc=[('ve', 1)])
        prev = {}
        for s in ('f', 'b'):
            wi_r = d[f'wi_{s}'].rearrange('(kt p) c -> p kt c', p=128)
            for j in range(NJ):
                jw = PROJ - (NJ - 1) * 128 if j == NJ - 1 else 128
                par = j % 2
                key = (s, j - 2) if j >= 2 else (('f', NJ - 1 - par) if s == 'b' else None)
                wts = [(f'sst{par}', prev[key][f'sst{par}'])] if key in prev else ()
                tld = self.dma('gp', sb['wpo'][:, par * 1024: par * 1024 + 8 * jw]
                               .rearrange('p (kt jj) -> p kt jj', jj=jw),
                               wi_r[:, :, j * 128: j * 128 + jw], waits=wts, sem=f'sld{par}')
                prev[(s, j)] = self.dma(
                    'sy', self.s_wi[s][j, :, :, 0:jw],
                    sb['wpo'][:, par * 1024: par * 1024 + 8 * jw]
                    .rearrange('p (kt jj) -> p kt jj', jj=jw),
                    waits=[(f'sld{par}', tld[f'sld{par}'])], sem=f'sst{par}')
        self._sst = [('sst0', self.now('sst0')), ('sst1', self.now('sst1'))]
        tlast_ve = {'ve': self.now('ve')}
        tlast_st = {'wst': 0}
        first_hl = [('sst0', self.now('sst0')), ('sst1', self.now('sst1'))]
        for wnm, base in [('wfr', 0), ('wfi', 2)]:
            for kt in range(8):
                tld = self.dma('gp', sb['wprep'][:, :], d[wnm][kt * 128:(kt + 1) * 128, :],
                               waits=[('ve', tlast_ve['ve'])], sem='wld')
                self.ve(lambda nc, e: nc.vector.tensor_copy(sb['wpo'][:, 0:1024], sb['wprep'][:, :]),
                        waits=[('wld', tld['wld']), ('wst', tlast_st['wst'])] + first_hl)
                first_hl = []
                tlast_ve = self.ve(lambda nc, e: nc.vector.tensor_sub(
                    sb['wpo'][:, 1024:2048], sb['wprep'][:, :], sb['wpo'][:, 0:1024]),
                    inc=[('ve', 1)])
                self.dma('sy', self.s_w[base, kt].transpose([1, 0, 2]),
                         sb['wpo'][:, 0:1024].rearrange('p (ot c) -> p ot c', c=128),
                         waits=[('ve', tlast_ve['ve'])], sem='wst')
                tlast_st = self.dma('sy', self.s_w[base + 1, kt].transpose([1, 0, 2]),
                                    sb['wpo'][:, 1024:2048].rearrange('p (ot c) -> p ot c', c=128),
                                    waits=[('ve', tlast_ve['ve'])], sem='wst')
        for src, tgt in [('bm', 'bmr'), ('fbias', 'fbr')]:
            tld = self.dma('gp', sb['wprep'][0:1, :], d[src].rearrange('(a n) -> a n', a=1),
                           waits=[('ve', self.now('ve')), ('wld', self.now('wld')),
                                  ('bld', self.now('bld'))], sem='bld')
            self.ve(lambda nc, e, tgt=tgt: nc.vector.tensor_copy(
                sb[tgt][0:1, :], sb['wprep'][0:1, :]), waits=[('bld', tld['bld'])])
            tlo = self.ve(lambda nc, e, tgt=tgt: nc.vector.tensor_sub(
                sb['t1'][0:1, 0:1024], sb['wprep'][0:1, :], sb[tgt][0:1, :]), inc=[('ve', 1)])
            self.dma('gp', sb[tgt][1:2, :], sb['t1'][0:1, 0:1024],
                     waits=[('ve', tlo['ve'])], sem='bld')

    def _ph0_evac(self, pd, dst_h, dst_l, first=(), tiled=False):
        sb = self.sb
        self.ve(lambda nc, e, pd=pd: nc.vector.tensor_copy(sb['wpo'][:, 0:512], pd),
                waits=[('te', self.now('te')), ('pev', self.now('pev'))] + list(first))
        tev = self.ve(lambda nc, e, pd=pd: nc.vector.tensor_sub(
            sb['wpo'][:, 512:1024], pd, sb['wpo'][:, 0:512]), inc=[('ve', 1)])
        src_h, src_l = sb['wpo'][:, 0:512], sb['wpo'][:, 512:1024]
        if tiled:
            src_h = src_h.rearrange('p (mt c) -> p mt c', c=128)
            src_l = src_l.rearrange('p (mt c) -> p mt c', c=128)
            dst_h = dst_h.transpose([1, 0, 2])
            dst_l = dst_l.transpose([1, 0, 2])
        self.dma('sy', dst_h, src_h, waits=[('ve', tev['ve'])], sem='pev')
        self.dma('sy', dst_l, src_l, waits=[('ve', tev['ve'])], sem='pev')
        return tev

    def _phase0(self):
        self.emit('sy', None, waits=[('dsy', self.now('dsy')), ('sst0', self.now('sst0')),
                                     ('sst1', self.now('sst1')), ('wst', self.now('wst'))])
        d, sb = self.d, self.sb
        CONST1 = ['cFr_h', 'cFr_l', 'cGi_h', 'cGi_l', 'cFi_h', 'cFi_l']
        terms_r = [(0, 0), (1, 0), (0, 1), (2, 2), (3, 2), (2, 3)]
        terms_i = [(2, 0), (3, 0), (2, 1), (0, 4), (1, 4), (0, 5)]
        lhs_tok = [{'te': 0}, {'te': 0}, {'te': 0}, {'te': 0}]
        wst_tot = self.now('wst')
        first_ev = [('sst0', self.now('sst0')), ('sst1', self.now('sst1')),
                    ('wst', self.now('wst'))]

        def rhs_cache(q):
            col = q * 512
            if col < 16384:
                return sb['b_wo'][:, col:col + 512]
            return sb['zsil'][:, col - 16384:col - 16384 + 512]

        def load_cache(tiles):
            gen = [('te', self.now('te'))]
            for q, (cn, r0, c0) in enumerate(tiles):
                self.dma('sy', rhs_cache(q), d[cn][r0:r0 + 128, c0:c0 + 512],
                         waits=gen if (q == 0 or self.fpo) else (), sem='pc')
            return self.now('pc')

        for mh in range(2):
            pc_tok = load_cache([(cn, kt * 128, mh * 512)
                                 for kt in range(8) for cn in CONST1])
            for ot in range(8):
                i_r = self.next_slot()
                i_i = self.next_slot()
                pdr, pdi = self.slots[i_r], self.slots[i_i]
                for kt in range(8):
                    kp = kt % 4
                    t_lhs = None
                    for wi_i in range(4):
                        t_lhs = self.dma('sy', sb['lstr'][:, (kp * 4 + wi_i) * 128:(kp * 4 + wi_i + 1) * 128],
                                         self.s_w[wi_i, kt, ot],
                                         waits=([('te', lhs_tok[kp]['te'])] +
                                                ([('wst', wst_tot)] if self.fpo else []))
                                         if (wi_i == 0 or self.fpo) else (),
                                         sem=f'ph{kp}')
                    for ri, (pd, idx, terms) in enumerate(((pdr, i_r, terms_r), (pdi, i_i, terms_i))):
                        for ti, (wl, cc) in enumerate(terms):
                            w0 = []
                            if ri == 0 and ti == 0:
                                w0 = [(f'ph{kp}', t_lhs[f'ph{kp}']), ('pc', pc_tok)]
                                if kt == 0:
                                    w0 += [('ve', self.slot_free[i_r]['ve']), ('se', self.slot_free[i_r]['se']),
                                           ('ve', self.slot_free[i_i]['ve']), ('se', self.slot_free[i_i]['se'])]
                            last = (ri == 1 and ti == 5)
                            self.mm(pd, sb['lstr'][:, (kp * 4 + wl) * 128:(kp * 4 + wl + 1) * 128],
                                    rhs_cache(kt * 6 + cc),
                                    start=(kt == 0 and ti == 0), stop=(kt == 7 and ti == 5),
                                    waits=w0, inc=[('te', 1)] if last else ())
                    lhs_tok[kp] = {'te': self.now('te')}
                for ri, (pd, idx) in enumerate(((pdr, i_r), (pdi, i_i))):
                    tev = self._ph0_evac(
                        pd,
                        self.s_a[ri * 2, ot, mh * 4:(mh + 1) * 4],
                        self.s_a[ri * 2 + 1, ot, mh * 4:(mh + 1) * 4],
                        first=first_ev, tiled=True)
                    first_ev = []
                    self.slot_free[idx] = {'ve': tev['ve'], 'se': 0}
        self.emit('sy', None, waits=[('pev', self.now('pev')), ('ph0', self.now('ph0')),
                                     ('ph1', self.now('ph1')), ('ph2', self.now('ph2')),
                                     ('ph3', self.now('ph3'))])
        terms2 = [(0, 0), (1, 0), (0, 1), (2, 2), (3, 2), (2, 3)]
        pev_s1 = self.now('pev')
        for oh in range(2):
            pc_tok = load_cache([(cn, ot * 128, oh * 512)
                                 for ot in range(8)
                                 for cn in ('cFrd_h', 'cFrd_l', 'cFid_h', 'cFid_l')])
            for mtl in range(8):
                idx = self.next_slot()
                pd = self.slots[idx]
                for ot in range(8):
                    kp = ot % 4
                    t_lhs = None
                    for ai in range(4):
                        t_lhs = self.dma('sy', sb['lstr'][:, (kp * 4 + ai) * 128:(kp * 4 + ai + 1) * 128],
                                         self.s_a[ai, ot, mtl],
                                         waits=([('te', lhs_tok[kp]['te'])] +
                                                ([('pev', pev_s1)] if self.fpo else []))
                                         if (ai == 0 or self.fpo) else (),
                                         sem=f'ph{kp}')
                    for ti, (al, cc) in enumerate(terms2):
                        w0 = []
                        if ti == 0:
                            w0 = [(f'ph{kp}', t_lhs[f'ph{kp}']), ('pc', pc_tok)]
                            if ot == 0:
                                w0 += [('ve', self.slot_free[idx]['ve']), ('se', self.slot_free[idx]['se'])]
                        self.mm(pd, sb['lstr'][:, (kp * 4 + al) * 128:(kp * 4 + al + 1) * 128],
                                rhs_cache(ot * 4 + cc),
                                start=(ot == 0 and ti == 0), stop=(ot == 7 and ti == 5),
                                waits=w0, inc=[('te', 1)] if ti == 5 else ())
                    lhs_tok[kp] = {'te': self.now('te')}
                tev = self._ph0_evac(
                    pd,
                    self.s_m[0, mtl, oh],
                    self.s_m[1, mtl, oh])
                self.slot_free[idx] = {'ve': tev['ve'], 'se': 0}

    def _dir(self, is_b):
        d, sb, tk = self.d, self.sb, self.tk
        self.emit('sy', None, waits=[('dsy', self.now('dsy')), ('pev', self.now('pev')),
                                     ('yf0', self.now('yf0')), ('yf1', self.now('yf1'))])
        dirw = 'b' if is_b else 'f'
        gd = tk['group_done']
        wo_gate = self.now('te')
        t_wo = None
        for kt in range(16):
            t_wo = self.dma('gp', sb['b_wo'][:, kt * 1024:(kt + 1) * 1024],
                            d[f'wo_{dirw}'][kt * 128:(kt + 1) * 128, :],
                            waits=[('te', wo_gate)] if (kt == 0 or self.fpo) else (), sem='wo')

        gdw = [('te', gd['te']), ('ve', gd['ve']), ('se', gd['se'])]
        self.dma('gp', sb['cw'][:, :].rearrange('p (tau ct) -> p tau ct', ct=18),
                 d[f'convw_{dirw}'].rearrange('tau (ct p) -> p tau ct', p=128), slow=True,
                 waits=gdw, sem='dp')
        gdw2 = gdw if self.fpo else []
        self.dma('gp', sb['cb'][:, :], d[f'convb_{dirw}'].rearrange('(ct p) -> p ct', p=128),
                 slow=True, waits=gdw2, sem='dp')
        t_al = self.dma('gp', sb['aexp'][:, :], d[f'alog_{dirw}'].rearrange('(a b) -> a b', b=1),
                        slow=True, waits=gdw2, sem='dp')
        self.dma('gp', sb['dtbc'][:, :], d[f'dtb_{dirw}'].rearrange('(a b) -> a b', b=1),
                 slow=True, waits=gdw2, sem='dp')
        t_dd = self.dma('gp', sb['ddiag'][:, :], d[f'ddg_{dirw}'],
                        waits=[('te', gd['te']), ('ve', gd['ve'])], sem='dd')
        tse = self.se(lambda nc, e: nc.scalar.activation(sb['negA'][:, :], sb['aexp'][:, :], AF.Exp),
                      waits=[('dp', self.now('dp'))], inc=[('se', 1)])
        t_na = self.ve(lambda nc, e: nc.vector.tensor_scalar_mul(sb['negA'][:, :], sb['negA'][:, :], -1.0),
                       waits=[('se', tse['se'])], inc=[('ve', 1)])
        t_h0 = self.ve(lambda nc, e: nc.vector.memset(sb['hst'][:, :], 0.0),
                       waits=[('te', tk['group_done']['te'])], inc=[('ve', 1)])
        self.ve(lambda nc, e: nc.vector.memset(sb['cstash'][:, :], 0.0), inc=[('ve', 1)])
        t_dirprep = {'dp': self.now('dp'), 've': self.now('ve')}

        tk_hup = t_h0
        tk_zsil_free = {'te': tk['group_done']['te'], 've': 0}
        tk_xbcs_free = {'te': tk['group_done']['te'], 'gp': tk['group_done']['gp']}
        tk_dt_free = {'ve': 0, 'co': 0}
        tk_xnT_free = {'te': tk['group_done']['te']}
        tk_cr_free = {'te': 0}
        tk_wv_users = {'te': 0}
        tk_mt_gp = {'gp': 0}
        tk_crc_users = {'te': 0}
        t_ddg = t_dd

        groups = list(range(NG - 1, -1, -1)) if is_b else list(range(NG))
        for g in groups:
            chunks = list(range(NCH - 1, -1, -1)) if is_b else list(range(NCH))
            tk_ln_sq = {'ve': getattr(self, '_conv_last', 0)}
            tk_ln_tr = {'te': 0}
            for c in chunks:
                gc = g * NCH + c
                par = gc % 2
                xqs = sb['xq'][:, par * 1024:(par + 1) * 1024]
                tld = self.dma('sy', xqs, d['x'][gc * 128:(gc + 1) * 128, :],
                               waits=[('ve', tk['xq_free'][par]['ve']),
                                      ('dp', t_dirprep['dp'])], sem=f'xq{par}')
                tsqr = self.se(lambda nc, e, xqs=xqs: nc.scalar.activation(
                    sb['wprep'][:, :], xqs, AF.Square),
                    waits=[(f'xq{par}', tld[f'xq{par}']), ('ve', tk_ln_sq['ve'])],
                    inc=[('se', 1)])
                def vc(fn, waits=()):
                    t = self.ve(fn, waits=list(waits) + [('ve', self.now('ve'))], inc=[('ve', 1)])
                    return t
                vc(lambda nc, e, xqs=xqs: nc.vector.reduce_sum(
                    sb['mean'][:, :], xqs, axis=mybir.AxisListType.X),
                    waits=[(f'xq{par}', tld[f'xq{par}'])])
                vc(lambda nc, e: nc.vector.tensor_scalar_mul(sb['mean'][:, :], sb['mean'][:, :], 1.0 / DM))
                tk_ln_sq = vc(lambda nc, e: nc.vector.reduce_sum(
                    sb['sqs'][:, :], sb['wprep'][:, :], axis=mybir.AxisListType.X),
                    waits=[('se', tsqr['se'])])
                vc(lambda nc, e: nc.vector.tensor_scalar_mul(sb['sqs'][:, :], sb['sqs'][:, :], 1.0 / DM))
                vc(lambda nc, e: nc.vector.tensor_mul(sb['rstd'][:, :], sb['mean'][:, :], sb['mean'][:, :]))
                tvar = vc(lambda nc, e: nc.vector.tensor_sub(sb['sqs'][:, :], sb['sqs'][:, :], sb['rstd'][:, :]))
                tsq = self.se(lambda nc, e: nc.scalar.activation(sb['rstd'][:, :], sb['sqs'][:, :], AF.Sqrt,
                                                                 bias=sb['epsc'][:, :]),
                              waits=[('ve', tvar['ve'])], inc=[('se', 1)])
                vc(lambda nc, e: nc.vector.reciprocal(sb['rstd'][:, :], sb['rstd'][:, :]),
                   waits=[('se', tsq['se'])])
                vc(lambda nc, e: nc.vector.tensor_mul(sb['nmrs'][:, :], sb['mean'][:, :], sb['rstd'][:, :]))
                vc(lambda nc, e: nc.vector.tensor_scalar_mul(sb['nmrs'][:, :], sb['nmrs'][:, :], -1.0))
                tnx = vc(lambda nc, e, xqs=xqs: nc.vector.tensor_scalar(
                    sb['xnch'][:, :], xqs, sb['rstd'][:, :], sb['nmrs'][:, :], ALU.mult, ALU.add),
                    waits=[('te', tk_ln_tr['te'])])
                ptok = {'xnch': tnx}
                if is_b:
                    vc(lambda nc, e, xqs=xqs: nc.vector.scalar_tensor_tensor(
                        sb['wprep'][:, :], xqs, sb['rstd'][:, :], sb['xnch'][:, :],
                        ALU.mult, ALU.subtract))
                    tnx = vc(lambda nc, e: nc.vector.tensor_scalar(
                        sb['xncl'][:, :], sb['wprep'][:, :], sb['nmrs'][:, :], None, ALU.add))
                    ptok['xncl'] = tnx
                self.tk['xq_free'][par] = tnx
                tk_ln_sq = {'ve': self.now('ve')}
                passes = [('xnch', 'xnT')] + ([('xncl', 'xnTl')] if is_b else [])
                for srcb, dstb in passes:
                    for kt in range(8):
                        self.trans(self.ps['pT'][:, kt * 128:(kt + 1) * 128],
                                   sb[srcb][:, kt * 128:(kt + 1) * 128],
                                   waits=([('ve', ptok[srcb]['ve']), ('ve', tk['pT_free']['ve']),
                                           ('te', tk_xnT_free['te'])] if kt == 0 else ()),
                                   inc=[('te', 1)] if kt == 7 else ())
                    tkte = self.now('te')
                    tk_ln_tr = {'te': tkte}
                    dst = sb[dstb][:, :].rearrange('p (kt t) -> p kt t', t=512)[:, :, c * 128:(c + 1) * 128]
                    tk['pT_free'] = self.ve(lambda nc, e, dst=dst: nc.vector.tensor_copy(
                        dst, self.ps['pT'][:, :].rearrange('p (kt t) -> p kt t', t=128)),
                        waits=[('te', tkte)], inc=[('ve', 1)])
            tk_ln = {'ve': self.now('ve')}

            jorder = [34] + list(range(16, 34)) + list(range(16))
            t_wld = {}
            def load_wi(ii):
                j = jorder[ii]
                jw = PROJ - (NJ - 1) * 128 if j == NJ - 1 else 128
                bpar = ii % 4
                t_wld[j] = self.dma('sy', sb['b_wi'][:, bpar * 1024: bpar * 1024 + 8 * jw]
                                    .rearrange('p (kt jj) -> p kt jj', jj=jw),
                                    self.s_wi[dirw][j, :, :, 0:jw],
                                    waits=[('te', tk['wi_free'][bpar]['te'])] +
                                    (self._sst if self.fpo else []),
                                    sem=f'wi{bpar}')
            load_wi(0); load_wi(1); load_wi(2); load_wi(3)
            par_evac = [dict(self.slot_free[i]) for i in range(4)]
            tk_conv_done = {}
            for ii, j in enumerate(jorder):
                jw = PROJ - (NJ - 1) * 128 if j == NJ - 1 else 128
                ppar = ii % 4
                bsl = ii % 4
                pd = self.slots[ppar] if jw == 128 else self.slots[ppar][0:32, :]
                for kt in range(8):
                    w0 = []
                    if kt == 0:
                        w0 = [(f'wi{bsl}', t_wld[j][f'wi{bsl}']), ('ve', tk_ln['ve']),
                              ('se', par_evac[ppar]['se']), ('ve', par_evac[ppar]['ve'])]
                    self.mm(pd, sb['b_wi'][:, bsl * 1024 + kt * jw: bsl * 1024 + (kt + 1) * jw],
                            sb['xnT'][:, kt * 512:(kt + 1) * 512],
                            start=(kt == 0), stop=(kt == 7),
                            waits=w0, inc=[('te', 1)] if kt == 7 else ())
                tk['wi_free'][bsl] = {'te': self.now('te')}
                if ii + 4 < NJ:
                    load_wi(ii + 4)
                tkte = self.now('te')
                if j < 16:
                    tz = self.se(lambda nc, e, j=j, ppar=ppar: nc.scalar.activation(
                        sb['zsil'][:, j * 512:(j + 1) * 512], self.slots[ppar][:, :], AF.Silu),
                        waits=[('te', tkte), ('te', tk_zsil_free['te']),
                               ('ve', tk_zsil_free['ve'])], inc=[('se', 1)])
                    par_evac[ppar] = {'se': tz['se'], 've': 0}
                elif j < 34:
                    ct = j - 16
                    cpar = ct % 2
                    if not is_b:
                        data_off = cpar * 520 + 3; stash_off = cpar * 520
                        save_off = cpar * 520 + 512; tap = [0, 1, 2, 3]
                    else:
                        data_off = cpar * 520; stash_off = cpar * 520 + 512
                        save_off = cpar * 520; tap = [3, 2, 1, 0]
                    tx = self.se(lambda nc, e, ppar=ppar, o=data_off: nc.scalar.activation(
                        sb['tmpc'][:, o:o + 512], self.slots[ppar], AF.Copy),
                        waits=[('te', tkte),
                               ('ve', tk_conv_done.get(ct - 2, {'ve': 0})['ve'])],
                        inc=[('se', 1)])
                    par_evac[ppar] = {'se': tx['se'], 've': 0}
                    self.ve(lambda nc, e, ct=ct, so=stash_off: nc.vector.tensor_copy(
                        sb['tmpc'][:, so:so + 3], sb['cstash'][:, ct * 3:ct * 3 + 3]),
                        waits=[('se', tx['se'])])
                    base = cpar * 520
                    self.ve(lambda nc, e, b=base, ct=ct, t0=tap[0], cpar=cpar: nc.vector.tensor_scalar(
                        sb['convt'][:, cpar * 512:(cpar + 1) * 512], sb['tmpc'][:, b + t0: b + t0 + 512],
                        sb['cw'][:, ct:ct + 1], None, ALU.mult))
                    for tau in (1, 2, 3):
                        self.ve(lambda nc, e, b=base, ct=ct, tau=tau, to=tap[tau], cpar=cpar:
                                nc.vector.scalar_tensor_tensor(
                                    sb['convt'][:, cpar * 512:(cpar + 1) * 512],
                                    sb['tmpc'][:, b + to: b + to + 512],
                                    sb['cw'][:, tau * 18 + ct: tau * 18 + ct + 1],
                                    sb['convt'][:, cpar * 512:(cpar + 1) * 512], ALU.mult, ALU.add))
                    tcv = self.ve(lambda nc, e, ct=ct, so=save_off: nc.vector.tensor_copy(
                        sb['cstash'][:, ct * 3:ct * 3 + 3], sb['tmpc'][:, so:so + 3]),
                        inc=[('ve', 1)])
                    tk_conv_done[ct] = tcv
                    self.se(lambda nc, e, ct=ct, cpar=cpar: nc.scalar.activation(
                        sb['xbcs'][:, ct * 512:(ct + 1) * 512], sb['convt'][:, cpar * 512:(cpar + 1) * 512],
                        AF.Silu, bias=sb['cb'][:, ct:ct + 1]),
                        waits=[('ve', tcv['ve']), ('te', tk_xbcs_free['te']),
                               ('gp', tk_xbcs_free['gp'])], inc=[('se', 1)])
                else:
                    self.se(lambda nc, e, ppar=ppar: nc.scalar.activation(
                        sb['lgda'][:, :], self.slots[ppar][0:32, :], AF.Abs,
                        bias=sb['dtbc'][:, :]),
                        waits=[('te', tkte), ('ve', tk_dt_free['ve']),
                               ('co', tk_dt_free['co']), ('dp', t_dirprep['dp'])],
                        inc=[('se', 1)])
                    self.se(lambda nc, e: nc.scalar.activation(
                        sb['cum'][:, :], sb['lgda'][:, :], AF.Exp, scale=-1.0))
                    self.se(lambda nc, e: nc.scalar.activation(
                        sb['cum'][:, :], sb['cum'][:, :], AF.Ln, bias=sb['onesf'][0:32, 0:1]))
                    tdt = self.se(lambda nc, e, ppar=ppar: nc.scalar.activation(
                        sb['dt_'][:, :], self.slots[ppar][0:32, :], AF.Relu,
                        bias=sb['dtbc'][:, :]), inc=[('se', 1)])
                    par_evac[ppar] = {'se': tdt['se'], 've': 0}
                    self._t_dtadd = self.ve(lambda nc, e: nc.vector.tensor_add(
                        sb['dt_'][:, :], sb['dt_'][:, :], sb['cum'][:, :]),
                        waits=[('se', tdt['se'])], inc=[('ve', 1)])
                if j == 33:
                    conv_se = self.now('se')
            self._conv_last = tk_conv_done[17]['ve']
            tk_sweep = {'se': self.now('se'), 've': self.now('ve'), 'te': self.now('te')}
            for _i in range(4):
                self.slot_free[_i] = {'ve': 0, 'se': tk_sweep['se']}

            tlgda = self.ve(lambda nc, e: nc.vector.tensor_scalar(
                sb['lgda'][:, :], sb['dt_'][:, :], sb['negA'][:, :], None, ALU.mult),
                waits=[('ve', self._t_dtadd['ve']), ('ve', t_na['ve']),
                       ('se', getattr(self, '_dtp_se', 0))], inc=[('ve', 1)])
            tscan = tlgda
            for c in range(NCH):
                tscan = self.ve(lambda nc, e, c=c: nc.vector.tensor_tensor_scan(
                    sb['cum'][:, c * 128:(c + 1) * 128], sb['lgda'][:, c * 128:(c + 1) * 128],
                    sb['lgda'][:, c * 128:(c + 1) * 128], 0.0, ALU.add, ALU.bypass),
                    waits=[('ve', tscan['ve'])], inc=[('ve', 1)])
            if is_b:
                self.ve(lambda nc, e: nc.vector.tensor_sub(sb['cump'][:, :], sb['cum'][:, :], sb['lgda'][:, :]),
                        waits=[('ve', tscan['ve'])])
            tlg = self.se(lambda nc, e: nc.scalar.activation(sb['lgdt'][:, :], sb['dt_'][:, :], AF.Ln),
                          waits=[('ve', self._t_dtadd['ve'])], inc=[('se', 1)])
            CC = sb['cump'] if is_b else sb['cum']
            self.ve(lambda nc, e, CC=CC: nc.vector.tensor_copy(sb['cumhi'][:, :], CC[:, :]),
                    waits=[('se', tlg['se']), ('pk', self.now('pk')), ('sc', self.now('sc')),
                           ('ve', tscan['ve'])])
            self.ve(lambda nc, e, CC=CC: nc.vector.tensor_tensor(
                sb['cumlo'][:, :], CC[:, :], sb['cumhi'][:, :], ALU.subtract))
            self.ve(lambda nc, e: nc.vector.tensor_copy(sb['ldthi'][:, :], sb['lgdt'][:, :]))
            self.ve(lambda nc, e: nc.vector.tensor_tensor(
                sb['ldtlo'][:, :], sb['lgdt'][:, :], sb['ldthi'][:, :], ALU.subtract))
            if is_b:
                self.ve(lambda nc, e: nc.vector.tensor_add(sb['wvin'][:, :], sb['lgdt'][:, :], sb['cump'][:, :]),
                        waits=[('se', tlg['se'])], inc=[('ve', 1)])
            else:
                self.ve(lambda nc, e: nc.vector.tensor_sub(sb['wvin'][:, :], sb['lgdt'][:, :], sb['cum'][:, :]),
                        waits=[('se', tlg['se'])], inc=[('ve', 1)])
            for c in range(NCH):
                kw = dict(bias=sb['cum'][:, c * 128 + 127:c * 128 + 128]) if not is_b else {}
                self.se(lambda nc, e, c=c, kw=kw: nc.scalar.activation(
                    sb['wv'][:, c * 128:(c + 1) * 128], sb['wvin'][:, c * 128:(c + 1) * 128],
                    AF.Exp, **kw),
                    waits=[('ve', self.now('ve')), ('te', tk_wv_users['te'])], inc=[('se', 1)])
            tpq = self.se(lambda nc, e: nc.scalar.activation(
                sb['pqb'][:, :], sb['cum'][:, 127:512:128], AF.Exp),
                waits=[('ve', tscan['ve']), ('sc', self.now('sc'))], inc=[('se', 1)])
            if is_b:
                for c in range(NCH):
                    def vc2(fn, waits=()):
                        return self.ve(fn, waits=list(waits) + [('ve', self.now('ve'))],
                                       inc=[('ve', 1)])
                    vc2(lambda nc, e, c=c: nc.vector.tensor_copy(
                        sb['clsc'][:, 0:1], sb['cum'][:, c * 128 + 127:c * 128 + 128]),
                        waits=[('se', tpq['se'])])
                    vc2(lambda nc, e, c=c: nc.vector.tensor_scalar(
                        sb['clh'][:, c * 128:(c + 1) * 128], sb['onesf'][:, 0:128],
                        sb['clsc'][:, 0:1], None, ALU.mult))
                    vc2(lambda nc, e, c=c: nc.vector.tensor_tensor(
                        sb['clsc'][:, 1:2], sb['clsc'][:, 0:1],
                        sb['clh'][:, c * 128:c * 128 + 1], ALU.subtract))
                    vc2(lambda nc, e, c=c: nc.vector.tensor_scalar(
                        sb['cll'][:, c * 128:(c + 1) * 128], sb['onesf'][:, 0:128],
                        sb['clsc'][:, 1:2], None, ALU.mult))
            tk_dtp = {'ve': self.now('ve'), 'se': self.now('se')}
            self._dtp_se = tk_dtp['se']
            wcr = [('ve', tk_dtp['ve']), ('se', tk_dtp['se']), ('te', tk_cr_free['te'])]
            if self.fpo:
                wcr = wcr + [('em', self.now('em'))]
            for qq, srcb in enumerate(('cumhi', 'cumlo', 'ldthi', 'ldtlo')):
                self.dma('gp', sb['pk32'][qq:128:4, :], sb[srcb][:, :],
                         waits=wcr if (qq == 0 or self.fpo) else (), sem='pk')
            tk_pk = {'pk': self.now('pk')}
            crstores = [(0, 'cumhi'), (1, 'cumlo')] + ([(2, 'clh'), (3, 'cll')] if is_b else [])
            wsc = [('ve', tk_dtp['ve']), ('se', tk_dtp['se']),
                   ('cr', self.now('cr')), ('co', self.now('co'))] if self.fpo else []
            for row, srcb in crstores:
                self.dma('gp', self.s_cr[row, :, :], sb[srcb][:, :], waits=wsc, sem='sc')
            self.dma('gp', self.s_cr[4, :, 0:4], sb['pqb'][:, :], waits=wsc, sem='sc')
            tk_crst = {'sc': self.now('sc')}
            t_coll = self.dma('gp', sb['pqrow'][0:1, :].rearrange('a (p c) -> a p c', c=4),
                              self.s_cr[4, :, 0:4].unsqueeze(0),
                              waits=[('sc', tk_crst['sc'])] +
                              ([('te', tk_cr_free['te'])] if self.fpo else []), sem='co')
            tk_coll = {'co': self.now('co')}
            tk_dt_free = {'ve': tk_dtp['ve'], 'co': tk_coll['co']}

            for ci, c in enumerate(chunks):
                gc = g * NCH + c
                par = gc % 2
                crpairs = [(0, 'cumhi'), (1, 'cumlo')] + ([(2, 'clh'), (3, 'cll')] if is_b else [])
                for row, srcb in crpairs:
                    self.dma('gp', sb['crc'][row:row + 1, :].rearrange('a (h t) -> a h t', t=128),
                             self.s_cr[row, :, c * 128:(c + 1) * 128].unsqueeze(0),
                             waits=[('te', tk_crc_users['te']), ('sc', tk_crst['sc'])]
                             if (row == 0 or self.fpo) else (),
                             sem='cr')
                tk_coll_c = {'cr': self.now('cr')}
                for bb8 in range(8):
                    self.dma('gp', sb['emc'][:, bb8 * 128:(bb8 + 1) * 128],
                             sb['pk32'][bb8 * 16:(bb8 + 1) * 16, c * 128:(c + 1) * 128],
                             waits=[('te', tk_crc_users['te']), ('pk', tk_pk['pk'])]
                             if (bb8 == 0 or self.fpo) else (),
                             sem='em')
                tk_em = {'em': self.now('em')}
                Btile = sb['xbcs'][:, 16 * 512 + c * 128: 16 * 512 + c * 128 + 128]
                Ctile = sb['xbcs'][:, 17 * 512 + c * 128: 17 * 512 + c * 128 + 128]
                for half in range(2):
                    for k8 in range(8):
                        cti = half * 8 + k8
                        self.trans(self.ps['pT'][:, k8 * 128:(k8 + 1) * 128],
                                   sb['xbcs'][:, cti * 512 + c * 128: cti * 512 + c * 128 + 128],
                                   waits=([('se', conv_se), ('ve', tk['pT_free']['ve']),
                                           ('gp', self.now('gp'))] if k8 == 0 else ()),
                                   inc=[('te', 1)] if k8 == 7 else ())
                    tkte = self.now('te')
                    tk['pT_free'] = self.ve(lambda nc, e, half=half: nc.vector.tensor_copy(
                        sb['xt'][:, half * 1024:(half + 1) * 1024], self.ps['pT'][:, :]),
                        waits=[('te', tkte)], inc=[('ve', 1)])
                self.trans(self.ps['pT'][:, 0:128], Btile,
                           waits=[('ve', tk['pT_free']['ve'])], inc=[('te', 1)])
                self.trans(self.ps['pT'][:, 128:160], sb['wv'][0:32, c * 128:(c + 1) * 128],
                           waits=[('se', tk_dtp['se'])], inc=[('te', 1)])
                tkte = self.now('te')
                tk_wv_users = {'te': tkte}
                self.ve(lambda nc, e: nc.vector.tensor_copy(sb['Bt'][:, :], self.ps['pT'][:, 0:128]),
                        waits=[('te', tkte)])
                tk['pT_free'] = self.ve(lambda nc, e: nc.vector.tensor_copy(
                    sb['wct'][:, :], self.ps['pT'][:, 128:160]), inc=[('ve', 1)])
                t_xt = tk['pT_free']
                isl = self.next_slot()
                gsl = self.slots[isl]
                self.mm(gsl[:, 0:128], Btile, Ctile, start=True, stop=True,
                        waits=[('ve', self.slot_free[isl]['ve']), ('se', self.slot_free[isl]['se']),
                               ('se', conv_se)], inc=[('te', 1)])
                tgm = self.ve(lambda nc, e, gsl=gsl, m=('triB' if is_b else 'triF'):
                              nc.vector.tensor_tensor(sb['gmask'][:, :], gsl[:, 0:128],
                                                      sb[m][:, :], ALU.mult),
                              waits=[('te', self.now('te'))], inc=[('ve', 1)])
                self.slot_free[isl] = {'ve': tgm['ve'], 'se': 0}
                grep = sb['gmask'][:, :].unsqueeze(1).broadcast_to([128, 16, 128])
                crep = Ctile.unsqueeze(1).broadcast_to([128, 16, 128])
                se_exp_tok = {'se': 0}
                tok_mt, tok_mv = [], []
                for half in range(2):
                    mtb = sb['mt' if half == 0 else 'mt2']
                    lvb = sb['lv' if half == 0 else 'lv2']
                    lv_tok = None
                    for bb in range(4):
                        b = half * 4 + bb
                        i_e = self.next_slot()
                        pe = self.slots[i_e]
                        crs = sb['crc'][0:2, b * 512:(b + 1) * 512]
                        self.mm(pe, sb['ones2n' if is_b else 'ones2'][:, :], crs,
                                start=True, stop=False,
                                waits=[('cr', tk_coll_c['cr']), ('em', tk_em['em']),
                                       ('ve', self.slot_free[i_e]['ve']),
                                       ('se', self.slot_free[i_e]['se'])])
                        self.mm(pe, sb['emc'][0:16, b * 128: b * 128 + 128],
                                sb['indB' if is_b else 'indF'][:, :], start=False, stop=True,
                                inc=[('te', 1)])
                        tke = self.now('te')
                        i_v = self.next_slot()
                        pv = self.slots[i_v]
                        if is_b:
                            self.mm(pv, sb['evB'][:, :], sb['crc'][0:4, b * 512:(b + 1) * 512],
                                    start=True, stop=True,
                                    waits=[('ve', self.slot_free[i_v]['ve']),
                                           ('se', self.slot_free[i_v]['se'])], inc=[('te', 1)])
                        else:
                            self.mm(pv, sb['ones2'][:, :], crs, start=True, stop=True,
                                    waits=[('ve', self.slot_free[i_v]['ve']),
                                           ('se', self.slot_free[i_v]['se'])], inc=[('te', 1)])
                        tkv = self.now('te')
                        e_tok = self.ve(lambda nc, e, pe=pe: nc.vector.tensor_scalar_min(
                            pe, pe, 30.0),
                            waits=[('te', tke)], inc=[('ve', 1)])
                        if self.debug and is_b and gc == 30 and half == 0 and bb == 0:
                            self.dbg_dump([('decl', pe)])
                        w_exp = [('ve', e_tok['ve'])]
                        if bb == 0:
                            w_exp.append(('te', tk['ymm_free']['te']))
                        se_exp_tok = self.se(lambda nc, e, bb=bb, mtb=mtb, pe=pe: nc.scalar.activation(
                            mtb[:, bb * 512:(bb + 1) * 512],
                            pe, AF.Exp),
                            waits=w_exp, inc=[('se', 1)])
                        self.slot_free[i_e] = {'ve': 0, 'se': se_exp_tok['se']}
                        lv_tok = self.se(lambda nc, e, pv=pv, bb=bb, lvb=lvb: nc.scalar.activation(
                            lvb[:, bb * 512:(bb + 1) * 512], pv, AF.Exp),
                            waits=[('te', tkv)], inc=[('se', 1)])
                        self.slot_free[i_v] = {'ve': 0, 'se': lv_tok['se']}
                    self.ve(lambda nc, e, grep=grep, mtb=mtb: nc.vector.tensor_tensor(
                        mtb[:, :].rearrange('p (h t) -> p h t', t=128),
                        mtb[:, :].rearrange('p (h t) -> p h t', t=128), grep, ALU.mult),
                        waits=[('se', se_exp_tok['se']), ('ve', tgm['ve'])])
                    tok_mt.append(self.ve(lambda nc, e, half=half, mtb=mtb: nc.vector.tensor_tensor(
                        mtb[:, :], mtb[:, :],
                        sb['ddiag'][:, half * 2048:(half + 1) * 2048], ALU.add),
                        waits=[('dd', t_ddg['dd'])], inc=[('ve', 1)]))
                    tok_mv.append(self.ve(lambda nc, e, crep=crep, lvb=lvb: nc.vector.tensor_tensor(
                        lvb[:, :].rearrange('p (h t) -> p h t', t=128),
                        lvb[:, :].rearrange('p (h t) -> p h t', t=128), crep, ALU.mult),
                        waits=[('se', lv_tok['se']), ('se', conv_se)], inc=[('ve', 1)]))
                tk_crc_users = {'te': self.now('te')}
                if self.debug and is_b and gc == 30:
                    self.dbg_dump([('dlexp', sb['mt2'][:, :]), ('dlv', sb['lv'][:, :]),
                                   ('dmt', sb['mt'][:, :]),
                                   ('dhprev', sb['hst'][:, :])])
                for half in range(2):
                    mtb = sb['mt' if half == 0 else 'mt2']
                    lvb = sb['lv' if half == 0 else 'lv2']
                    tmt, tmv = tok_mt[half], tok_mv[half]
                    for jj in range(8):
                        j = half * 8 + jj
                        h0, h1 = 2 * j, 2 * j + 1
                        l0, l1 = 2 * jj, 2 * jj + 1
                        slot = self.ps['pY'][:, jj * 128:(jj + 1) * 128]
                        w0 = [('ve', tmt['ve']), ('ve', tmv['ve']), ('ve', tk['pY_free']['ve']),
                              ('ve', tk_hup['ve']), ('ve', t_xt['ve'])] if jj == 0 else ()
                        self.mm(slot[0:64, :], sb['xt'][:, h0 * 64:(h0 + 1) * 64],
                                mtb[:, l0 * 128:(l0 + 1) * 128], True, False, waits=w0, tp=(0, 0))
                        self.mm(slot[0:64, :], sb['hst'][:, h0 * 64:(h0 + 1) * 64],
                                lvb[:, l0 * 128:(l0 + 1) * 128], False, True, tp=(0, 0))
                        self.mm(slot[64:128, :], sb['xt'][:, h1 * 64:(h1 + 1) * 64],
                                mtb[:, l1 * 128:(l1 + 1) * 128], True, False, tp=(0, 64))
                        self.mm(slot[64:128, :], sb['hst'][:, h1 * 64:(h1 + 1) * 64],
                                lvb[:, l1 * 128:(l1 + 1) * 128], False, True, tp=(0, 64),
                                inc=[('te', 1)] if jj == 7 else ())
                    tk_ymm = {'te': self.now('te')}
                    tk['ymm_free'] = tk_ymm
                    tk['hbf_users'] = tk_ymm
                    zslc = sb['zsil'][:, :].rearrange('p (j t) -> p j t', t=512)[
                        :, half * 8:(half + 1) * 8, c * 128:(c + 1) * 128]
                    tk['pY_free'] = self.ve(lambda nc, e, zslc=zslc, par=par, half=half:
                                            nc.vector.tensor_tensor(
                        sb['yg'][:, par * 2048 + half * 1024: par * 2048 + (half + 1) * 1024]
                        .rearrange('p (j t) -> p j t', t=128),
                        self.ps['pY'][:, :].rearrange('p (j t) -> p j t', t=128),
                        zslc, ALU.mult),
                        waits=[('te', tk_ymm['te']), ('te', tk['yg_free'][par]['te']),
                               ('se', tk_sweep['se'])], inc=[('ve', 1)])
                tk_zsil_users = tk['pY_free']
                i_p = self.next_slot()
                psl = self.slots[i_p]
                self.mm(psl[:, 0:32], sb['ones2'][0:1, :], sb['pqrow'][0:1, :].rearrange('a (p c) -> a p c', c=4)[:, :, c], True, True,
                        waits=[('co', tk_coll['co']), ('ve', self.slot_free[i_p]['ve']),
                               ('se', self.slot_free[i_p]['se'])], inc=[('te', 1)])
                tpqb = self.ve(lambda nc, e, psl=psl: nc.vector.tensor_copy(
                    sb['pqbc'][:, :], psl[:, 0:32]),
                    waits=[('te', self.now('te'))], inc=[('ve', 1)])
                self.slot_free[i_p] = {'ve': tpqb['ve'], 'se': 0}
                tk_xw = self.ve(lambda nc, e: nc.vector.tensor_tensor(
                    sb['mt'][:, :].rearrange('p (h q) -> p h q', q=64),
                    sb['xt'][:, :].rearrange('p (h q) -> p h q', q=64),
                    sb['wct'][:, :].unsqueeze(2).broadcast_to([128, NH, 64]), ALU.mult),
                    waits=[('ve', t_xt['ve']), ('te', tk_ymm['te'])], inc=[('ve', 1)])
                tk_mt_gp = {'gp': 0}
                for qq in range(4):
                    i_h = self.next_slot()
                    hs = self.slots[i_h]
                    w0 = [('ve', self.slot_free[i_h]['ve']), ('se', self.slot_free[i_h]['se'])]
                    if qq == 0:
                        w0.append(('ve', tk_xw['ve']))
                    self.mm(hs, sb['Bt'][:, :], sb['mt'][:, qq * 512:(qq + 1) * 512], True, True,
                            waits=w0, inc=[('te', 1)])
                    tkh = self.now('te')
                    hsl = sb['hst'][:, qq * 512:(qq + 1) * 512]
                    self.ve(lambda nc, e, hsl=hsl, qq=qq: nc.vector.tensor_tensor(
                        hsl.rearrange('p (h q) -> p h q', q=64),
                        hsl.rearrange('p (h q) -> p h q', q=64),
                        sb['pqbc'][:, qq * 8:(qq + 1) * 8].unsqueeze(2).broadcast_to([128, 8, 64]),
                        ALU.mult),
                        waits=[('te', tkh), ('ve', tpqb['ve'])])
                    tup = self.ve(lambda nc, e, hsl=hsl, hs=hs: nc.vector.tensor_tensor(
                        hsl, hsl, hs, ALU.add), inc=[('ve', 1)])
                    self.slot_free[i_h] = {'ve': tup['ve'], 'se': 0}
                tk['ymm_free'] = {'te': self.now('te')}
                tk_hup = {'ve': self.now('ve')}
                for dh in range(2):
                    for kt in range(16):
                        w0 = [('ve', tk['pY_free']['ve']), ('wo', t_wo['wo'])] \
                            if (dh == 0 and kt == 0) else ()
                        self.mm(self.ps['pY'][:, dh * 512:(dh + 1) * 512],
                                sb['yg'][:, par * 2048 + kt * 128: par * 2048 + (kt + 1) * 128],
                                sb['b_wo'][:, kt * 1024 + dh * 512: kt * 1024 + dh * 512 + 512],
                                start=(kt == 0), stop=(kt == 15),
                                waits=w0, inc=[('te', 1)] if (kt == 15 and dh == 1) else ())
                tk_wo = {'te': self.now('te')}
                tk['yg_free'][par] = tk_wo
                if self.debug and is_b and gc == 30:
                    self.dbg_dump([('dxnT', sb['xnT'][:, :]), ('dzsil', sb['zsil'][:, :]),
                                   ('dxbcs', sb['xbcs'][:, :]), ('ddt', sb['dt_'][:, :]),
                                   ('dcum', sb['cum'][:, :]),
                                   ('dgm', sb['gmask'][:, :]),
                                   ('dyg', sb['yg'][:, 0:2048]),
                                   ('demega', sb['pk32'][:, :]), ('dcrc', sb['crc'][:, :]),
                                   ('dwv', sb['wv'][:, :]), ('dxt', sb['xt'][:, :]),
                                   ('dhst', sb['hst'][:, :])])
                if not is_b:
                    tev = self.ve(lambda nc, e, par=par: nc.vector.tensor_copy(
                        sb['yfc'][:, par * 1024:(par + 1) * 1024], self.ps['pY'][:, :]),
                        waits=[('te', tk_wo['te']), (f'yf{par}', tk['yfc_free'][par]['st'])],
                        inc=[('ve', 1)])
                    tk['pY_free'] = tev
                    tst = self.dma('sy', self.s_yf[gc * 128:(gc + 1) * 128, :],
                                   sb['yfc'][:, par * 1024:(par + 1) * 1024],
                                   waits=[('ve', tev['ve'])], sem=f'yf{par}')
                    tk['yfc_free'][par] = {'st': tst[f'yf{par}'], 've': tev['ve']}
                else:
                    tldy = self.dma('sy', sb['yfc'][:, par * 1024:(par + 1) * 1024],
                                    self.s_yf[gc * 128:(gc + 1) * 128, :],
                                    waits=[('ve', tk['yfc_free'][par]['ve'])] +
                                    ([(f'yf{par}', self.now(f'yf{par}'))] if self.fpo else []),
                                    sem=f'yl{par}')
                    tev = self.ve(lambda nc, e, par=par: nc.vector.tensor_tensor(
                        sb['xm'][:, par * 1024:(par + 1) * 1024], self.ps['pY'][:, :],
                        sb['yfc'][:, par * 1024:(par + 1) * 1024], ALU.add),
                        waits=[('te', tk_wo['te']), (f'yl{par}', tldy[f'yl{par}']),
                               ('ve', tk['xm_free'][par]['ve'])],
                        inc=[('ve', 1)])
                    tk['pY_free'] = tev
                    tk['yfc_free'][par] = {'st': tk['yfc_free'][par]['st'], 've': tev['ve']}
                    tk['xm_full'][par] = tev
                    if ci % 2 == 1:
                        self._final_pair(g, [chunks[ci - 1], c])
            tk_zsil_free = {'te': self.now('te'), 've': tk_zsil_users['ve']}
            tk_xbcs_free = {'te': self.now('te'), 'gp': self.now('gp')}
            tk_cr_free = {'te': self.now('te')}
            tk_xnT_free = {'te': self.now('te')}
        self.tk['group_done'] = {'te': self.now('te'), 've': self.now('ve'),
                                 'se': self.now('se'), 'gp': self.now('gp')}

    def _final_pair(self, g, ccs):
        sb, tk, d = self.sb, self.tk, self.d
        for idx, c in enumerate(ccs):
            gc = g * NCH + c
            par = gc % 2
            txb = self.ve(lambda nc, e, par=par: nc.vector.tensor_copy(
                sb['xmb'][:, :], sb['xm'][:, par * 1024:(par + 1) * 1024]),
                waits=[('ve', tk['xm_full'][par]['ve'])], inc=[('ve', 1)])
            for kt in range(8):
                self.trans(self.ps['pT'][:, kt * 128:(kt + 1) * 128],
                           sb['xmb'][:, kt * 128:(kt + 1) * 128],
                           waits=([('ve', txb['ve']), ('ve', tk['pT_free']['ve'])] if kt == 0 else ()),
                           inc=[('te', 1)] if kt == 7 else ())
            tk['pT_free'] = self.ve(lambda nc, e: nc.vector.tensor_copy(
                sb['xmT'][:, :], self.ps['pT'][:, :]),
                waits=[('te', self.now('te'))], inc=[('ve', 1)])
            txt = tk['pT_free']
            for dh in range(2):
                i_w = self.next_slot()
                pw = self.slots[i_w]
                for kt in range(8):
                    sl = kt % 4
                    twm = self.dma('gp', sb['b_m'][:, sl * 512: (sl + 1) * 512],
                                   d['wm'].rearrange('(kt p) n -> kt p n', p=128)[kt, :, dh * 512:(dh + 1) * 512],
                                   waits=[('te', tk['bm_free'][sl]['te'])], sem=f'wm{sl}')
                    w0 = [(f'wm{sl}', twm[f'wm{sl}'])]
                    if kt == 0:
                        w0 += [('ve', self.slot_free[i_w]['ve']), ('se', self.slot_free[i_w]['se']),
                               ('ve', txt['ve'])]
                    self.mm(pw, sb['xmT'][:, kt * 128:(kt + 1) * 128],
                            sb['b_m'][:, sl * 512:(sl + 1) * 512],
                            start=(kt == 0), stop=False, waits=w0,
                            inc=[('te', 1)])
                    tk['bm_free'][sl] = {'te': self.now('te')}
                self.mm(pw, sb['ones2'][:, :], sb['bmr'][:, dh * 512:(dh + 1) * 512],
                        start=False, stop=True, inc=[('te', 1)])
                tth = self.se(lambda nc, e, pw=pw, idx=idx, dh=dh: nc.scalar.activation(
                    sb['t1'][:, idx * 1024 + dh * 512: idx * 1024 + (dh + 1) * 512], pw, AF.Tanh),
                    waits=[('te', self.now('te')),
                           (f'do{idx}', tk['outb_free'][idx].get('do', 0))], inc=[('se', 1)])
                self.slot_free[i_w] = {'ve': 0, 'se': tth['se']}
        tk_t1 = {'se': self.now('se')}
        for dh in range(2):
            for kt in range(8):
                sl = kt % 4
                pevw = [('pev', self.now('pev'))] if self.fpo else []
                tmh = self.dma('sy', sb['b_m'][:, sl * 512: (sl + 1) * 512],
                               self.s_m[0, kt, dh],
                               waits=[('te', tk['bm_free'][sl]['te'])] + pevw,
                               sem=f'fm{sl}')
                tml = self.dma('sy', sb['b_m'][:, 2048 + sl * 512: 2048 + (sl + 1) * 512],
                               self.s_m[1, kt, dh],
                               waits=[('te', tk['bm_free'][4 + sl]['te'])] + pevw,
                               sem=f'fl{sl}')
                for idx, c in enumerate(ccs):
                    w0 = [(f'fm{sl}', tmh[f'fm{sl}']), (f'fl{sl}', tml[f'fl{sl}'])]
                    if kt == 0:
                        w0.append(('ve', tk['pY_free']['ve']))
                    self.mm(self.ps['pY'][:, idx * 512:(idx + 1) * 512],
                            sb['xnT'][:, kt * 512 + c * 128: kt * 512 + c * 128 + 128],
                            sb['b_m'][:, sl * 512:(sl + 1) * 512],
                            start=(kt == 0), stop=False, waits=w0)
                    self.mm(self.ps['pY'][:, idx * 512:(idx + 1) * 512],
                            sb['xnTl'][:, kt * 512 + c * 128: kt * 512 + c * 128 + 128],
                            sb['b_m'][:, sl * 512:(sl + 1) * 512], False, False,
                            inc=[('te', 1)] if idx == 1 else ())
                    if idx == 1:
                        tk['bm_free'][sl] = {'te': self.now('te')}
                    self.mm(self.ps['pY'][:, idx * 512:(idx + 1) * 512],
                            sb['xnT'][:, kt * 512 + c * 128: kt * 512 + c * 128 + 128],
                            sb['b_m'][:, 2048 + sl * 512: 2048 + (sl + 1) * 512], False, False,
                            inc=[('te', 1)] if idx == 1 else ())
                    if idx == 1:
                        tk['bm_free'][4 + sl] = {'te': self.now('te')}
            for idx, c in enumerate(ccs):
                self.mm(self.ps['pY'][:, idx * 512:(idx + 1) * 512],
                        sb['ones2'][:, :], sb['fbr'][:, dh * 512:(dh + 1) * 512],
                        False, True, inc=[('te', 1)])
            tkff = {'te': self.now('te')}
            tlast = None
            for idx, c in enumerate(ccs):
                tlast = self.ve(lambda nc, e, idx=idx, dh=dh: nc.vector.scalar_tensor_tensor(
                    sb['t1'][:, idx * 1024 + dh * 512: idx * 1024 + (dh + 1) * 512],
                    sb['t1'][:, idx * 1024 + dh * 512: idx * 1024 + (dh + 1) * 512],
                    0.1, self.ps['pY'][:, idx * 512:(idx + 1) * 512], ALU.mult, ALU.add),
                    waits=[('te', tkff['te']), ('se', tk_t1['se'])], inc=[('ve', 1)] if idx == 1 else ())
            tk['pY_free'] = {'ve': self.now('ve')}
        for idx, c in enumerate(ccs):
            gc = g * NCH + c
            par = gc % 2
            self.ve(lambda nc, e, idx=idx, par=par: nc.vector.tensor_tensor(
                sb['t1'][:, idx * 1024:(idx + 1) * 1024],
                sb['t1'][:, idx * 1024:(idx + 1) * 1024],
                sb['xm'][:, par * 1024:(par + 1) * 1024], ALU.add), inc=[('ve', 1)])
            tk['xm_free'][par] = {'ve': self.now('ve')}
            tld = self.dma('sy', sb['xq'][:, par * 1024:(par + 1) * 1024],
                           d['x'][gc * 128:(gc + 1) * 128, :],
                           waits=[('ve', tk['xq_free'][par]['ve'])], sem=f'xq{par}')
            tfin = self.ve(lambda nc, e, idx=idx, par=par: nc.vector.tensor_tensor(
                sb['t1'][:, idx * 1024:(idx + 1) * 1024],
                sb['t1'][:, idx * 1024:(idx + 1) * 1024],
                sb['xq'][:, par * 1024:(par + 1) * 1024], ALU.add),
                waits=[(f'xq{par}', tld[f'xq{par}'])], inc=[('ve', 1)])
            tk['xq_free'][par] = tfin
            tst = self.dma('sy', self.out[gc * 128:(gc + 1) * 128, :],
                           sb['t1'][:, idx * 1024:(idx + 1) * 1024],
                           waits=[('ve', tfin['ve'])], sem=f'do{idx}')
            tk['outb_free'][idx] = {'do': tst[f'do{idx}']}

    def _emit_block(self):
        nc = self.nc
        sems = self.sems

        def runner(name, eng):
            for fn, waits, incs in self.q[name]:
                for s, v in waits:
                    eng.wait_ge(sems[s], v)
                if fn is None:
                    continue
                inst = fn(nc, eng)
                for s, a in incs:
                    inst.then_inc(sems[s], a)

        with nc.Block() as blk:
            @blk.tensor
            def _(eng):
                runner('te', eng)

            @blk.vector
            def _(eng):
                runner('ve', eng)

            @blk.scalar
            def _(eng):
                runner('se', eng)

            @blk.gpsimd
            def _(eng):
                runner('gp', eng)

            @blk.sync
            def _(eng):
                runner('sy', eng)


def make_inputs(inp):
    c = host_consts()
    m = {}
    m['x'] = None
    for s in ('f', 'b'):
        m[f'wi_{s}'] = np.ascontiguousarray(inp[f'L0_attn_Wi_{s}'], np.float32)
        m[f'wo_{s}'] = np.ascontiguousarray(inp[f'L0_attn_Wo_{s}'], np.float32)
        m[f'convw_{s}'] = np.ascontiguousarray(inp[f'L0_attn_conv_{s}'], np.float32)
        m[f'convb_{s}'] = np.ascontiguousarray(inp[f'L0_attn_convb_{s}'], np.float32)
        m[f'alog_{s}'] = np.ascontiguousarray(inp[f'L0_attn_Alog_{s}'], np.float32)
        m[f'dtb_{s}'] = np.ascontiguousarray(inp[f'L0_attn_dtb_{s}'], np.float32)
        dd = np.zeros((128, 4096), np.float32)
        Dv = np.asarray(inp[f'L0_attn_D_{s}'], np.float32)
        for h in range(32):
            for ss in range(128):
                dd[ss, h * 128 + ss] = Dv[h]
        m[f'ddg_{s}'] = dd
    m['wm'] = np.ascontiguousarray(inp['L0_attn_Wm'], np.float32)
    m['bm'] = np.ascontiguousarray(inp['L0_attn_bm'], np.float32)
    m['wfr'] = np.ascontiguousarray(inp['L0_ffn_w_freq_real'], np.float32)
    m['wfi'] = np.ascontiguousarray(inp['L0_ffn_w_freq_imag'], np.float32)
    m['fbias'] = np.ascontiguousarray(inp['L0_ffn_bias'], np.float32)
    for k, v in c.items():
        m[k] = v
    return m


def kernel(**inp):
    x = np.asarray(inp['L0_attn_x'], np.float32)
    base = make_inputs(inp)
    b = Builder()
    nc = b.build()
    in_maps = []
    for core in range(8):
        mm_ = dict(base)
        mm_['x'] = np.ascontiguousarray(x[core])
        in_maps.append(mm_)
    res = run_bass_kernel_spmd(nc, in_maps, core_ids=list(range(8)))
    out = np.stack([r['out'] for r in res.results], axis=0)
    return out
